# version 01
# baseline (speedup 1.0000x reference)
import numpy as np
import ml_dtypes
from contextlib import ExitStack

import concourse.bass as bass
import concourse.mybir as mybir
from concourse.bass_utils import run_bass_kernel_spmd

F32 = mybir.dt.float32
BF16 = mybir.dt.bfloat16
AF = mybir.ActivationFunctionType
ALU = mybir.AluOpType
AX = mybir.AxisListType
BF16NP = ml_dtypes.bfloat16

D = 1024
PG = D // 128
NH = 16
HD = 64
LAT = 256
DH = 4096
NE = 8
DE = 2816
LN_EPS = 1e-5
RMS_EPS = 1e-6


class Ctx:

    def __init__(self, nc, es):
        self.nc = nc
        self.es = es
        self._sems = {}
        self._by_num = {}
        for nm in ("qs", "qg", "qa", "t", "v", "s", "g"):
            h = es.enter_context(nc.semaphore("sem_" + nm))
            self._sems[nm] = [h, 0]

    def inc(self, nm, instr, k=1):
        ent = self._sems[nm]
        instr.then_inc(ent[0], k)
        ent[1] += k
        return (nm, ent[1])

    def n(self, nm):
        return self._sems[nm][1]

    def tok(self, nm):
        return (nm, self._sems[nm][1])

    _uctr = 0

    def u(self, nm):
        Ctx._uctr += 1
        return f"{nm}_{Ctx._uctr}"

    POOL_HW = 44
    POOL_SW = 24

    def chan(self, scope_es, name="ch", kind="hw"):
        if not hasattr(self, "_pool"):
            self._pool = {"hw": [], "sw": []}
            self._pool_i = {"hw": 0, "sw": 0}
            for i in range(self.POOL_HW):
                key = f"hch{i}"
                h = self.es.enter_context(self.nc.semaphore(key))
                self._sems[key] = [h, 0]
                self._pool["hw"].append(key)
            for i in range(self.POOL_SW):
                key = f"sch{i}"
                h = self.es.enter_context(self.nc.semaphore(key))
                self._sems[key] = [h, 0]
                self._pool["sw"].append(key)
        pool = self._pool[kind]
        key = pool[self._pool_i[kind] % len(pool)]
        self._pool_i[kind] += 1
        return key

    def wait(self, eng, tok):
        if tok is None:
            return
        if isinstance(tok, list):
            best = {}
            for t in tok:
                if t is None:
                    continue
                nm, val = t
                best[nm] = max(best.get(nm, 0), val)
            for nm, val in best.items():
                self.wait(eng, (nm, val))
            return
        nm, val = tok
        ent = self._sems[nm]
        assert val <= ent[1], f"wait({nm},{val}) but only {ent[1]} emitted"
        if val > 0:
            eng.wait_ge(ent[0], val)

    def barrier(self):
        nc = self.nc
        for key, ent in list(self._sems.items()):
            if key in ("t", "v", "s", "g"):
                continue
            if ent[1] > 0:
                nc.vector.wait_ge(ent[0], ent[1])
        nc.all_engine_barrier()


def _ts(i, n=128):
    return slice(i * n, (i + 1) * n)


def _ceil(a, b):
    return (a + b - 1) // b


def emit_ln(C, x_in_tm, xT_sc, xn_sc, n_tok, *, is_ln1, y_sc, ident_b, ident_f, cst):
    nc = C.nc
    NT = n_tok // 128
    with ExitStack() as ph:
        sb = lambda nm, shape, dt: ph.enter_context(nc.sbuf_tensor(C.u(nm), shape, dt))
        psum = lambda nm, shape, dt: ph.enter_context(nc.psum_tensor(C.u(nm), shape, dt))
        NB = 3
        xt = [sb(f"ln_x{i}", [128, D], F32) for i in range(NB)]
        fxn = [sb(f"ln_fxn{i}", [128, D], F32) for i in range(NB)]
        xsqd = sb("ln_xsqd", [128, D], F32)
        stats = [sb(f"ln_st{i}", [128, 12], F32) for i in range(NB)]
        xn_tm = [sb(f"ln_xn{i}", [128, D], BF16) for i in range(NB)]
        xs_tm = [sb(f"ln_xs{i}", [128, D], F32) for i in range(NB)]
        y_tm = [sb(f"ln_y{i}", [128, D], BF16) for i in range(NB)]
        stg_b = [sb(f"ln_sgb{i}", [128, D], BF16) for i in range(2)]
        stg_f = [sb(f"ln_sgf{i}", [128, D], F32) for i in range(2)]
        stg_y = [sb(f"ln_sgy{i}", [128, D], BF16) for i in range(2)]
        ldf = [sb(f"ln_ldf{i}", [128, D], F32) for i in range(2)]
        ps_b = [psum(f"ln_pb{i}", [128, 1024], BF16) for i in range(2)]
        ps_f = [psum(f"ln_pf{i}", [128, 512], F32) for i in range(2)]

        x_free = [None] * NB
        st_free = [None] * NB
        buf_free = [None] * NB
        stg_free = [None] * 2
        psb_free = [None] * 2
        psf_free = [None] * 2
        ldf_free = [None] * 2
        ch_x = [C.chan(ph, "ln_chx") for i in range(NB)]
        ch_ldf = [C.chan(ph, "ln_chl") for i in range(2)]
        ch_st = [C.chan(ph, "ln_chs", kind="sw") for i in range(2)]

        for tt in range(NT):
            i = tt % NB
            j = tt % 2
            ts = _ts(tt)
            if is_ln1:
                C.wait(nc.sync, ldf_free[j])
                ld = C.inc(ch_ldf[j], nc.sync.dma_start(
                    out=ldf[j][:].rearrange("p (g c) -> p g c", g=PG),
                    in_=x_in_tm[:, :, ts].rearrange("g p c -> p g c")), 16)
                C.wait(nc.tensor, ld)
                C.wait(nc.tensor, psf_free[j])
                C.wait(nc.scalar, x_free[i])
                ev = None
                for g0 in range(0, PG, 4):
                    tr = None
                    for g in range(g0, g0 + 4):
                        tr = nc.tensor.transpose(ps_f[j][:, _ts(g - g0)], ldf[j][:, _ts(g)], ident_f)
                    tt_tok = C.inc("t", tr)
                    C.wait(nc.scalar, tt_tok)
                    for g in range(g0, g0 + 4):
                        ev = nc.scalar.copy(xt[i][:, _ts(g)], ps_f[j][:, _ts(g - g0)])
                    ev = C.inc("s", ev)
                    if g0 + 4 < PG:
                        C.wait(nc.tensor, ev)
                psf_free[j] = ev
                ldf_free[j] = C.tok("t")
                nc.scalar.drain()
                ld = ev
            else:
                C.wait(nc.sync, x_free[i])
                ld = C.inc(ch_x[i], nc.sync.dma_start(out=xt[i][:], in_=x_in_tm[ts, :]), 16)

            st = stats[i]
            s1, ssq, mean, var, rstd = st[:, 0:1], st[:, 1:2], st[:, 2:3], st[:, 3:4], st[:, 4:5]
            msq, lnv = st[:, 5:6], st[:, 6:7]
            C.wait(nc.scalar, ld)
            C.wait(nc.scalar, st_free[i])
            sq = C.inc("s", nc.scalar.activation(xsqd[:], xt[i][:], AF.Square, accum_out=ssq))
            C.wait(nc.vector, ld)
            C.wait(nc.vector, sq)
            nc.vector.tensor_reduce(s1, xt[i][:], AX.X, ALU.add)
            nc.vector.drain()
            nc.vector.tensor_scalar(mean, s1, 1.0 / D, 0.0, ALU.mult, ALU.add)
            nc.vector.drain()
            nc.vector.tensor_tensor(msq, mean, mean, ALU.mult)
            nc.vector.drain()
            vv = C.inc("v", nc.vector.scalar_tensor_tensor(var, ssq, 1.0 / D, msq, ALU.mult, ALU.subtract))
            C.wait(nc.scalar, vv)
            nc.scalar.activation(lnv, var, AF.Ln, bias=cst[:, 0:1])
            nc.scalar.drain()
            rs = C.inc("s", nc.scalar.activation(rstd, lnv, AF.Exp, scale=-0.5))
            C.wait(nc.vector, rs)
            C.wait(nc.vector, buf_free[i])
            if is_ln1:
                nc.vector.tensor_scalar(xn_tm[i][:], xt[i][:], mean, rstd, ALU.subtract, ALU.mult)
                fx = C.inc("v", nc.vector.tensor_scalar(fxn[i][:], xt[i][:], mean, rstd, ALU.subtract, ALU.mult))
                C.wait(nc.scalar, fx)
                sq2 = C.inc("s", nc.scalar.activation(xsqd[:], fxn[i][:], AF.Square, accum_out=st[:, 7:8]))
                nc.scalar.drain()
                nc.scalar.activation(st[:, 8:9], st[:, 7:8], AF.Ln, bias=cst[:, 1:2])
                nc.scalar.drain()
                rs2 = C.inc("s", nc.scalar.activation(st[:, 9:10], st[:, 8:9], AF.Exp, scale=-0.5))
                C.wait(nc.vector, rs2)
                nc.vector.tensor_scalar(st[:, 10:11], st[:, 9:10], float(np.sqrt(D)), 0.0, ALU.mult, ALU.add)
                nc.vector.drain()
                nc.vector.tensor_scalar(y_tm[i][:], fxn[i][:], st[:, 10:11], 0.0, ALU.mult, ALU.add)
                apl = C.inc("v", nc.vector.tensor_add(xs_tm[i][:], xt[i][:], fxn[i][:]))
                sq2_tok = sq2
            else:
                apl = C.inc("v", nc.vector.tensor_scalar(xn_tm[i][:], xt[i][:], mean, rstd, ALU.subtract, ALU.mult))
                sq2_tok = None
            st_free[i] = apl

            src_f = xs_tm[i] if is_ln1 else xt[i]
            C.wait(nc.tensor, apl)
            C.wait(nc.tensor, psb_free[j])
            C.wait(nc.vector, stg_free[j])
            C.wait(nc.scalar, stg_free[j])
            tr = None
            for g in range(PG):
                tr = nc.tensor.transpose(ps_b[j][:, _ts(g)], xn_tm[i][:, _ts(g)], ident_b)
            tb = C.inc("t", tr)
            C.wait(nc.vector, tb)
            ev_b = None
            for g in range(PG):
                ev_b = nc.vector.tensor_copy(stg_b[j][:, _ts(g)], ps_b[j][:, _ts(g)])
            ev_b = C.inc("v", ev_b)
            psb_free[j] = ev_b
            C.wait(nc.tensor, psf_free[j])
            ev_f = None
            for g0 in range(0, PG, 4):
                tr = None
                for g in range(g0, g0 + 4):
                    tr = nc.tensor.transpose(ps_f[j][:, _ts(g - g0)], src_f[:, _ts(g)], ident_f)
                tf = C.inc("t", tr)
                C.wait(nc.scalar, tf)
                for g in range(g0, g0 + 4):
                    ev_f = nc.scalar.copy(stg_f[j][:, _ts(g)], ps_f[j][:, _ts(g - g0)])
                ev_f = C.inc("s", ev_f)
                if g0 + 4 < PG:
                    C.wait(nc.tensor, ev_f)
            psf_free[j] = ev_f
            ev_y = None
            if is_ln1:
                C.wait(nc.tensor, ev_b)
                tr = None
                for g in range(PG):
                    tr = nc.tensor.transpose(ps_b[j][:, _ts(g)], y_tm[i][:, _ts(g)], ident_b)
                ty = C.inc("t", tr)
                C.wait(nc.vector, ty)
                for g in range(PG):
                    ev_y = nc.vector.tensor_copy(stg_y[j][:, _ts(g)], ps_b[j][:, _ts(g)])
                ev_y = C.inc("v", ev_y)
                psb_free[j] = ev_y
            t_done = C.tok("t")
            buf_free[i] = [t_done, sq2_tok]
            x_free[i] = [apl, sq, t_done]

            C.wait(nc.gpsimd, ev_b if ev_y is None else ev_y)
            C.wait(nc.gpsimd, ev_f)
            C.inc(ch_st[j], nc.gpsimd.dma_start(
                out=xn_sc[:, :, ts].rearrange("g p c -> p g c"),
                in_=stg_b[j][:].rearrange("p (g c) -> p g c", g=PG)), 16)
            sg = C.inc(ch_st[j], nc.gpsimd.dma_start(
                out=xT_sc[:, :, ts].rearrange("g p c -> p g c"),
                in_=stg_f[j][:].rearrange("p (g c) -> p g c", g=PG)), 16)
            if is_ln1:
                sg = C.inc(ch_st[j], nc.gpsimd.dma_start(
                    out=y_sc[:, :, ts].rearrange("g p c -> p g c"),
                    in_=stg_y[j][:].rearrange("p (g c) -> p g c", g=PG)), 16)
            stg_free[j] = sg
        C.barrier()


def emit_gemm_oi(C, Ws, O, I, in_sc, n_tok, *, og, epilogue, pre=None,
                 ident_f, name, tb_bytes=512, rhs_scale=None):
    nc = C.nc
    O_t, I_t, NTB = O // 128, I // 128, n_tok // 512
    NW = len(Ws)
    with ExitStack() as ph:
        sb = lambda nm, shape, dt: ph.enter_context(nc.sbuf_tensor(C.u(name + nm), shape, dt))
        psum = lambda nm, shape, dt: ph.enter_context(nc.psum_tensor(C.u(name + nm), shape, dt))
        wstage = [sb(f"_wst{i}", [128, I], F32) for i in range(2)]
        wT = [sb(f"_wT{w}", [128, og * I_t * 128], BF16) for w in range(NW)]
        inb = [sb(f"_in{i}", [128, I_t * 512], BF16) for i in range(2)]
        wps = [psum(f"_wps{i}", [128, 512], F32) for i in range(2)]
        acc = [psum(f"_acc{i}", [128, 512], F32) for i in range(2 * NW)]

        wstage_free = [None] * 2
        wps_free = [None] * 2
        inb_free = [None] * 2
        acc_free = [None] * (2 * NW)
        group_mm_done = None
        ch_w = [C.chan(ph, name + "_chw") for i in range(2)]
        ch_in = [C.chan(ph, name + "_chi") for i in range(2)]

        n_groups = _ceil(O_t, og)
        for grp in range(n_groups):
            ots = list(range(grp * og, min((grp + 1) * og, O_t)))
            k = 0
            for w in range(NW):
                for osl, ot in enumerate(ots):
                    sl = k % 2
                    k += 1
                    C.wait(nc.sync, wstage_free[sl])
                    ld = C.inc(ch_w[sl], nc.sync.dma_start(
                        out=wstage[sl][:], in_=Ws[w][_ts(ot), :]), 16)
                    C.wait(nc.tensor, ld)
                    ev = None
                    for it0 in range(0, I_t, 4):
                        b = (it0 // 4) % 2
                        C.wait(nc.tensor, wps_free[b])
                        tr = None
                        nsub = min(4, I_t - it0)
                        for q in range(nsub):
                            tr = nc.tensor.transpose(
                                wps[b][:, _ts(q)], wstage[sl][:, _ts(it0 + q)], ident_f)
                        trt = C.inc("t", tr)
                        C.wait(nc.vector, trt)
                        if group_mm_done is not None:
                            C.wait(nc.vector, group_mm_done)
                        for q in range(nsub):
                            if rhs_scale is None:
                                ev = nc.vector.tensor_copy(
                                    wT[w][:, _ts(osl * I_t + it0 + q)], wps[b][:, _ts(q)])
                            else:
                                ev = nc.vector.tensor_scalar(
                                    wT[w][:, _ts(osl * I_t + it0 + q)], wps[b][:, _ts(q)],
                                    rhs_scale[:, it0 + q:it0 + q + 1], 0.0, ALU.mult, ALU.add)
                        ev = C.inc("v", ev)
                        wps_free[b] = ev
                    wstage_free[sl] = C.tok("t")
                    wT_ready = ev
            wT_tok = wT_ready

            for tb in range(NTB):
                sl = tb % 2
                C.wait(nc.sync, inb_free[sl])
                ldi = C.inc(ch_in[sl], nc.sync.dma_start(
                    out=inb[sl][:].rearrange("p (g c) -> p g c", g=I_t),
                    in_=in_sc[:, :, _ts(tb, 512)].rearrange("g p c -> p g c")), 16)
                C.wait(nc.tensor, ldi)
                if tb == 0:
                    C.wait(nc.tensor, wT_tok)
                last_mm = None
                for osl, ot in enumerate(ots):
                    aux = pre(C, ot, tb) if pre is not None else None
                    pss = []
                    for w in range(NW):
                        a = (((tb * len(ots) + osl) % 2) * NW) + w
                        C.wait(nc.tensor, acc_free[a])
                        for it in range(I_t):
                            last_mm = nc.tensor.matmul(
                                acc[a][:], wT[w][:, _ts(osl * I_t + it)],
                                inb[sl][:, _ts(it, 512)],
                                start=(it == 0), stop=(it == I_t - 1))
                        pss.append(acc[a])
                    rdy = C.inc("t", last_mm)
                    frees = epilogue(C, ot, tb, pss, rdy, aux)
                    for w in range(NW):
                        a = (((tb * len(ots) + osl) % 2) * NW) + w
                        acc_free[a] = frees
                inb_free[sl] = C.tok("t")
            group_mm_done = C.tok("t")
        C.barrier()


def emit_ffn0(C, wd, xn_sc, u_sc, g_sc, xT_sc, n_tok, sc_w1, sc_w2, sc_h1,
              sc_h2, sc_w3, ident_f, stages="ABCD"):
    nc = C.nc

    if "A" in stages:
      with ExitStack() as ph:
        silu_b = [ph.enter_context(nc.sbuf_tensor(C.u(f"fA_sl{i}"), [128, 512], F32)) for i in range(2)]
        out_b = [ph.enter_context(nc.sbuf_tensor(C.u(f"fA_ou{i}"), [128, 512], BF16)) for i in range(2)]
        ch_o = [C.chan(ph, "fA_cho", kind="sw") for i in range(2)]
        free_tok = [None, None]
        kctr = [0]

        def epA(C, ot, tb, pss, rdy, aux):
            k = kctr[0] % 2
            kctr[0] += 1
            C.wait(nc.scalar, rdy)
            C.wait(nc.scalar, free_tok[k])
            a = C.inc("s", nc.scalar.activation(silu_b[k][:], pss[0][:], AF.Silu,
                                                scale=sc_w1[:, ot:ot + 1]))
            C.wait(nc.vector, a)
            C.wait(nc.vector, rdy)
            u = C.inc("v", nc.vector.scalar_tensor_tensor(
                out_b[k][:], pss[1][:], sc_w2[:, ot:ot + 1], silu_b[k][:],
                ALU.mult, ALU.mult))
            C.wait(nc.gpsimd, u)
            d = C.inc(ch_o[k], nc.gpsimd.dma_start(
                out=u_sc[ot, :, _ts(tb, 512)], in_=out_b[k][:]), 16)
            free_tok[k] = [u, d]
            return [a, u]

        emit_gemm_oi(C, [wd["L0_ffn_w1_w"], wd["L0_ffn_w2_w"]], DH, D, xn_sc,
                     n_tok, og=8, epilogue=epA, ident_f=ident_f, name="fA")

    for tag, wkey, scl, src, dst in (("fB", "L0_ffn_h1_w", sc_h1, u_sc, g_sc),
                                     ("fC", "L0_ffn_h2_w", sc_h2, g_sc, u_sc)):
        if tag[1] not in stages:
            continue
        with ExitStack() as ph:
            out_b = [ph.enter_context(nc.sbuf_tensor(C.u(f"{tag}_ou{i}"), [128, 512], BF16)) for i in range(2)]
            ch_o = [C.chan(ph, tag + "_cho", kind="sw") for i in range(2)]
            free_tok = [None, None]
            kctr = [0]

            def epB(C, ot, tb, pss, rdy, aux, out_b=out_b, free_tok=free_tok,
                    kctr=kctr, scl=scl, dst=dst, ch_o=ch_o):
                k = kctr[0] % 2
                kctr[0] += 1
                C.wait(nc.scalar, rdy)
                C.wait(nc.scalar, free_tok[k])
                a = C.inc("s", nc.scalar.activation(out_b[k][:], pss[0][:], AF.Gelu,
                                                    scale=scl[:, ot:ot + 1]))
                C.wait(nc.gpsimd, a)
                d = C.inc(ch_o[k], nc.gpsimd.dma_start(
                    out=dst[ot, :, _ts(tb, 512)], in_=out_b[k][:]), 16)
                free_tok[k] = [d]
                return [a]

            emit_gemm_oi(C, [wd[wkey]], DH, DH, src, n_tok, og=8, epilogue=epB,
                         ident_f=ident_f, name=tag)

    if "D" in stages:
      with ExitStack() as ph:
        xt_b = [ph.enter_context(nc.sbuf_tensor(C.u(f"fD_xt{i}"), [128, 512], F32)) for i in range(3)]
        out_b = [ph.enter_context(nc.sbuf_tensor(C.u(f"fD_ou{i}"), [128, 512], F32)) for i in range(2)]
        ch_xt = [C.chan(ph, "fD_chx", kind="sw") for i in range(3)]
        ch_o = [C.chan(ph, "fD_cho", kind="sw") for i in range(2)]
        xt_free = [None] * 3
        out_free = [None, None]
        prectr = [0]
        epctr = [0]

        def preD(C, ot, tb):
            k = prectr[0] % 3
            prectr[0] += 1
            C.wait(nc.gpsimd, xt_free[k])
            ld = C.inc(ch_xt[k], nc.gpsimd.dma_start(
                out=xt_b[k][:], in_=xT_sc[ot, :, _ts(tb, 512)]), 16)
            return (k, ld)

        def epD(C, ot, tb, pss, rdy, aux):
            k, ld = aux
            m = epctr[0] % 2
            epctr[0] += 1
            C.wait(nc.vector, rdy)
            C.wait(nc.vector, ld)
            C.wait(nc.vector, out_free[m])
            u = C.inc("v", nc.vector.scalar_tensor_tensor(
                out_b[m][:], pss[0][:], sc_w3[:, ot:ot + 1], xt_b[k][:],
                ALU.mult, ALU.add))
            xt_free[k] = u
            C.wait(nc.gpsimd, u)
            d = C.inc(ch_o[m], nc.gpsimd.dma_start(
                out=xT_sc[ot, :, _ts(tb, 512)], in_=out_b[m][:]), 16)
            out_free[m] = [d]
            return [u]

        emit_gemm_oi(C, [wd["L0_ffn_w3_w"]], D, DH, u_sc, n_tok, og=8,
                     epilogue=epD, pre=preD, ident_f=ident_f, name="fD")


def load_weight_cast(C, ph, W_dram, I, O, name):
    nc = C.nc
    I_t = I // 128
    dst = ph.enter_context(nc.sbuf_tensor(C.u(name), [128, I_t * O], BF16))
    stage = [ph.enter_context(nc.sbuf_tensor(C.u(name + f"_st{i}"), [128, O], F32)) for i in range(2)]
    ch_st = [C.chan(ph, name + "_ch") for i in range(2)]
    st_free = [None, None]
    ev = None
    for it in range(I_t):
        sl = it % 2
        C.wait(nc.sync, st_free[sl])
        ld = C.inc(ch_st[sl], nc.sync.dma_start(out=stage[sl][:], in_=W_dram[_ts(it), :]), 16)
        C.wait(nc.vector, ld)
        ev = C.inc("v", nc.vector.tensor_copy(dst[:, it * O:(it + 1) * O], stage[sl][:]))
        st_free[sl] = ev
    return dst, ev


def emit_attn(C, wd, xn_sc, xT_sc, n_tok, seq, *, ident_b, ident_f, perm_b,
              cos_t, sin_t, mask128, gam_bet, cst, dbg=None, upto=3):
    nc = C.nc
    n_batch = n_tok // seq
    SB = seq // 512
    ST = seq // 128
    LT = LAT // 128
    with ExitStack() as ph:
        sb = lambda nm, shape, dt: ph.enter_context(nc.sbuf_tensor(C.u(nm), shape, dt))
        psum = lambda nm, shape, dt: ph.enter_context(nc.psum_tensor(C.u(nm), shape, dt))
        q_sb = sb("at_q", [128, PG * seq], BF16)
        k_sb = sb("at_k", [128, PG * seq], BF16)
        v_tm = sb("at_v", [128, ST * D], BF16)
        inb = [sb(f"at_in{i}", [128, PG * 512], BF16) for i in range(2)]
        ch_in = [C.chan(ph, "at_chi") for i in range(2)]
        ch_xt = [C.chan(ph, "at_chx") for i in range(2)]
        ch_o = [C.chan(ph, "at_cho", kind="sw") for i in range(2)]

        for b in range(n_batch):
            cols = lambda tb: slice((b * SB + tb) * 512, (b * SB + tb + 1) * 512)

            with ExitStack() as s1:
                sb1 = lambda nm, shape, dt: s1.enter_context(nc.sbuf_tensor(C.u(nm), shape, dt))
                ps1 = lambda nm, shape, dt: s1.enter_context(nc.psum_tensor(C.u(nm), shape, dt))
                wdkv, wtok = load_weight_cast(C, s1, wd["L0_attn_W_dkv"], D, LAT, "at_wdkv")
                wuk, wtok2 = load_weight_cast(C, s1, wd["L0_attn_W_uk"], LAT, D, "at_wuk")
                wuv, wtok3 = load_weight_cast(C, s1, wd["L0_attn_W_uv"], LAT, D, "at_wuv")
                latn = sb1("at_latn", [128, LT * seq], BF16)
                lat_c = sb1("at_latc", [128, LT * 512], F32)
                sqd = sb1("at_sqd", [128, 512], F32)
                srow = sb1("at_srow", [1, 512], F32)
                srow2 = sb1("at_srow2", [1, 512], F32)
                mbc = sb1("at_mbc", [128, 512], F32)
                rsc = sb1("at_rsc", [128, 512], F32)
                ones128 = sb1("at_o128", [128, 1], F32)
                ones1 = sb1("at_o1", [1, 128], F32)
                acc = [ps1(f"at_ac{i}", [128, 512], F32) for i in range(4)]
                ps_row = ps1("at_pr", [1, 512], F32)
                ps_bc = ps1("at_bc", [128, 512], F32)
                ms = C.inc("v", nc.vector.memset(ones128[:], 1.0))
                ms = C.inc("v", nc.vector.memset(ones1[:], 1.0))
                nc.vector.drain()
                C.wait(nc.tensor, ms)
                C.wait(nc.tensor, [wtok, wtok2, wtok3])
                inb_free = [C.tok("t"), C.tok("t")]
                acc_free = [None] * 4
                lat_free = None
                bc_free = None
                latn_done = None
                for tb in range(SB):
                    sl = tb % 2
                    C.wait(nc.sync, inb_free[sl])
                    ldi = C.inc(ch_in[sl], nc.sync.dma_start(
                        out=inb[sl][:].rearrange("p (g c) -> p g c", g=PG),
                        in_=xn_sc[:, :, cols(tb)].rearrange("g p c -> p g c")), 16)
                    C.wait(nc.tensor, ldi)
                    lat_ev = None
                    for ot in range(LT):
                        a = ot % 2
                        C.wait(nc.tensor, acc_free[a])
                        mm = None
                        for it in range(PG):
                            mm = nc.tensor.matmul(
                                acc[a][:], wdkv[:, it * LAT + ot * 128:it * LAT + (ot + 1) * 128],
                                inb[sl][:, _ts(it, 512)], start=(it == 0), stop=(it == PG - 1))
                        rdy = C.inc("t", mm)
                        C.wait(nc.vector, rdy)
                        C.wait(nc.vector, lat_free)
                        lat_ev = C.inc("v", nc.vector.tensor_copy(lat_c[:, _ts(ot, 512)], acc[a][:]))
                        acc_free[a] = lat_ev
                    inb_free[sl] = C.tok("t")
                    C.wait(nc.tensor, lat_ev)
                    mm = None
                    for g in range(LT):
                        mm = nc.tensor.matmul(ps_row[:], ones128[:], lat_c[:, _ts(g, 512)],
                                              start=(g == 0), stop=(g == LT - 1))
                    r1 = C.inc("t", mm)
                    C.wait(nc.vector, r1)
                    e1 = C.inc("v", nc.vector.tensor_copy(srow[0:1, :], ps_row[0:1, :]))
                    C.wait(nc.tensor, e1)
                    sq_rd = None
                    r2 = None
                    for g in range(LT):
                        C.wait(nc.vector, sq_rd)
                        lslc = lat_c[:, _ts(g, 512)]
                        sq = C.inc("v", nc.vector.tensor_mul(sqd[:], lslc, lslc))
                        C.wait(nc.tensor, sq)
                        r2 = C.inc("t", nc.tensor.matmul(ps_row[:], ones128[:], sqd[:],
                                                         start=(g == 0), stop=(g == LT - 1)))
                        sq_rd = r2
                    C.wait(nc.vector, r2)
                    e2 = C.inc("v", nc.vector.tensor_copy(srow2[0:1, :], ps_row[0:1, :]))
                    C.wait(nc.tensor, e2)
                    nc.vector.drain()
                    nc.vector.tensor_scalar(srow[0:1, :], srow[0:1, :], 1.0 / LAT, 0.0, ALU.mult, ALU.add)
                    nc.vector.drain()
                    nc.vector.tensor_scalar(srow2[0:1, :], srow2[0:1, :], 1.0 / LAT, 0.0, ALU.mult, ALU.add)
                    nc.vector.drain()
                    nc.vector.tensor_mul(sqd[0:1, :], srow[0:1, :], srow[0:1, :])
                    nc.vector.drain()
                    vv = C.inc("v", nc.vector.tensor_sub(srow2[0:1, :], srow2[0:1, :], sqd[0:1, :]))
                    C.wait(nc.scalar, vv)
                    nc.scalar.activation(srow2[0:1, :], srow2[0:1, :], AF.Ln, bias=cst[0:1, 0:1])
                    nc.scalar.drain()
                    rr = C.inc("s", nc.scalar.activation(srow2[0:1, :], srow2[0:1, :], AF.Exp, scale=-0.5))
                    C.wait(nc.tensor, rr)
                    C.wait(nc.tensor, bc_free)
                    mm = nc.tensor.matmul(ps_bc[:], ones1[:], srow[0:1, :], start=True, stop=True)
                    r = C.inc("t", mm)
                    C.wait(nc.vector, r)
                    b1 = C.inc("v", nc.vector.tensor_copy(mbc[:], ps_bc[:]))
                    C.wait(nc.tensor, b1)
                    mm = nc.tensor.matmul(ps_bc[:], ones1[:], srow2[0:1, :], start=True, stop=True)
                    r = C.inc("t", mm)
                    C.wait(nc.vector, r)
                    b2 = C.inc("v", nc.vector.tensor_copy(rsc[:], ps_bc[:]))
                    bc_free = b2
                    for g in range(LT):
                        nc.vector.tensor_sub(lat_c[:, _ts(g, 512)], lat_c[:, _ts(g, 512)], mbc[:])
                        nc.vector.drain()
                        nc.vector.tensor_mul(lat_c[:, _ts(g, 512)], lat_c[:, _ts(g, 512)], rsc[:])
                        nc.vector.drain()
                        latn_done = C.inc("v", nc.vector.tensor_scalar(
                            latn[:, g * seq + tb * 512: g * seq + (tb + 1) * 512],
                            lat_c[:, _ts(g, 512)],
                            gam_bet[:, g:g + 1], gam_bet[:, LT + g:LT + g + 1], ALU.mult, ALU.add))
                    lat_free = latn_done
                bdone = latn_done
                C.wait(nc.tensor, bdone)
                vtmp = sb1("at_vtmp", [128, 512], BF16)
                ps_tr = ps1("at_ptr", [128, 1024], BF16)
                vfm_dbg = sb1("at_vfm", [128, PG * seq], BF16) if (dbg is not None and "vfm" in dbg) else None
                for tb in range(SB):
                    for ot in range(PG):
                        a = (tb * PG + ot) % 4
                        C.wait(nc.tensor, acc_free[a])
                        mm = None
                        for it in range(LT):
                            mm = nc.tensor.matmul(
                                acc[a][:], wuk[:, it * D + ot * 128:it * D + (ot + 1) * 128],
                                latn[:, it * seq + tb * 512: it * seq + (tb + 1) * 512],
                                start=(it == 0), stop=(it == LT - 1))
                        rdy = C.inc("t", mm)
                        C.wait(nc.vector, rdy)
                        evk = C.inc("v", nc.vector.tensor_copy(
                            k_sb[:, ot * seq + tb * 512: ot * seq + (tb + 1) * 512], acc[a][:]))
                        C.wait(nc.tensor, evk)
                        mm = None
                        for it in range(LT):
                            mm = nc.tensor.matmul(
                                acc[a][:], wuv[:, it * D + ot * 128:it * D + (ot + 1) * 128],
                                latn[:, it * seq + tb * 512: it * seq + (tb + 1) * 512],
                                start=(it == 0), stop=(it == LT - 1))
                        rdy = C.inc("t", mm)
                        C.wait(nc.vector, rdy)
                        if dbg is not None and "vfm" in dbg:
                            nc.vector.tensor_copy(
                                vfm_dbg[:, ot * seq + tb * 512: ot * seq + (tb + 1) * 512], acc[a][:])
                        evv = C.inc("v", nc.vector.tensor_copy(vtmp[:], acc[a][:]))
                        C.wait(nc.tensor, evv)
                        trr = None
                        for c in range(4):
                            trr = nc.tensor.transpose(ps_tr[:, _ts(c)], vtmp[:, _ts(c)], ident_b)
                        trt = C.inc("t", trr)
                        C.wait(nc.vector, trt)
                        evt = None
                        for c in range(4):
                            kt = tb * 4 + c
                            evt = nc.vector.tensor_copy(
                                v_tm[:, kt * D + ot * 128: kt * D + (ot + 1) * 128], ps_tr[:, _ts(c)])
                        evt = C.inc("v", evt)
                        acc_free[a] = evt
                        C.wait(nc.tensor, evt)
                kv_barrier_marker = True
                C.barrier()
                if dbg is not None and "vfm" in dbg:
                    ddv = C.tok("v")
                    C.wait(nc.gpsimd, ddv)
                    dv2 = C.inc("qg", nc.gpsimd.dma_start(out=dbg["vfm"][b, :, :], in_=vfm_dbg[:]), 16)
                    C.wait(nc.gpsimd, dv2)
                    C.barrier()
            with ExitStack() as s2:
                sb2 = lambda nm, shape, dt: s2.enter_context(nc.sbuf_tensor(C.u(nm), shape, dt))
                ps2 = lambda nm, shape, dt: s2.enter_context(nc.psum_tensor(C.u(nm), shape, dt))
                wq, wtok = load_weight_cast(C, s2, wd["L0_attn_W_q"], D, D, "at_wq")
                acc = [ps2(f"at_qa{i}", [128, 512], F32) for i in range(2)]
                ps_sw = ps2("at_sw", [128, 512], F32)
                t1 = sb2("at_t1", [128, 512], F32)
                t2 = sb2("at_t2", [128, 512], F32)
                acc_free = [None] * 2
                inb_free = [C.tok("t"), C.tok("t")]
                C.wait(nc.tensor, wtok)
                for tb in range(SB):
                    sl = tb % 2
                    C.wait(nc.sync, inb_free[sl])
                    ldi = C.inc(ch_in[sl], nc.sync.dma_start(
                        out=inb[sl][:].rearrange("p (g c) -> p g c", g=PG),
                        in_=xn_sc[:, :, cols(tb)].rearrange("g p c -> p g c")), 16)
                    C.wait(nc.tensor, ldi)
                    for ot in range(PG):
                        a = (tb * PG + ot) % 2
                        C.wait(nc.tensor, acc_free[a])
                        mm = None
                        for it in range(PG):
                            mm = nc.tensor.matmul(
                                acc[a][:], wq[:, it * D + ot * 128:it * D + (ot + 1) * 128],
                                inb[sl][:, _ts(it, 512)], start=(it == 0), stop=(it == PG - 1))
                        rdy = C.inc("t", mm)
                        C.wait(nc.vector, rdy)
                        ev = C.inc("v", nc.vector.tensor_copy(
                            q_sb[:, ot * seq + tb * 512: ot * seq + (tb + 1) * 512], acc[a][:]))
                        acc_free[a] = ev
                    inb_free[sl] = C.tok("t")
                if dbg is not None and "q0" in dbg:
                    dd = C.tok("v")
                    C.wait(nc.gpsimd, dd)
                    dq = C.inc("qg", nc.gpsimd.dma_start(out=dbg["q0"][b, :, :], in_=q_sb[:]), 16)
                    C.wait(nc.gpsimd, dq)
                    C.barrier()
                rope_done = C.tok("v")
                for which, buf in (("q", q_sb), ("k", k_sb)):
                    for g in range(PG):
                        for tc in range(SB):
                            slc = slice(g * seq + tc * 512, g * seq + (tc + 1) * 512)
                            C.wait(nc.tensor, rope_done)
                            mm = nc.tensor.matmul(ps_sw[:], perm_b, buf[:, slc], start=True, stop=True)
                            r = C.inc("t", mm)
                            C.wait(nc.vector, r)
                            nc.vector.tensor_mul(t1[:], buf[:, slc], cos_t[:, _ts(tc, 512)])
                            nc.vector.tensor_mul(t2[:], ps_sw[:], sin_t[:, _ts(tc, 512)])
                            nc.vector.drain()
                            rope_done = C.inc("v", nc.vector.tensor_add(buf[:, slc], t1[:], t2[:]))
                C.barrier()
            if dbg is not None:
                dd = C.tok("v")
                C.wait(nc.gpsimd, dd)
                C.wait(nc.gpsimd, C.tok("s"))
                dq = C.inc("qg", nc.gpsimd.dma_start(out=dbg["q"][b, :, :], in_=q_sb[:]), 16)
                dq = C.inc("qg", nc.gpsimd.dma_start(out=dbg["k"][b, :, :], in_=k_sb[:]), 16)
                dq = C.inc("qg", nc.gpsimd.dma_start(out=dbg["v"][b, :, :], in_=v_tm[:]), 16)
                C.wait(nc.gpsimd, dq)
                C.barrier()
            if upto < 3:
                continue
            with ExitStack() as s3:
                sb3 = lambda nm, shape, dt: s3.enter_context(nc.sbuf_tensor(C.u(nm), shape, dt))
                ps3 = lambda nm, shape, dt: s3.enter_context(nc.psum_tensor(C.u(nm), shape, dt))
                wo, wtok = load_weight_cast(C, s3, wd["L0_attn_W_o"], D, D, "at_wo")
                ps_S = ps3("at_S", [128, min(seq, 2048)], F32)
                ps_tr2 = ps3("at_tr2", [128, 2048], BF16)
                ps_pv = ps3("at_pv", [64, 512], F32)
                ps_wo = ps3("at_wo_ps", [128, 512], F32)
                prow = sb3("at_prow", [128, seq], BF16)
                PT = sb3("at_PT", [128, ST * 512], BF16)
                attnT = sb3("at_attnT", [128, PG * 512], BF16)
                sms = sb3("at_sms", [128, 8], F32)
                xt_b = [sb3(f"at_xt{i}", [128, 512], F32) for i in range(2)]
                out_b = [sb3(f"at_ou{i}", [128, 512], F32) for i in range(2)]
                rope_all = C.tok("v")
                C.wait(nc.tensor, rope_all)
                C.wait(nc.tensor, wtok)
                S_free = None
                tr_free = None
                pv_free = None
                wo_free = None
                xt_free = [None, None]
                out_free = [None, None]
                attnT_free = None
                for qb in range(SB):
                    for h in range(NH):
                        hp = (h % 2) * 64
                        hg = h // 2
                        C.wait(nc.gpsimd, pv_free)
                        gz = C.inc("g", nc.gpsimd.memset(
                            PT[:, (qb * 4) * 512:(qb * 4 + 4) * 512], 0.0))
                        for qt4 in range(4):
                            qt = qb * 4 + qt4
                            valid = (qt + 1) * 128
                            nkb = _ceil(valid, 512)
                            C.wait(nc.tensor, S_free)
                            mm = None
                            for kb in range(nkb):
                                mm = nc.tensor.matmul(
                                    ps_S[:, _ts(kb, 512)],
                                    q_sb[hp:hp + 64, hg * seq + qt * 128: hg * seq + (qt + 1) * 128],
                                    k_sb[hp:hp + 64, hg * seq + kb * 512: hg * seq + (kb + 1) * 512],
                                    start=True, stop=True)
                            smm = C.inc("t", mm)
                            C.wait(nc.vector, smm)
                            dg = slice(qt4 * 128 + (qb * 4) * 128, qt4 * 128 + (qb * 4) * 128 + 128)
                            dg = slice(qt * 128, qt * 128 + 128)
                            nc.vector.tensor_add(ps_S[:, dg], ps_S[:, dg], mask128)
                            nc.vector.drain()
                            nc.vector.tensor_reduce(sms[:, 0:1], ps_S[:, 0:valid], AX.X, ALU.max, negate=True)
                            nc.vector.drain()
                            nm = C.inc("v", nc.vector.tensor_scalar(sms[:, 1:2], sms[:, 0:1], 0.125, 0.0, ALU.mult, ALU.add))
                            C.wait(nc.scalar, nm)
                            ex = C.inc("s", nc.scalar.activation(
                                prow[:, 0:valid], ps_S[:, 0:valid], AF.Exp,
                                bias=sms[:, 1:2], scale=0.125, accum_out=sms[:, 2:3]))
                            S_free = ex
                            C.wait(nc.vector, ex)
                            nc.vector.reciprocal(sms[:, 3:4], sms[:, 2:3])
                            nc.vector.drain()
                            pn = C.inc("v", nc.vector.tensor_scalar(
                                prow[:, 0:valid], prow[:, 0:valid], sms[:, 3:4], 0.0, ALU.mult, ALU.add))
                            C.wait(nc.tensor, pn)
                            C.wait(nc.tensor, tr_free)
                            tr = None
                            for kt in range(qt + 1):
                                tr = nc.tensor.transpose(ps_tr2[:, _ts(kt)], prow[:, _ts(kt)], ident_b)
                            trt = C.inc("t", tr)
                            C.wait(nc.vector, trt)
                            C.wait(nc.vector, gz)
                            tr_free = C.inc("v", nc.vector.tensor_copy(
                                PT[:].rearrange("p (k c) -> p k c", c=512)[:, 0:qt + 1, qt4 * 128: qt4 * 128 + 128],
                                ps_tr2[:].rearrange("p (k c) -> p k c", c=128)[:, 0:qt + 1, :]))
                        pt_done = tr_free
                        C.wait(nc.tensor, pt_done)
                        C.wait(nc.tensor, pv_free)
                        mm = None
                        nkt = qb * 4 + 4
                        for kt in range(nkt):
                            mm = nc.tensor.matmul(
                                ps_pv[:, :],
                                v_tm[:, kt * D + h * 64: kt * D + (h + 1) * 64],
                                PT[:, kt * 512:(kt + 1) * 512],
                                start=(kt == 0), stop=(kt == nkt - 1))
                        pvt = C.inc("t", mm)
                        C.wait(nc.vector, pvt)
                        C.wait(nc.vector, attnT_free)
                        pv_free = C.inc("v", nc.vector.tensor_copy(
                            attnT[hp:hp + 64, hg * 512:(hg + 1) * 512], ps_pv[:, :]))
                    at_done = C.tok("v")
                    C.wait(nc.tensor, at_done)
                    gcols = slice((b * SB + qb) * 512, (b * SB + qb + 1) * 512)
                    for ot in range(PG):
                        m = ot % 2
                        C.wait(nc.sync, xt_free[m])
                        ldx = C.inc(ch_xt[m], nc.sync.dma_start(
                            out=xt_b[m][:], in_=xT_sc[ot, :, gcols]), 16)
                        C.wait(nc.tensor, wo_free)
                        mm = None
                        for it in range(PG):
                            mm = nc.tensor.matmul(
                                ps_wo[:], wo[:, it * D + ot * 128:it * D + (ot + 1) * 128],
                                attnT[:, _ts(it, 512)], start=(it == 0), stop=(it == PG - 1))
                        rdy = C.inc("t", mm)
                        C.wait(nc.vector, rdy)
                        C.wait(nc.vector, out_free[m])
                        C.wait(nc.vector, ldx)
                        u = C.inc("v", nc.vector.tensor_add(out_b[m][:], ps_wo[:], xt_b[m][:]))
                        xt_free[m] = u
                        wo_free = u
                        C.wait(nc.gpsimd, u)
                        d = C.inc(ch_o[m], nc.gpsimd.dma_start(
                            out=xT_sc[ot, :, gcols], in_=out_b[m][:]), 16)
                        out_free[m] = d
                    attnT_free = C.tok("t")
                C.barrier()
        C.barrier()


def emit_scan(C, wd, xn_sc, xT_sc, n_tok, seq, *, bias_pp):
    nc = C.nc
    n_batch = n_tok // seq
    SB = seq // 512
    with ExitStack() as ph:
        sb = lambda nm, shape, dt: ph.enter_context(nc.sbuf_tensor(C.u(nm), shape, dt))
        psum = lambda nm, shape, dt: ph.enter_context(nc.psum_tensor(C.u(nm), shape, dt))
        wz, wtok1 = load_weight_cast(C, ph, wd["L1_attn_Wz"], D, D, "sc_wz")
        wh, wtok2 = load_weight_cast(C, ph, wd["L1_attn_Wh"], D, D, "sc_wh")
        negb = sb("sc_negb", [128, PG], F32)
        inb = [sb(f"sc_in{i}", [128, PG * 512], BF16) for i in range(seq // 512)]
        zb = sb("sc_zb", [128, 512], F32)
        ab = sb("sc_ab", [128, seq], F32)
        bb = sb("sc_bb", [128, seq], F32)
        hb = sb("sc_hb", [128, seq], F32)
        xt_b = [sb(f"sc_xt{i}", [128, 512], F32) for i in range(2)]
        out_b = [sb(f"sc_ou{i}", [128, 512], F32) for i in range(2)]
        accz = [psum(f"sc_az{i}", [128, 512], F32) for i in range(2)]
        acch = [psum(f"sc_ah{i}", [128, 512], F32) for i in range(2)]
        ch_in = [C.chan(ph, "sc_chi") for i in range(seq // 512)]
        ch_xt = [C.chan(ph, "sc_chx") for i in range(2)]
        ch_o = [C.chan(ph, "sc_cho", kind="sw") for i in range(2)]
        nb = C.inc("v", nc.vector.tensor_scalar(negb[:], bias_pp[:, 0:PG], -1.0, 0.0, ALU.mult, ALU.add))
        nc.vector.drain()
        C.wait(nc.scalar, nb)
        C.wait(nc.tensor, [wtok1, wtok2])

        inb_free = [None] * SB
        ld_toks = [None] * SB
        accz_free = [None, None]
        acch_free = [None, None]
        xt_free = [None, None]
        out_free = [None, None]
        buf_free = None
        for b in range(n_batch):
            for tb in range(SB):
                gtb = b * SB + tb
                C.wait(nc.sync, inb_free[tb])
                ld_toks[tb] = C.inc(ch_in[tb], nc.sync.dma_start(
                    out=inb[tb][:].rearrange("p (g c) -> p g c", g=PG),
                    in_=xn_sc[:, :, _ts(gtb, 512)].rearrange("g p c -> p g c")), 16)
            for ot in range(PG):
                vb_prev = None
                for tb in range(SB):
                    sl = tb
                    if ot == 0:
                        C.wait(nc.tensor, ld_toks[tb])
                    m = tb % 2
                    C.wait(nc.tensor, accz_free[m])
                    mm = None
                    for it in range(PG):
                        mm = nc.tensor.matmul(
                            accz[m][:], wz[:, it * D + ot * 128:it * D + (ot + 1) * 128],
                            inb[sl][:, _ts(it, 512)], start=(it == 0), stop=(it == PG - 1))
                    rz = C.inc("t", mm)
                    C.wait(nc.tensor, acch_free[m])
                    mm = None
                    for it in range(PG):
                        mm = nc.tensor.matmul(
                            acch[m][:], wh[:, it * D + ot * 128:it * D + (ot + 1) * 128],
                            inb[sl][:, _ts(it, 512)], start=(it == 0), stop=(it == PG - 1))
                    rh = C.inc("t", mm)
                    C.wait(nc.scalar, rz)
                    C.wait(nc.scalar, buf_free)
                    C.wait(nc.scalar, vb_prev)
                    sz = C.inc("s", nc.scalar.activation(zb[:], accz[m][:], AF.Sigmoid,
                                                         bias=bias_pp[:, ot:ot + 1]))
                    nc.scalar.drain()
                    sa = C.inc("s", nc.scalar.activation(ab[:, _ts(tb, 512)], accz[m][:], AF.Sigmoid,
                                                         scale=-1.0, bias=negb[:, ot:ot + 1]))
                    accz_free[m] = sa
                    C.wait(nc.vector, [sz, sa, rh])
                    vb = C.inc("v", nc.vector.scalar_tensor_tensor(
                        bb[:, _ts(tb, 512)], acch[m][:], bias_pp[:, PG + ot:PG + ot + 1],
                        zb[:], ALU.add, ALU.mult))
                    acch_free[m] = vb
                    vb_prev = vb
                C.wait(nc.vector, C.tok("s"))
                nc.vector.drain()
                sc = C.inc("v", nc.vector.tensor_tensor_scan(
                    hb[:], ab[:], bb[:], 0.0, ALU.mult, ALU.add))
                buf_free = sc
                for tb in range(SB):
                    m2 = tb % 2
                    gtb = b * SB + tb
                    C.wait(nc.sync, xt_free[m2])
                    ldx = C.inc(ch_xt[m2], nc.sync.dma_start(
                        out=xt_b[m2][:], in_=xT_sc[ot, :, _ts(gtb, 512)]), 16)
                    C.wait(nc.vector, [ldx, sc, out_free[m2]])
                    u = C.inc("v", nc.vector.tensor_add(out_b[m2][:], hb[:, _ts(tb, 512)], xt_b[m2][:]))
                    xt_free[m2] = u
                    C.wait(nc.gpsimd, u)
                    d = C.inc(ch_o[m2], nc.gpsimd.dma_start(
                        out=xT_sc[ot, :, _ts(gtb, 512)], in_=out_b[m2][:]), 16)
                    out_free[m2] = d
            t_all = C.tok("t")
            inb_free = [t_all] * SB
        C.barrier()


def emit_moe(C, wd, y_sc, xT_sc, h_sc, wrow_sc, n_tok, *, ident_b, ident_f, rmsw_pp):
    nc = C.nc
    NTB = n_tok // 512
    NT = n_tok // 128
    E_t = DE // 128
    with ExitStack() as ph:
        sb = lambda nm, shape, dt: ph.enter_context(nc.sbuf_tensor(C.u(nm), shape, dt))
        psum = lambda nm, shape, dt: ph.enter_context(nc.psum_tensor(C.u(nm), shape, dt))
        gwf = sb("mg_gwf", [8, D], F32)
        gwT = sb("mg_gwT", [128, PG * 8], BF16)
        logit = sb("mg_log", [8, n_tok], F32)
        ltm = sb("mg_ltm", [128, NT * 8], F32)
        wtm = sb("mg_wtm", [128, NT * 8], F32)
        wfm = sb("mg_wfm", [8, n_tok], F32)
        sms = sb("mg_sms", [128, 8], F32)
        tt8 = sb("mg_tt8", [128, 8], F32)
        inb = [sb(f"mg_in{i}", [128, PG * 512], BF16) for i in range(2)]
        ps8 = psum("mg_ps8", [8, 512], F32)
        pst = psum("mg_pst", [128, 1024], BF16)
        pstf = psum("mg_pstf", [128, 512], F32)
        ch_in = [C.chan(ph, "mg_chi") for i in range(2)]
        ch_g = C.chan(ph, "mg_chg")
        ldg = C.inc(ch_g, nc.sync.dma_start(out=gwf[:], in_=wd["L1_ffn_gate_w"][:, :]), 16)
        gwb = sb("mg_gwb", [8, D], BF16)
        C.wait(nc.vector, ldg)
        cb = C.inc("v", nc.vector.tensor_copy(gwb[:], gwf[:]))
        C.wait(nc.tensor, cb)
        tr = None
        for it in range(PG):
            tr = nc.tensor.transpose(pst[:, it * 8:(it + 1) * 8], gwb[:, _ts(it)], ident_b[0:8, 0:8])
        trt = C.inc("t", tr)
        C.wait(nc.vector, trt)
        gt = None
        for it in range(PG):
            gt = nc.vector.tensor_scalar(
                gwT[:, it * 8:(it + 1) * 8], pst[:, it * 8:(it + 1) * 8],
                rmsw_pp[:, it:it + 1], 0.0, ALU.mult, ALU.add)
        gt = C.inc("v", gt)
        C.wait(nc.tensor, gt)
        inb_free = [None, None]
        lg_ev = None
        for tb in range(NTB):
            sl = tb % 2
            C.wait(nc.sync, inb_free[sl])
            ldi = C.inc(ch_in[sl], nc.sync.dma_start(
                out=inb[sl][:].rearrange("p (g c) -> p g c", g=PG),
                in_=y_sc[:, :, _ts(tb, 512)].rearrange("g p c -> p g c")), 16)
            C.wait(nc.tensor, ldi)
            C.wait(nc.tensor, lg_ev)
            mm = None
            for it in range(PG):
                mm = nc.tensor.matmul(ps8[:], gwT[:, it * 8:(it + 1) * 8],
                                      inb[sl][:, _ts(it, 512)],
                                      start=(it == 0), stop=(it == PG - 1))
            r = C.inc("t", mm)
            C.wait(nc.vector, r)
            lg_ev = C.inc("v", nc.vector.tensor_copy(logit[:, _ts(tb, 512)], ps8[:]))
            inb_free[sl] = C.tok("t")
        C.wait(nc.tensor, lg_ev)
        wb_ev = None
        for tt in range(NT):
            tr = nc.tensor.transpose(pstf[:, 0:8], logit[:, _ts(tt)], ident_f[0:8, 0:8])
            r = C.inc("t", tr)
            C.wait(nc.vector, r)
            ev = C.inc("v", nc.vector.tensor_copy(ltm[:, tt * 8:(tt + 1) * 8], pstf[:, 0:8]))
            C.wait(nc.tensor, ev)
            lt = ltm[:, tt * 8:(tt + 1) * 8]
            wt = wtm[:, tt * 8:(tt + 1) * 8]
            nc.vector.drain()
            nc.vector.tensor_reduce(sms[:, 0:1], lt, AX.X, ALU.max)
            nc.vector.drain()
            nc.vector.tensor_tensor(tt8[:], lt, sms[:, 0:1].broadcast_to([128, 8]), ALU.is_equal)
            nc.vector.drain()
            nc.vector.scalar_tensor_tensor(wt, tt8[:], -1e30, lt, ALU.mult, ALU.add)
            nc.vector.drain()
            nc.vector.tensor_reduce(sms[:, 1:2], wt, AX.X, ALU.max)
            nc.vector.drain()
            nc.vector.tensor_tensor(wt, wt, sms[:, 1:2].broadcast_to([128, 8]), ALU.is_equal)
            nc.vector.drain()
            sg = C.inc("v", nc.vector.tensor_sub(sms[:, 2:3], sms[:, 0:1], sms[:, 1:2]))
            C.wait(nc.scalar, sg)
            ss = C.inc("s", nc.scalar.activation(sms[:, 3:4], sms[:, 2:3], AF.Sigmoid))
            C.wait(nc.vector, ss)
            nc.vector.tensor_scalar(tt8[:], tt8[:], sms[:, 3:4], 0.0, ALU.mult, ALU.add)
            nc.vector.drain()
            nc.vector.scalar_tensor_tensor(wt, wt, sms[:, 3:4], wt, ALU.mult, ALU.subtract)
            nc.vector.drain()
            wv = C.inc("v", nc.vector.tensor_sub(wt, tt8[:], wt))
            C.wait(nc.tensor, wv)
            tr2 = nc.tensor.transpose(pstf[0:8, 0:128], wt, ident_f)
            r2 = C.inc("t", tr2)
            C.wait(nc.vector, r2)
            wb_ev = C.inc("v", nc.vector.tensor_copy(wfm[:, _ts(tt)], pstf[0:8, 0:128]))
            C.wait(nc.tensor, wb_ev)
        C.wait(nc.gpsimd, wb_ev)
        dw = C.inc("qg", nc.gpsimd.dma_start(out=wrow_sc[:, :], in_=wfm[:]), 16)
        C.wait(nc.gpsimd, dw)
        C.barrier()

    with ExitStack() as phe:
        wbe = phe.enter_context(nc.sbuf_tensor(C.u("me_wbe"), [128, n_tok], F32))
        ch_wb = C.chan(phe, "me_chwb")
        for e in range(NE):
            C.wait(nc.sync, C.tok("v"))
            lw = C.inc(ch_wb, nc.sync.dma_start(
                out=wbe[:], in_=wrow_sc[e:e + 1, :].broadcast_to([128, n_tok])), 16)
            with ExitStack() as ph:
                silu_b = [ph.enter_context(nc.sbuf_tensor(C.u(f"mA_sl{i}"), [128, 512], F32)) for i in range(2)]
                out_b = [ph.enter_context(nc.sbuf_tensor(C.u(f"mA_ou{i}"), [128, 512], BF16)) for i in range(2)]
                ch_o = [C.chan(ph, "mA_cho", kind="sw") for i in range(2)]
                free_tok = [None, None]
                kctr = [0]

                def epA(C, ot, tb, pss, rdy, aux):
                    k = kctr[0] % 2
                    kctr[0] += 1
                    C.wait(nc.scalar, rdy)
                    C.wait(nc.scalar, free_tok[k])
                    a = C.inc("s", nc.scalar.activation(silu_b[k][:], pss[0][:], AF.Silu))
                    C.wait(nc.vector, a)
                    C.wait(nc.vector, rdy)
                    u = C.inc("v", nc.vector.scalar_tensor_tensor(
                        out_b[k][:], pss[1][:], 1.0, silu_b[k][:], ALU.mult, ALU.mult))
                    C.wait(nc.gpsimd, u)
                    d = C.inc(ch_o[k], nc.gpsimd.dma_start(
                        out=h_sc[ot, :, _ts(tb, 512)], in_=out_b[k][:]), 16)
                    free_tok[k] = [u, d]
                    return [a, u]

                emit_gemm_oi(C, [wd["L1_ffn_W1"][e], wd["L1_ffn_W2"][e]], DE, D,
                             y_sc, n_tok, og=11, epilogue=epA, ident_f=ident_f,
                             name=C.u("mA"), rhs_scale=rmsw_pp)
            with ExitStack() as ph:
                t_b = [ph.enter_context(nc.sbuf_tensor(C.u(f"mB_t{i}"), [128, 512], F32)) for i in range(2)]
                xt_b = [ph.enter_context(nc.sbuf_tensor(C.u(f"mB_xt{i}"), [128, 512], F32)) for i in range(3)]
                out_b = [ph.enter_context(nc.sbuf_tensor(C.u(f"mB_ou{i}"), [128, 512], F32)) for i in range(2)]
                ch_xt = [C.chan(ph, "mB_chx") for i in range(3)]
                ch_o = [C.chan(ph, "mB_cho", kind="sw") for i in range(2)]
                xt_free = [None] * 3
                out_free = [None, None]
                t_free = [None, None]
                prectr = [0]
                epctr = [0]

                def preB(C, ot, tb):
                    k = prectr[0] % 3
                    prectr[0] += 1
                    C.wait(nc.sync, xt_free[k])
                    ld = C.inc(ch_xt[k], nc.sync.dma_start(
                        out=xt_b[k][:], in_=xT_sc[ot, :, _ts(tb, 512)]), 16)
                    return (k, ld)

                def epB(C, ot, tb, pss, rdy, aux):
                    k, ld = aux
                    m = epctr[0] % 2
                    epctr[0] += 1
                    C.wait(nc.vector, [rdy, lw, t_free[m]])
                    t1 = C.inc("v", nc.vector.tensor_mul(t_b[m][:], pss[0][:], wbe[:, _ts(tb, 512)]))
                    nc.vector.drain()
                    C.wait(nc.vector, [ld, out_free[m]])
                    u = C.inc("v", nc.vector.tensor_add(out_b[m][:], t_b[m][:], xt_b[k][:]))
                    xt_free[k] = u
                    t_free[m] = u
                    C.wait(nc.gpsimd, u)
                    d = C.inc(ch_o[m], nc.gpsimd.dma_start(
                        out=xT_sc[ot, :, _ts(tb, 512)], in_=out_b[m][:]), 16)
                    out_free[m] = d
                    return [t1]

                emit_gemm_oi(C, [wd["L1_ffn_W3"][e]], D, DE, h_sc, n_tok, og=8,
                             epilogue=epB, pre=preB, ident_f=ident_f, name=C.u("mB"))


def emit_out(C, xT_sc, y_out, n_tok, *, ident_f):
    nc = C.nc
    NT = n_tok // 128
    with ExitStack() as ph:
        sb = lambda nm, shape, dt: ph.enter_context(nc.sbuf_tensor(C.u(nm), shape, dt))
        psum = lambda nm, shape, dt: ph.enter_context(nc.psum_tensor(C.u(nm), shape, dt))
        ldf = [sb(f"ou_ld{i}", [128, D], F32) for i in range(2)]
        otm = [sb(f"ou_tm{i}", [128, D], F32) for i in range(2)]
        ps_f = [psum(f"ou_ps{i}", [128, 512], F32) for i in range(2)]
        ch_l = [C.chan(ph, "ou_chl") for i in range(2)]
        ch_s = [C.chan(ph, "ou_chs", kind="sw") for i in range(2)]
        ldf_free = [None, None]
        otm_free = [None, None]
        psf_free = [None, None]
        for tt in range(NT):
            j = tt % 2
            C.wait(nc.sync, ldf_free[j])
            ld = C.inc(ch_l[j], nc.sync.dma_start(
                out=ldf[j][:].rearrange("p (g c) -> p g c", g=PG),
                in_=xT_sc[:, :, _ts(tt)].rearrange("g p c -> p g c")), 16)
            C.wait(nc.tensor, ld)
            C.wait(nc.tensor, psf_free[j])
            ev = None
            for g0 in range(0, PG, 4):
                tr = None
                for g in range(g0, g0 + 4):
                    tr = nc.tensor.transpose(ps_f[j][:, _ts(g - g0)], ldf[j][:, _ts(g)], ident_f)
                trt = C.inc("t", tr)
                C.wait(nc.vector, trt)
                C.wait(nc.vector, otm_free[j])
                for g in range(g0, g0 + 4):
                    ev = nc.vector.tensor_copy(otm[j][:, _ts(g)], ps_f[j][:, _ts(g - g0)])
                ev = C.inc("v", ev)
                if g0 + 4 < PG:
                    C.wait(nc.tensor, ev)
            psf_free[j] = ev
            ldf_free[j] = C.tok("t")
            C.wait(nc.gpsimd, ev)
            st = C.inc(ch_s[j], nc.gpsimd.dma_start(out=y_out[_ts(tt), :], in_=otm[j][:]), 16)
            otm_free[j] = st
        C.barrier()


N_TOK_FULL = 4096
SEQ_FULL = 2048
N_CORES = 8

WSHAPES = {
    "L0_attn_W_q": (D, D), "L0_attn_W_dkv": (D, LAT), "L0_attn_W_uk": (LAT, D),
    "L0_attn_W_uv": (LAT, D), "L0_attn_W_o": (D, D),
    "L0_ffn_w1_w": (DH, D), "L0_ffn_w2_w": (DH, D), "L0_ffn_h1_w": (DH, DH),
    "L0_ffn_h2_w": (DH, DH), "L0_ffn_w3_w": (D, DH),
    "L1_attn_Wz": (D, D), "L1_attn_Wh": (D, D),
    "L1_ffn_gate_w": (NE, D), "L1_ffn_W1": (NE, DE, D), "L1_ffn_W2": (NE, DE, D),
    "L1_ffn_W3": (NE, D, DE),
}


def host_consts(seq):
    p = np.arange(128)
    dl = p % 64
    inv = 1.0 / 10000.0 ** (np.arange(32, dtype=np.float64) / 32)
    tpos = np.arange(seq, dtype=np.float64)
    ang = tpos[None, :] * inv[dl % 32][:, None]
    cos_t = np.cos(ang).astype(BF16NP)
    sin_t = np.where((dl < 32)[:, None], -np.sin(ang), np.sin(ang)).astype(BF16NP)
    perm = np.zeros((128, 128), np.float32)
    for m in range(128):
        blk = (m // 64) * 64
        perm[blk + ((m - blk + 32) % 64), m] = 1.0
    mask = np.where(np.arange(128)[:, None] >= np.arange(128)[None, :],
                    0.0, -1e30).astype(np.float32)
    cvec = np.zeros((128, 16), np.float32)
    cvec[:, 0] = LN_EPS
    cvec[:, 1] = RMS_EPS * D
    return {"c_ident_b": np.eye(128, dtype=BF16NP),
            "c_ident_f": np.eye(128, dtype=np.float32),
            "c_perm": perm.astype(BF16NP),
            "c_cos": cos_t, "c_sin": sin_t, "c_mask": mask, "c_cvec": cvec}


def pack_params(inputs):
    r = {}
    scl = np.empty((128, 136), np.float32)
    scl[:, 0:32] = inputs["L0_ffn_w1_s"].reshape(32, 128).T
    scl[:, 32:64] = inputs["L0_ffn_w2_s"].reshape(32, 128).T
    scl[:, 64:96] = inputs["L0_ffn_h1_s"].reshape(32, 128).T
    scl[:, 96:128] = inputs["L0_ffn_h2_s"].reshape(32, 128).T
    scl[:, 128:136] = inputs["L0_ffn_w3_s"].reshape(8, 128).T
    r["p_scl"] = scl
    gb = np.empty((128, 4), np.float32)
    gb[:, 0:2] = inputs["L0_attn_ln_gamma"].reshape(2, 128).T
    gb[:, 2:4] = inputs["L0_attn_ln_beta"].reshape(2, 128).T
    r["p_gam_bet"] = gb
    bp = np.empty((128, 2 * PG), np.float32)
    bp[:, 0:PG] = inputs["L1_attn_bz"].reshape(PG, 128).T
    bp[:, PG:] = inputs["L1_attn_bh"].reshape(PG, 128).T
    r["p_bias"] = bp
    r["p_rmsw"] = np.ascontiguousarray(inputs["L1_ffn_rms_w"].reshape(PG, 128).T)
    return r


def build_full(n_tok=N_TOK_FULL, seq=SEQ_FULL):
    nc = bass.Bass("TRN2", target_bir_lowering=False)
    F = {"x": nc.dram_tensor("x", [n_tok, D], F32, kind="ExternalInput").ap()}
    wd = {}
    for nm, shp in WSHAPES.items():
        wd[nm] = nc.dram_tensor(nm, list(shp), F32, kind="ExternalInput").ap()
    cshapes = {"c_ident_b": ([128, 128], BF16), "c_ident_f": ([128, 128], F32),
               "c_perm": ([128, 128], BF16), "c_cos": ([128, seq], BF16),
               "c_sin": ([128, seq], BF16), "c_mask": ([128, 128], F32),
               "c_cvec": ([128, 16], F32), "p_scl": ([128, 136], F32),
               "p_gam_bet": ([128, 4], F32), "p_bias": ([128, 2 * PG], F32),
               "p_rmsw": ([128, PG], F32)}
    cd = {nm: nc.dram_tensor(nm, list(shp), dt, kind="ExternalInput").ap()
          for nm, (shp, dt) in cshapes.items()}
    y_out = nc.dram_tensor("y", [n_tok, D], F32, kind="ExternalOutput").ap()

    xT_sc = nc.dram_tensor("xT_sc", [PG, 128, n_tok], F32, kind="Internal").ap()
    xn0_sc = nc.dram_tensor("xn0_sc", [PG, 128, n_tok], BF16, kind="Internal").ap()
    xn1_sc = nc.dram_tensor("xn1_sc", [PG, 128, n_tok], BF16, kind="Internal").ap()
    y_sc = nc.dram_tensor("ysc", [PG, 128, n_tok], BF16, kind="Internal").ap()
    u_sc = nc.dram_tensor("u_sc", [DH // 128, 128, n_tok], BF16, kind="Internal").ap()
    g_sc = nc.dram_tensor("g_sc", [DH // 128, 128, n_tok], BF16, kind="Internal").ap()
    wrow_sc = nc.dram_tensor("wrow_sc", [NE, n_tok], F32, kind="Internal").ap()

    with ExitStack() as es:
        C = Ctx(nc, es)
        cs = {}
        ld = None
        for nm, (shp, dt) in cshapes.items():
            t = es.enter_context(nc.sbuf_tensor(nm + "_sb", shp, dt))
            ld = C.inc("qs", nc.sync.dma_start(out=t[:], in_=cd[nm][:, :]), 16)
            cs[nm] = t[:]
        C.wait(nc.sync, ld)
        C.barrier()
        emit_ln(C, F["x"], xT_sc, xn0_sc, n_tok, is_ln1=False, y_sc=None,
                ident_b=cs["c_ident_b"], ident_f=cs["c_ident_f"], cst=cs["c_cvec"])
        emit_attn(C, wd, xn0_sc, xT_sc, n_tok, seq,
                  ident_b=cs["c_ident_b"], ident_f=cs["c_ident_f"],
                  perm_b=cs["c_perm"], cos_t=cs["c_cos"], sin_t=cs["c_sin"],
                  mask128=cs["c_mask"], gam_bet=cs["p_gam_bet"], cst=cs["c_cvec"])
        scl = cs["p_scl"]
        emit_ffn0(C, wd, xn0_sc, u_sc, g_sc, xT_sc, n_tok,
                  scl[:, 0:32], scl[:, 32:64], scl[:, 64:96], scl[:, 96:128],
                  scl[:, 128:136], cs["c_ident_f"])
        emit_ln(C, xT_sc, xT_sc, xn1_sc, n_tok, is_ln1=True, y_sc=y_sc,
                ident_b=cs["c_ident_b"], ident_f=cs["c_ident_f"], cst=cs["c_cvec"])
        emit_scan(C, wd, xn1_sc, xT_sc, n_tok, seq, bias_pp=cs["p_bias"])
        emit_moe(C, wd, y_sc, xT_sc, u_sc[0:DE // 128], wrow_sc, n_tok,
                 ident_b=cs["c_ident_b"], ident_f=cs["c_ident_f"],
                 rmsw_pp=cs["p_rmsw"])
        C.barrier()
        emit_out(C, xT_sc, y_out, n_tok, ident_f=cs["c_ident_f"])
    return nc


def kernel(**inputs):
    x = np.ascontiguousarray(np.asarray(inputs["L0_attn_x"], dtype=np.float32))
    B, S, _ = x.shape
    per = B // N_CORES
    consts = host_consts(S)
    params = pack_params({k: np.asarray(v, dtype=np.float32) for k, v in inputs.items()
                          if k.endswith(("_s", "gamma", "beta", "bz", "bh", "rms_w"))})
    weights = {nm: np.ascontiguousarray(np.asarray(inputs[nm], dtype=np.float32))
               for nm in WSHAPES}
    nc = build_full(per * S, S)
    in_maps = []
    for c in range(N_CORES):
        m = {"x": np.ascontiguousarray(x[c * per:(c + 1) * per].reshape(per * S, D))}
        m.update(weights)
        m.update(consts)
        m.update(params)
        in_maps.append(m)
    res = run_bass_kernel_spmd(nc, in_maps, core_ids=list(range(N_CORES)))
    out = np.concatenate([res.results[c]["y"].reshape(per, S, D)
                          for c in range(N_CORES)], axis=0)
    return out


# version 02
# speedup vs baseline: 1.0204x; 1.0204x over previous
import numpy as np
import ml_dtypes
from contextlib import ExitStack

import concourse.bass as bass
import concourse.mybir as mybir
from concourse.bass_utils import run_bass_kernel_spmd

F32 = mybir.dt.float32
BF16 = mybir.dt.bfloat16
AF = mybir.ActivationFunctionType
ALU = mybir.AluOpType
AX = mybir.AxisListType
BF16NP = ml_dtypes.bfloat16

D = 1024
PG = D // 128
NH = 16
HD = 64
LAT = 256
DH = 4096
NE = 8
DE = 2816
LN_EPS = 1e-5
RMS_EPS = 1e-6


class Ctx:

    def __init__(self, nc, es):
        self.nc = nc
        self.es = es
        self._sems = {}
        self._by_num = {}
        for nm in ("qs", "qg", "qa", "t", "v", "s", "g"):
            h = es.enter_context(nc.semaphore("sem_" + nm))
            self._sems[nm] = [h, 0]

    def inc(self, nm, instr, k=1):
        ent = self._sems[nm]
        instr.then_inc(ent[0], k)
        ent[1] += k
        return (nm, ent[1])

    def n(self, nm):
        return self._sems[nm][1]

    def tok(self, nm):
        return (nm, self._sems[nm][1])

    _uctr = 0

    def u(self, nm):
        Ctx._uctr += 1
        return f"{nm}_{Ctx._uctr}"

    POOL_HW = 44
    POOL_SW = 24

    def chan(self, scope_es, name="ch", kind="hw"):
        if not hasattr(self, "_pool"):
            self._pool = {"hw": [], "sw": []}
            self._pool_i = {"hw": 0, "sw": 0}
            for i in range(self.POOL_HW):
                key = f"hch{i}"
                h = self.es.enter_context(self.nc.semaphore(key))
                self._sems[key] = [h, 0]
                self._pool["hw"].append(key)
            for i in range(self.POOL_SW):
                key = f"sch{i}"
                h = self.es.enter_context(self.nc.semaphore(key))
                self._sems[key] = [h, 0]
                self._pool["sw"].append(key)
        pool = self._pool[kind]
        key = pool[self._pool_i[kind] % len(pool)]
        self._pool_i[kind] += 1
        return key

    def wait(self, eng, tok):
        if tok is None:
            return
        if isinstance(tok, list):
            best = {}
            for t in tok:
                if t is None:
                    continue
                nm, val = t
                best[nm] = max(best.get(nm, 0), val)
            for nm, val in best.items():
                self.wait(eng, (nm, val))
            return
        nm, val = tok
        ent = self._sems[nm]
        assert val <= ent[1], f"wait({nm},{val}) but only {ent[1]} emitted"
        if val > 0:
            eng.wait_ge(ent[0], val)

    def barrier(self):
        nc = self.nc
        for key, ent in list(self._sems.items()):
            if key in ("t", "v", "s", "g"):
                continue
            if ent[1] > 0:
                nc.vector.wait_ge(ent[0], ent[1])
        nc.all_engine_barrier()


def _ts(i, n=128):
    return slice(i * n, (i + 1) * n)


def _ceil(a, b):
    return (a + b - 1) // b


def emit_ln(C, x_in_tm, xT_sc, xn_sc, n_tok, *, is_ln1, y_sc, ident_b, ident_f, cst):
    nc = C.nc
    NT = n_tok // 128
    with ExitStack() as ph:
        sb = lambda nm, shape, dt: ph.enter_context(nc.sbuf_tensor(C.u(nm), shape, dt))
        psum = lambda nm, shape, dt: ph.enter_context(nc.psum_tensor(C.u(nm), shape, dt))
        NB = 3
        xt = [sb(f"ln_x{i}", [128, D], F32) for i in range(NB)]
        fxn = [sb(f"ln_fxn{i}", [128, D], F32) for i in range(NB)]
        xsqd = sb("ln_xsqd", [128, D], F32)
        stats = [sb(f"ln_st{i}", [128, 12], F32) for i in range(NB)]
        xn_tm = [sb(f"ln_xn{i}", [128, D], BF16) for i in range(NB)]
        xs_tm = [sb(f"ln_xs{i}", [128, D], F32) for i in range(NB)]
        y_tm = [sb(f"ln_y{i}", [128, D], BF16) for i in range(NB)]
        stg_b = [sb(f"ln_sgb{i}", [128, D], BF16) for i in range(2)]
        stg_f = [sb(f"ln_sgf{i}", [128, D], F32) for i in range(2)]
        stg_y = [sb(f"ln_sgy{i}", [128, D], BF16) for i in range(2)]
        ldf = [sb(f"ln_ldf{i}", [128, D], F32) for i in range(2)]
        ps_b = [psum(f"ln_pb{i}", [128, 1024], BF16) for i in range(2)]
        ps_f = [psum(f"ln_pf{i}", [128, 512], F32) for i in range(2)]

        x_free = [None] * NB
        st_free = [None] * NB
        buf_free = [None] * NB
        stg_free = [None] * 2
        psb_free = [None] * 2
        psf_free = [None] * 2
        ldf_free = [None] * 2
        ch_x = [C.chan(ph, "ln_chx") for i in range(NB)]
        ch_ldf = [C.chan(ph, "ln_chl") for i in range(2)]
        ch_st = [C.chan(ph, "ln_chs", kind="sw") for i in range(2)]

        for tt in range(NT):
            i = tt % NB
            j = tt % 2
            ts = _ts(tt)
            if is_ln1:
                C.wait(nc.sync, ldf_free[j])
                ld = C.inc(ch_ldf[j], nc.sync.dma_start(
                    out=ldf[j][:].rearrange("p (g c) -> p g c", g=PG),
                    in_=x_in_tm[:, :, ts].rearrange("g p c -> p g c")), 16)
                C.wait(nc.tensor, ld)
                C.wait(nc.tensor, psf_free[j])
                C.wait(nc.scalar, x_free[i])
                ev = None
                for g0 in range(0, PG, 4):
                    tr = None
                    for g in range(g0, g0 + 4):
                        tr = nc.tensor.transpose(ps_f[j][:, _ts(g - g0)], ldf[j][:, _ts(g)], ident_f)
                    tt_tok = C.inc("t", tr)
                    C.wait(nc.scalar, tt_tok)
                    for g in range(g0, g0 + 4):
                        ev = nc.scalar.copy(xt[i][:, _ts(g)], ps_f[j][:, _ts(g - g0)])
                    ev = C.inc("s", ev)
                    if g0 + 4 < PG:
                        C.wait(nc.tensor, ev)
                psf_free[j] = ev
                ldf_free[j] = C.tok("t")
                nc.scalar.drain()
                ld = ev
            else:
                C.wait(nc.sync, x_free[i])
                ld = C.inc(ch_x[i], nc.sync.dma_start(out=xt[i][:], in_=x_in_tm[ts, :]), 16)

            st = stats[i]
            s1, ssq, mean, var, rstd = st[:, 0:1], st[:, 1:2], st[:, 2:3], st[:, 3:4], st[:, 4:5]
            msq, lnv = st[:, 5:6], st[:, 6:7]
            C.wait(nc.scalar, ld)
            C.wait(nc.scalar, st_free[i])
            sq = C.inc("s", nc.scalar.activation(xsqd[:], xt[i][:], AF.Square, accum_out=ssq))
            C.wait(nc.vector, ld)
            C.wait(nc.vector, sq)
            nc.vector.tensor_reduce(s1, xt[i][:], AX.X, ALU.add)
            nc.vector.drain()
            nc.vector.tensor_scalar(mean, s1, 1.0 / D, 0.0, ALU.mult, ALU.add)
            nc.vector.drain()
            nc.vector.tensor_tensor(msq, mean, mean, ALU.mult)
            nc.vector.drain()
            vv = C.inc("v", nc.vector.scalar_tensor_tensor(var, ssq, 1.0 / D, msq, ALU.mult, ALU.subtract))
            C.wait(nc.scalar, vv)
            nc.scalar.activation(lnv, var, AF.Ln, bias=cst[:, 0:1])
            nc.scalar.drain()
            rs = C.inc("s", nc.scalar.activation(rstd, lnv, AF.Exp, scale=-0.5))
            C.wait(nc.vector, rs)
            C.wait(nc.vector, buf_free[i])
            if is_ln1:
                nc.vector.tensor_scalar(xn_tm[i][:], xt[i][:], mean, rstd, ALU.subtract, ALU.mult)
                fx = C.inc("v", nc.vector.tensor_scalar(fxn[i][:], xt[i][:], mean, rstd, ALU.subtract, ALU.mult))
                C.wait(nc.scalar, fx)
                sq2 = C.inc("s", nc.scalar.activation(xsqd[:], fxn[i][:], AF.Square, accum_out=st[:, 7:8]))
                nc.scalar.drain()
                nc.scalar.activation(st[:, 8:9], st[:, 7:8], AF.Ln, bias=cst[:, 1:2])
                nc.scalar.drain()
                rs2 = C.inc("s", nc.scalar.activation(st[:, 9:10], st[:, 8:9], AF.Exp, scale=-0.5))
                C.wait(nc.vector, rs2)
                nc.vector.tensor_scalar(st[:, 10:11], st[:, 9:10], float(np.sqrt(D)), 0.0, ALU.mult, ALU.add)
                nc.vector.drain()
                nc.vector.tensor_scalar(y_tm[i][:], fxn[i][:], st[:, 10:11], 0.0, ALU.mult, ALU.add)
                apl = C.inc("v", nc.vector.tensor_add(xs_tm[i][:], xt[i][:], fxn[i][:]))
                sq2_tok = sq2
            else:
                apl = C.inc("v", nc.vector.tensor_scalar(xn_tm[i][:], xt[i][:], mean, rstd, ALU.subtract, ALU.mult))
                sq2_tok = None
            st_free[i] = apl

            src_f = xs_tm[i] if is_ln1 else xt[i]
            C.wait(nc.tensor, apl)
            C.wait(nc.tensor, psb_free[j])
            C.wait(nc.vector, stg_free[j])
            C.wait(nc.scalar, stg_free[j])
            tr = None
            for g in range(PG):
                tr = nc.tensor.transpose(ps_b[j][:, _ts(g)], xn_tm[i][:, _ts(g)], ident_b)
            tb = C.inc("t", tr)
            C.wait(nc.vector, tb)
            ev_b = None
            for g in range(PG):
                ev_b = nc.vector.tensor_copy(stg_b[j][:, _ts(g)], ps_b[j][:, _ts(g)])
            ev_b = C.inc("v", ev_b)
            psb_free[j] = ev_b
            C.wait(nc.tensor, psf_free[j])
            ev_f = None
            for g0 in range(0, PG, 4):
                tr = None
                for g in range(g0, g0 + 4):
                    tr = nc.tensor.transpose(ps_f[j][:, _ts(g - g0)], src_f[:, _ts(g)], ident_f)
                tf = C.inc("t", tr)
                C.wait(nc.scalar, tf)
                for g in range(g0, g0 + 4):
                    ev_f = nc.scalar.copy(stg_f[j][:, _ts(g)], ps_f[j][:, _ts(g - g0)])
                ev_f = C.inc("s", ev_f)
                if g0 + 4 < PG:
                    C.wait(nc.tensor, ev_f)
            psf_free[j] = ev_f
            ev_y = None
            if is_ln1:
                C.wait(nc.tensor, ev_b)
                tr = None
                for g in range(PG):
                    tr = nc.tensor.transpose(ps_b[j][:, _ts(g)], y_tm[i][:, _ts(g)], ident_b)
                ty = C.inc("t", tr)
                C.wait(nc.vector, ty)
                for g in range(PG):
                    ev_y = nc.vector.tensor_copy(stg_y[j][:, _ts(g)], ps_b[j][:, _ts(g)])
                ev_y = C.inc("v", ev_y)
                psb_free[j] = ev_y
            t_done = C.tok("t")
            buf_free[i] = [t_done, sq2_tok]
            x_free[i] = [apl, sq, t_done]

            C.wait(nc.gpsimd, ev_b if ev_y is None else ev_y)
            C.wait(nc.gpsimd, ev_f)
            C.inc(ch_st[j], nc.gpsimd.dma_start(
                out=xn_sc[:, :, ts].rearrange("g p c -> p g c"),
                in_=stg_b[j][:].rearrange("p (g c) -> p g c", g=PG)), 16)
            sg = C.inc(ch_st[j], nc.gpsimd.dma_start(
                out=xT_sc[:, :, ts].rearrange("g p c -> p g c"),
                in_=stg_f[j][:].rearrange("p (g c) -> p g c", g=PG)), 16)
            if is_ln1:
                sg = C.inc(ch_st[j], nc.gpsimd.dma_start(
                    out=y_sc[:, :, ts].rearrange("g p c -> p g c"),
                    in_=stg_y[j][:].rearrange("p (g c) -> p g c", g=PG)), 16)
            stg_free[j] = sg
        C.barrier()


def emit_gemm_oi(C, Ws, O, I, in_sc, n_tok, *, og, epilogue, pre=None,
                 ident_f, name, tb_bytes=512, rhs_scale=None):
    nc = C.nc
    O_t, I_t, NTB = O // 128, I // 128, n_tok // 512
    NW = len(Ws)
    with ExitStack() as ph:
        sb = lambda nm, shape, dt: ph.enter_context(nc.sbuf_tensor(C.u(name + nm), shape, dt))
        psum = lambda nm, shape, dt: ph.enter_context(nc.psum_tensor(C.u(name + nm), shape, dt))
        wstage = [sb(f"_wst{i}", [128, I], F32) for i in range(2)]
        wT = [sb(f"_wT{w}", [128, og * I_t * 128], BF16) for w in range(NW)]
        inb = [sb(f"_in{i}", [128, I_t * 512], BF16) for i in range(2)]
        wps = [psum(f"_wps{i}", [128, 512], F32) for i in range(2)]
        acc = [psum(f"_acc{i}", [128, 512], F32) for i in range(2 * NW)]

        wstage_free = [None] * 2
        wps_free = [None] * 2
        inb_free = [None] * 2
        acc_free = [None] * (2 * NW)
        group_mm_done = None
        ch_w = [C.chan(ph, name + "_chw") for i in range(2)]
        ch_in = [C.chan(ph, name + "_chi") for i in range(2)]

        n_groups = _ceil(O_t, og)
        for grp in range(n_groups):
            ots = list(range(grp * og, min((grp + 1) * og, O_t)))
            k = 0
            for w in range(NW):
                for osl, ot in enumerate(ots):
                    sl = k % 2
                    k += 1
                    C.wait(nc.sync, wstage_free[sl])
                    ld = C.inc(ch_w[sl], nc.sync.dma_start(
                        out=wstage[sl][:], in_=Ws[w][_ts(ot), :]), 16)
                    C.wait(nc.tensor, ld)
                    ev = None
                    for it0 in range(0, I_t, 4):
                        b = (it0 // 4) % 2
                        C.wait(nc.tensor, wps_free[b])
                        tr = None
                        nsub = min(4, I_t - it0)
                        for q in range(nsub):
                            tr = nc.tensor.transpose(
                                wps[b][:, _ts(q)], wstage[sl][:, _ts(it0 + q)], ident_f)
                        trt = C.inc("t", tr)
                        C.wait(nc.vector, trt)
                        if group_mm_done is not None:
                            C.wait(nc.vector, group_mm_done)
                        for q in range(nsub):
                            if rhs_scale is None:
                                ev = nc.vector.tensor_copy(
                                    wT[w][:, _ts(osl * I_t + it0 + q)], wps[b][:, _ts(q)])
                            else:
                                ev = nc.vector.tensor_scalar(
                                    wT[w][:, _ts(osl * I_t + it0 + q)], wps[b][:, _ts(q)],
                                    rhs_scale[:, it0 + q:it0 + q + 1], 0.0, ALU.mult, ALU.add)
                        ev = C.inc("v", ev)
                        wps_free[b] = ev
                    wstage_free[sl] = C.tok("t")
                    wT_ready = ev
            wT_tok = wT_ready

            for tb in range(NTB):
                sl = tb % 2
                C.wait(nc.sync, inb_free[sl])
                ldi = C.inc(ch_in[sl], nc.sync.dma_start(
                    out=inb[sl][:].rearrange("p (g c) -> p g c", g=I_t),
                    in_=in_sc[:, :, _ts(tb, 512)].rearrange("g p c -> p g c")), 16)
                C.wait(nc.tensor, ldi)
                if tb == 0:
                    C.wait(nc.tensor, wT_tok)
                last_mm = None
                for osl, ot in enumerate(ots):
                    aux = pre(C, ot, tb) if pre is not None else None
                    pss = []
                    for w in range(NW):
                        a = (((tb * len(ots) + osl) % 2) * NW) + w
                        C.wait(nc.tensor, acc_free[a])
                        for it in range(I_t):
                            last_mm = nc.tensor.matmul(
                                acc[a][:], wT[w][:, _ts(osl * I_t + it)],
                                inb[sl][:, _ts(it, 512)],
                                start=(it == 0), stop=(it == I_t - 1))
                        pss.append(acc[a])
                    rdy = C.inc("t", last_mm)
                    frees = epilogue(C, ot, tb, pss, rdy, aux)
                    for w in range(NW):
                        a = (((tb * len(ots) + osl) % 2) * NW) + w
                        acc_free[a] = frees
                inb_free[sl] = C.tok("t")
            group_mm_done = C.tok("t")
        C.barrier()


def emit_ffn0(C, wd, xn_sc, u_sc, g_sc, xT_sc, n_tok, sc_w1, sc_w2, sc_h1,
              sc_h2, sc_w3, ident_f, stages="ABCD"):
    nc = C.nc

    if "A" in stages:
      with ExitStack() as ph:
        silu_b = [ph.enter_context(nc.sbuf_tensor(C.u(f"fA_sl{i}"), [128, 512], F32)) for i in range(2)]
        out_b = [ph.enter_context(nc.sbuf_tensor(C.u(f"fA_ou{i}"), [128, 512], BF16)) for i in range(2)]
        ch_o = [C.chan(ph, "fA_cho", kind="sw") for i in range(2)]
        free_tok = [None, None]
        kctr = [0]

        def epA(C, ot, tb, pss, rdy, aux):
            k = kctr[0] % 2
            kctr[0] += 1
            C.wait(nc.scalar, rdy)
            C.wait(nc.scalar, free_tok[k])
            a = C.inc("s", nc.scalar.activation(silu_b[k][:], pss[0][:], AF.Silu,
                                                scale=sc_w1[:, ot:ot + 1]))
            C.wait(nc.vector, a)
            C.wait(nc.vector, rdy)
            u = C.inc("v", nc.vector.scalar_tensor_tensor(
                out_b[k][:], pss[1][:], sc_w2[:, ot:ot + 1], silu_b[k][:],
                ALU.mult, ALU.mult))
            C.wait(nc.gpsimd, u)
            d = C.inc(ch_o[k], nc.gpsimd.dma_start(
                out=u_sc[ot, :, _ts(tb, 512)], in_=out_b[k][:]), 16)
            free_tok[k] = [u, d]
            return [a, u]

        emit_gemm_oi(C, [wd["L0_ffn_w1_w"], wd["L0_ffn_w2_w"]], DH, D, xn_sc,
                     n_tok, og=8, epilogue=epA, ident_f=ident_f, name="fA")

    for tag, wkey, scl, src, dst in (("fB", "L0_ffn_h1_w", sc_h1, u_sc, g_sc),
                                     ("fC", "L0_ffn_h2_w", sc_h2, g_sc, u_sc)):
        if tag[1] not in stages:
            continue
        with ExitStack() as ph:
            out_b = [ph.enter_context(nc.sbuf_tensor(C.u(f"{tag}_ou{i}"), [128, 512], BF16)) for i in range(2)]
            ch_o = [C.chan(ph, tag + "_cho", kind="sw") for i in range(2)]
            free_tok = [None, None]
            kctr = [0]

            def epB(C, ot, tb, pss, rdy, aux, out_b=out_b, free_tok=free_tok,
                    kctr=kctr, scl=scl, dst=dst, ch_o=ch_o):
                k = kctr[0] % 2
                kctr[0] += 1
                C.wait(nc.scalar, rdy)
                C.wait(nc.scalar, free_tok[k])
                a = C.inc("s", nc.scalar.activation(out_b[k][:], pss[0][:], AF.Gelu,
                                                    scale=scl[:, ot:ot + 1]))
                C.wait(nc.gpsimd, a)
                d = C.inc(ch_o[k], nc.gpsimd.dma_start(
                    out=dst[ot, :, _ts(tb, 512)], in_=out_b[k][:]), 16)
                free_tok[k] = [d]
                return [a]

            emit_gemm_oi(C, [wd[wkey]], DH, DH, src, n_tok, og=8, epilogue=epB,
                         ident_f=ident_f, name=tag)

    if "D" in stages:
      with ExitStack() as ph:
        xt_b = [ph.enter_context(nc.sbuf_tensor(C.u(f"fD_xt{i}"), [128, 512], F32)) for i in range(3)]
        out_b = [ph.enter_context(nc.sbuf_tensor(C.u(f"fD_ou{i}"), [128, 512], F32)) for i in range(2)]
        ch_xt = [C.chan(ph, "fD_chx", kind="sw") for i in range(3)]
        ch_o = [C.chan(ph, "fD_cho", kind="sw") for i in range(2)]
        xt_free = [None] * 3
        out_free = [None, None]
        prectr = [0]
        epctr = [0]

        def preD(C, ot, tb):
            k = prectr[0] % 3
            prectr[0] += 1
            C.wait(nc.gpsimd, xt_free[k])
            ld = C.inc(ch_xt[k], nc.gpsimd.dma_start(
                out=xt_b[k][:], in_=xT_sc[ot, :, _ts(tb, 512)]), 16)
            return (k, ld)

        def epD(C, ot, tb, pss, rdy, aux):
            k, ld = aux
            m = epctr[0] % 2
            epctr[0] += 1
            C.wait(nc.vector, rdy)
            C.wait(nc.vector, ld)
            C.wait(nc.vector, out_free[m])
            u = C.inc("v", nc.vector.scalar_tensor_tensor(
                out_b[m][:], pss[0][:], sc_w3[:, ot:ot + 1], xt_b[k][:],
                ALU.mult, ALU.add))
            xt_free[k] = u
            C.wait(nc.gpsimd, u)
            d = C.inc(ch_o[m], nc.gpsimd.dma_start(
                out=xT_sc[ot, :, _ts(tb, 512)], in_=out_b[m][:]), 16)
            out_free[m] = [d]
            return [u]

        emit_gemm_oi(C, [wd["L0_ffn_w3_w"]], D, DH, u_sc, n_tok, og=8,
                     epilogue=epD, pre=preD, ident_f=ident_f, name="fD")


def load_weight_cast(C, ph, W_dram, I, O, name):
    nc = C.nc
    I_t = I // 128
    dst = ph.enter_context(nc.sbuf_tensor(C.u(name), [128, I_t * O], BF16))
    stage = [ph.enter_context(nc.sbuf_tensor(C.u(name + f"_st{i}"), [128, O], F32)) for i in range(2)]
    ch_st = [C.chan(ph, name + "_ch") for i in range(2)]
    st_free = [None, None]
    ev = None
    for it in range(I_t):
        sl = it % 2
        C.wait(nc.sync, st_free[sl])
        ld = C.inc(ch_st[sl], nc.sync.dma_start(out=stage[sl][:], in_=W_dram[_ts(it), :]), 16)
        C.wait(nc.vector, ld)
        ev = C.inc("v", nc.vector.tensor_copy(dst[:, it * O:(it + 1) * O], stage[sl][:]))
        st_free[sl] = ev
    return dst, ev


def emit_attn(C, wd, xn_sc, xT_sc, n_tok, seq, *, ident_b, ident_f, perm_b,
              cos_t, sin_t, mask128, gam_bet, cst, dbg=None, upto=3):
    nc = C.nc
    n_batch = n_tok // seq
    SB = seq // 512
    ST = seq // 128
    LT = LAT // 128
    with ExitStack() as ph:
        sb = lambda nm, shape, dt: ph.enter_context(nc.sbuf_tensor(C.u(nm), shape, dt))
        psum = lambda nm, shape, dt: ph.enter_context(nc.psum_tensor(C.u(nm), shape, dt))
        q_sb = sb("at_q", [128, PG * seq], BF16)
        k_sb = sb("at_k", [128, PG * seq], BF16)
        v_tm = sb("at_v", [128, ST * D], BF16)
        inb = [sb(f"at_in{i}", [128, PG * 512], BF16) for i in range(2)]
        ch_in = [C.chan(ph, "at_chi") for i in range(2)]
        ch_xt = [C.chan(ph, "at_chx") for i in range(2)]
        ch_o = [C.chan(ph, "at_cho", kind="sw") for i in range(2)]

        for b in range(n_batch):
            cols = lambda tb: slice((b * SB + tb) * 512, (b * SB + tb + 1) * 512)

            with ExitStack() as s1:
                sb1 = lambda nm, shape, dt: s1.enter_context(nc.sbuf_tensor(C.u(nm), shape, dt))
                ps1 = lambda nm, shape, dt: s1.enter_context(nc.psum_tensor(C.u(nm), shape, dt))
                wdkv, wtok = load_weight_cast(C, s1, wd["L0_attn_W_dkv"], D, LAT, "at_wdkv")
                wuk, wtok2 = load_weight_cast(C, s1, wd["L0_attn_W_uk"], LAT, D, "at_wuk")
                wuv, wtok3 = load_weight_cast(C, s1, wd["L0_attn_W_uv"], LAT, D, "at_wuv")
                latn = sb1("at_latn", [128, LT * seq], BF16)
                lat_c = sb1("at_latc", [128, LT * 512], F32)
                sqd = sb1("at_sqd", [128, 512], F32)
                srow = sb1("at_srow", [1, 512], F32)
                srow2 = sb1("at_srow2", [1, 512], F32)
                mbc = sb1("at_mbc", [128, 512], F32)
                rsc = sb1("at_rsc", [128, 512], F32)
                ones128 = sb1("at_o128", [128, 1], F32)
                ones1 = sb1("at_o1", [1, 128], F32)
                acc = [ps1(f"at_ac{i}", [128, 512], F32) for i in range(4)]
                ps_row = ps1("at_pr", [1, 512], F32)
                ps_bc = ps1("at_bc", [128, 512], F32)
                ms = C.inc("v", nc.vector.memset(ones128[:], 1.0))
                ms = C.inc("v", nc.vector.memset(ones1[:], 1.0))
                nc.vector.drain()
                C.wait(nc.tensor, ms)
                C.wait(nc.tensor, [wtok, wtok2, wtok3])
                inb_free = [C.tok("t"), C.tok("t")]
                acc_free = [None] * 4
                lat_free = None
                bc_free = None
                latn_done = None
                for tb in range(SB):
                    sl = tb % 2
                    C.wait(nc.sync, inb_free[sl])
                    ldi = C.inc(ch_in[sl], nc.sync.dma_start(
                        out=inb[sl][:].rearrange("p (g c) -> p g c", g=PG),
                        in_=xn_sc[:, :, cols(tb)].rearrange("g p c -> p g c")), 16)
                    C.wait(nc.tensor, ldi)
                    lat_ev = None
                    for ot in range(LT):
                        a = ot % 2
                        C.wait(nc.tensor, acc_free[a])
                        mm = None
                        for it in range(PG):
                            mm = nc.tensor.matmul(
                                acc[a][:], wdkv[:, it * LAT + ot * 128:it * LAT + (ot + 1) * 128],
                                inb[sl][:, _ts(it, 512)], start=(it == 0), stop=(it == PG - 1))
                        rdy = C.inc("t", mm)
                        C.wait(nc.vector, rdy)
                        C.wait(nc.vector, lat_free)
                        lat_ev = C.inc("v", nc.vector.tensor_copy(lat_c[:, _ts(ot, 512)], acc[a][:]))
                        acc_free[a] = lat_ev
                    inb_free[sl] = C.tok("t")
                    C.wait(nc.tensor, lat_ev)
                    mm = None
                    for g in range(LT):
                        mm = nc.tensor.matmul(ps_row[:], ones128[:], lat_c[:, _ts(g, 512)],
                                              start=(g == 0), stop=(g == LT - 1))
                    r1 = C.inc("t", mm)
                    C.wait(nc.vector, r1)
                    e1 = C.inc("v", nc.vector.tensor_copy(srow[0:1, :], ps_row[0:1, :]))
                    C.wait(nc.tensor, e1)
                    sq_rd = None
                    r2 = None
                    for g in range(LT):
                        C.wait(nc.vector, sq_rd)
                        lslc = lat_c[:, _ts(g, 512)]
                        sq = C.inc("v", nc.vector.tensor_mul(sqd[:], lslc, lslc))
                        C.wait(nc.tensor, sq)
                        r2 = C.inc("t", nc.tensor.matmul(ps_row[:], ones128[:], sqd[:],
                                                         start=(g == 0), stop=(g == LT - 1)))
                        sq_rd = r2
                    C.wait(nc.vector, r2)
                    e2 = C.inc("v", nc.vector.tensor_copy(srow2[0:1, :], ps_row[0:1, :]))
                    C.wait(nc.tensor, e2)
                    nc.vector.drain()
                    nc.vector.tensor_scalar(srow[0:1, :], srow[0:1, :], 1.0 / LAT, 0.0, ALU.mult, ALU.add)
                    nc.vector.drain()
                    nc.vector.tensor_scalar(srow2[0:1, :], srow2[0:1, :], 1.0 / LAT, 0.0, ALU.mult, ALU.add)
                    nc.vector.drain()
                    nc.vector.tensor_mul(sqd[0:1, :], srow[0:1, :], srow[0:1, :])
                    nc.vector.drain()
                    vv = C.inc("v", nc.vector.tensor_sub(srow2[0:1, :], srow2[0:1, :], sqd[0:1, :]))
                    C.wait(nc.scalar, vv)
                    nc.scalar.activation(srow2[0:1, :], srow2[0:1, :], AF.Ln, bias=cst[0:1, 0:1])
                    nc.scalar.drain()
                    rr = C.inc("s", nc.scalar.activation(srow2[0:1, :], srow2[0:1, :], AF.Exp, scale=-0.5))
                    C.wait(nc.tensor, rr)
                    C.wait(nc.tensor, bc_free)
                    mm = nc.tensor.matmul(ps_bc[:], ones1[:], srow[0:1, :], start=True, stop=True)
                    r = C.inc("t", mm)
                    C.wait(nc.vector, r)
                    b1 = C.inc("v", nc.vector.tensor_copy(mbc[:], ps_bc[:]))
                    C.wait(nc.tensor, b1)
                    mm = nc.tensor.matmul(ps_bc[:], ones1[:], srow2[0:1, :], start=True, stop=True)
                    r = C.inc("t", mm)
                    C.wait(nc.vector, r)
                    b2 = C.inc("v", nc.vector.tensor_copy(rsc[:], ps_bc[:]))
                    bc_free = b2
                    for g in range(LT):
                        nc.vector.tensor_sub(lat_c[:, _ts(g, 512)], lat_c[:, _ts(g, 512)], mbc[:])
                        nc.vector.drain()
                        nc.vector.tensor_mul(lat_c[:, _ts(g, 512)], lat_c[:, _ts(g, 512)], rsc[:])
                        nc.vector.drain()
                        latn_done = C.inc("v", nc.vector.tensor_scalar(
                            latn[:, g * seq + tb * 512: g * seq + (tb + 1) * 512],
                            lat_c[:, _ts(g, 512)],
                            gam_bet[:, g:g + 1], gam_bet[:, LT + g:LT + g + 1], ALU.mult, ALU.add))
                    lat_free = latn_done
                bdone = latn_done
                C.wait(nc.tensor, bdone)
                vtmp = sb1("at_vtmp", [128, 512], BF16)
                ps_tr = ps1("at_ptr", [128, 1024], BF16)
                vfm_dbg = sb1("at_vfm", [128, PG * seq], BF16) if (dbg is not None and "vfm" in dbg) else None
                for tb in range(SB):
                    for ot in range(PG):
                        a = (tb * PG + ot) % 4
                        C.wait(nc.tensor, acc_free[a])
                        mm = None
                        for it in range(LT):
                            mm = nc.tensor.matmul(
                                acc[a][:], wuk[:, it * D + ot * 128:it * D + (ot + 1) * 128],
                                latn[:, it * seq + tb * 512: it * seq + (tb + 1) * 512],
                                start=(it == 0), stop=(it == LT - 1))
                        rdy = C.inc("t", mm)
                        C.wait(nc.vector, rdy)
                        evk = C.inc("v", nc.vector.tensor_copy(
                            k_sb[:, ot * seq + tb * 512: ot * seq + (tb + 1) * 512], acc[a][:]))
                        C.wait(nc.tensor, evk)
                        mm = None
                        for it in range(LT):
                            mm = nc.tensor.matmul(
                                acc[a][:], wuv[:, it * D + ot * 128:it * D + (ot + 1) * 128],
                                latn[:, it * seq + tb * 512: it * seq + (tb + 1) * 512],
                                start=(it == 0), stop=(it == LT - 1))
                        rdy = C.inc("t", mm)
                        C.wait(nc.vector, rdy)
                        if dbg is not None and "vfm" in dbg:
                            nc.vector.tensor_copy(
                                vfm_dbg[:, ot * seq + tb * 512: ot * seq + (tb + 1) * 512], acc[a][:])
                        evv = C.inc("v", nc.vector.tensor_copy(vtmp[:], acc[a][:]))
                        C.wait(nc.tensor, evv)
                        trr = None
                        for c in range(4):
                            trr = nc.tensor.transpose(ps_tr[:, _ts(c)], vtmp[:, _ts(c)], ident_b)
                        trt = C.inc("t", trr)
                        C.wait(nc.vector, trt)
                        evt = None
                        for c in range(4):
                            kt = tb * 4 + c
                            evt = nc.vector.tensor_copy(
                                v_tm[:, kt * D + ot * 128: kt * D + (ot + 1) * 128], ps_tr[:, _ts(c)])
                        evt = C.inc("v", evt)
                        acc_free[a] = evt
                        C.wait(nc.tensor, evt)
                kv_barrier_marker = True
                C.barrier()
                if dbg is not None and "vfm" in dbg:
                    ddv = C.tok("v")
                    C.wait(nc.gpsimd, ddv)
                    dv2 = C.inc("qg", nc.gpsimd.dma_start(out=dbg["vfm"][b, :, :], in_=vfm_dbg[:]), 16)
                    C.wait(nc.gpsimd, dv2)
                    C.barrier()
            with ExitStack() as s2:
                sb2 = lambda nm, shape, dt: s2.enter_context(nc.sbuf_tensor(C.u(nm), shape, dt))
                ps2 = lambda nm, shape, dt: s2.enter_context(nc.psum_tensor(C.u(nm), shape, dt))
                wq, wtok = load_weight_cast(C, s2, wd["L0_attn_W_q"], D, D, "at_wq")
                acc = [ps2(f"at_qa{i}", [128, 512], F32) for i in range(2)]
                ps_sw = ps2("at_sw", [128, 512], F32)
                t1 = sb2("at_t1", [128, 512], F32)
                t2 = sb2("at_t2", [128, 512], F32)
                acc_free = [None] * 2
                inb_free = [C.tok("t"), C.tok("t")]
                C.wait(nc.tensor, wtok)
                for tb in range(SB):
                    sl = tb % 2
                    C.wait(nc.sync, inb_free[sl])
                    ldi = C.inc(ch_in[sl], nc.sync.dma_start(
                        out=inb[sl][:].rearrange("p (g c) -> p g c", g=PG),
                        in_=xn_sc[:, :, cols(tb)].rearrange("g p c -> p g c")), 16)
                    C.wait(nc.tensor, ldi)
                    for ot in range(PG):
                        a = (tb * PG + ot) % 2
                        C.wait(nc.tensor, acc_free[a])
                        mm = None
                        for it in range(PG):
                            mm = nc.tensor.matmul(
                                acc[a][:], wq[:, it * D + ot * 128:it * D + (ot + 1) * 128],
                                inb[sl][:, _ts(it, 512)], start=(it == 0), stop=(it == PG - 1))
                        rdy = C.inc("t", mm)
                        C.wait(nc.vector, rdy)
                        ev = C.inc("v", nc.vector.tensor_copy(
                            q_sb[:, ot * seq + tb * 512: ot * seq + (tb + 1) * 512], acc[a][:]))
                        acc_free[a] = ev
                    inb_free[sl] = C.tok("t")
                if dbg is not None and "q0" in dbg:
                    dd = C.tok("v")
                    C.wait(nc.gpsimd, dd)
                    dq = C.inc("qg", nc.gpsimd.dma_start(out=dbg["q0"][b, :, :], in_=q_sb[:]), 16)
                    C.wait(nc.gpsimd, dq)
                    C.barrier()
                rope_done = C.tok("v")
                for which, buf in (("q", q_sb), ("k", k_sb)):
                    for g in range(PG):
                        for tc in range(SB):
                            slc = slice(g * seq + tc * 512, g * seq + (tc + 1) * 512)
                            C.wait(nc.tensor, rope_done)
                            mm = nc.tensor.matmul(ps_sw[:], perm_b, buf[:, slc], start=True, stop=True)
                            r = C.inc("t", mm)
                            C.wait(nc.vector, r)
                            nc.vector.tensor_mul(t1[:], buf[:, slc], cos_t[:, _ts(tc, 512)])
                            nc.vector.tensor_mul(t2[:], ps_sw[:], sin_t[:, _ts(tc, 512)])
                            nc.vector.drain()
                            rope_done = C.inc("v", nc.vector.tensor_add(buf[:, slc], t1[:], t2[:]))
                C.barrier()
            if dbg is not None:
                dd = C.tok("v")
                C.wait(nc.gpsimd, dd)
                C.wait(nc.gpsimd, C.tok("s"))
                dq = C.inc("qg", nc.gpsimd.dma_start(out=dbg["q"][b, :, :], in_=q_sb[:]), 16)
                dq = C.inc("qg", nc.gpsimd.dma_start(out=dbg["k"][b, :, :], in_=k_sb[:]), 16)
                dq = C.inc("qg", nc.gpsimd.dma_start(out=dbg["v"][b, :, :], in_=v_tm[:]), 16)
                C.wait(nc.gpsimd, dq)
                C.barrier()
            if upto < 3:
                continue
            with ExitStack() as s3:
                sb3 = lambda nm, shape, dt: s3.enter_context(nc.sbuf_tensor(C.u(nm), shape, dt))
                ps3 = lambda nm, shape, dt: s3.enter_context(nc.psum_tensor(C.u(nm), shape, dt))
                wo, wtok = load_weight_cast(C, s3, wd["L0_attn_W_o"], D, D, "at_wo")
                ps_S = ps3("at_S", [128, min(seq, 2048)], F32)
                ps_tr2 = ps3("at_tr2", [128, 2048], BF16)
                ps_pv = ps3("at_pv", [64, 512], F32)
                ps_wo = ps3("at_wo_ps", [128, 512], F32)
                prow = [sb3(f"at_prow{i}", [128, min(seq, 2048)], BF16) for i in range(2)]
                PT = sb3("at_PT", [128, ST * 512], BF16)
                attnT = sb3("at_attnT", [128, PG * 512], BF16)
                sms = sb3("at_sms", [128, 8], F32)
                xt_b = [sb3(f"at_xt{i}", [128, 512], F32) for i in range(2)]
                out_b = [sb3(f"at_ou{i}", [128, 512], F32) for i in range(2)]
                rope_all = C.tok("v")
                C.wait(nc.tensor, rope_all)
                C.wait(nc.tensor, wtok)
                bank_free = [None] * 4
                prow_free = [None, None]
                tr_free = None
                pv_free = None
                wo_free = None
                xt_free = [None, None]
                out_free = [None, None]
                attnT_free = None
                for qb in range(SB):
                    for h in range(NH):
                        hp = (h % 2) * 64
                        hg = h // 2
                        C.wait(nc.gpsimd, pv_free)
                        gz = C.inc("g", nc.gpsimd.memset(
                            PT[:, (qb * 4) * 512:(qb * 4 + 4) * 512], 0.0))
                        pend_tr = None
                        for qt4 in range(4):
                            qt = qb * 4 + qt4
                            valid = (qt + 1) * 128
                            nkb = _ceil(valid, 512)
                            last_s = None
                            for kb in range(nkb):
                                C.wait(nc.tensor, bank_free[kb])
                                last_s = nc.tensor.matmul(
                                    ps_S[:, _ts(kb, 512)],
                                    q_sb[hp:hp + 64, hg * seq + qt * 128: hg * seq + (qt + 1) * 128],
                                    k_sb[hp:hp + 64, hg * seq + kb * 512: hg * seq + (kb + 1) * 512],
                                    start=True, stop=True)
                            smm = C.inc("t", last_s)
                            if pend_tr is not None:
                                pqt4, ppn = pend_tr
                                pqt = qb * 4 + pqt4
                                C.wait(nc.tensor, ppn)
                                C.wait(nc.tensor, tr_free)
                                tr = None
                                for kt in range(pqt + 1):
                                    tr = nc.tensor.transpose(ps_tr2[:, _ts(kt)],
                                                             prow[pqt4 % 2][:, _ts(kt)], ident_b)
                                trt = C.inc("t", tr)
                                prow_free[pqt4 % 2] = trt
                                C.wait(nc.vector, trt)
                                C.wait(nc.vector, gz)
                                tr_free = C.inc("v", nc.vector.tensor_copy(
                                    PT[:].rearrange("p (k c) -> p k c", c=512)[:, 0:pqt + 1, pqt4 * 128: pqt4 * 128 + 128],
                                    ps_tr2[:].rearrange("p (k c) -> p k c", c=128)[:, 0:pqt + 1, :]))
                            C.wait(nc.vector, smm)
                            dg = slice(qt * 128, qt * 128 + 128)
                            nc.vector.tensor_add(ps_S[:, dg], ps_S[:, dg], mask128)
                            nc.vector.drain()
                            nc.vector.tensor_reduce(sms[:, 0:1], ps_S[:, 0:valid], AX.X, ALU.max, negate=True)
                            nc.vector.drain()
                            nm = C.inc("v", nc.vector.tensor_scalar(sms[:, 1:2], sms[:, 0:1], 0.125, 0.0, ALU.mult, ALU.add))
                            C.wait(nc.scalar, nm)
                            C.wait(nc.scalar, prow_free[qt4 % 2])
                            for kb in range(nkb):
                                hi = min(valid, (kb + 1) * 512)
                                ex = C.inc("s", nc.scalar.activation(
                                    prow[qt4 % 2][:, kb * 512:hi], ps_S[:, kb * 512:hi], AF.Exp,
                                    bias=sms[:, 1:2], scale=0.125, accum_out=sms[:, 4 + kb:5 + kb]))
                                bank_free[kb] = ex
                            nc.scalar.drain()
                            C.wait(nc.vector, C.tok("s"))
                            nc.vector.tensor_reduce(sms[:, 2:3], sms[:, 4:4 + nkb], AX.X, ALU.add)
                            nc.vector.drain()
                            nc.vector.reciprocal(sms[:, 3:4], sms[:, 2:3])
                            nc.vector.drain()
                            pn = C.inc("v", nc.vector.tensor_scalar(
                                prow[qt4 % 2][:, 0:valid], prow[qt4 % 2][:, 0:valid], sms[:, 3:4], 0.0, ALU.mult, ALU.add))
                            prow_free[qt4 % 2] = None
                            pend_tr = (qt4, pn)
                        pqt4, ppn = pend_tr
                        pqt = qb * 4 + pqt4
                        C.wait(nc.tensor, ppn)
                        C.wait(nc.tensor, tr_free)
                        tr = None
                        for kt in range(pqt + 1):
                            tr = nc.tensor.transpose(ps_tr2[:, _ts(kt)],
                                                     prow[pqt4 % 2][:, _ts(kt)], ident_b)
                        trt = C.inc("t", tr)
                        C.wait(nc.vector, trt)
                        C.wait(nc.vector, gz)
                        tr_free = C.inc("v", nc.vector.tensor_copy(
                            PT[:].rearrange("p (k c) -> p k c", c=512)[:, 0:pqt + 1, pqt4 * 128: pqt4 * 128 + 128],
                            ps_tr2[:].rearrange("p (k c) -> p k c", c=128)[:, 0:pqt + 1, :]))
                        prow_free[0] = C.tok("t")
                        prow_free[1] = C.tok("t")
                        pt_done = tr_free
                        C.wait(nc.tensor, pt_done)
                        C.wait(nc.tensor, pv_free)
                        mm = None
                        nkt = qb * 4 + 4
                        for kt in range(nkt):
                            mm = nc.tensor.matmul(
                                ps_pv[:, :],
                                v_tm[:, kt * D + h * 64: kt * D + (h + 1) * 64],
                                PT[:, kt * 512:(kt + 1) * 512],
                                start=(kt == 0), stop=(kt == nkt - 1))
                        pvt = C.inc("t", mm)
                        C.wait(nc.vector, pvt)
                        C.wait(nc.vector, attnT_free)
                        pv_free = C.inc("v", nc.vector.tensor_copy(
                            attnT[hp:hp + 64, hg * 512:(hg + 1) * 512], ps_pv[:, :]))
                    at_done = C.tok("v")
                    C.wait(nc.tensor, at_done)
                    gcols = slice((b * SB + qb) * 512, (b * SB + qb + 1) * 512)
                    for ot in range(PG):
                        m = ot % 2
                        C.wait(nc.sync, xt_free[m])
                        ldx = C.inc(ch_xt[m], nc.sync.dma_start(
                            out=xt_b[m][:], in_=xT_sc[ot, :, gcols]), 16)
                        C.wait(nc.tensor, wo_free)
                        mm = None
                        for it in range(PG):
                            mm = nc.tensor.matmul(
                                ps_wo[:], wo[:, it * D + ot * 128:it * D + (ot + 1) * 128],
                                attnT[:, _ts(it, 512)], start=(it == 0), stop=(it == PG - 1))
                        rdy = C.inc("t", mm)
                        C.wait(nc.vector, rdy)
                        C.wait(nc.vector, out_free[m])
                        C.wait(nc.vector, ldx)
                        u = C.inc("v", nc.vector.tensor_add(out_b[m][:], ps_wo[:], xt_b[m][:]))
                        xt_free[m] = u
                        wo_free = u
                        C.wait(nc.gpsimd, u)
                        d = C.inc(ch_o[m], nc.gpsimd.dma_start(
                            out=xT_sc[ot, :, gcols], in_=out_b[m][:]), 16)
                        out_free[m] = d
                    attnT_free = C.tok("t")
                C.barrier()
        C.barrier()


def emit_scan(C, wd, xn_sc, xT_sc, n_tok, seq, *, bias_pp):
    nc = C.nc
    n_batch = n_tok // seq
    SB = seq // 512
    with ExitStack() as ph:
        sb = lambda nm, shape, dt: ph.enter_context(nc.sbuf_tensor(C.u(nm), shape, dt))
        psum = lambda nm, shape, dt: ph.enter_context(nc.psum_tensor(C.u(nm), shape, dt))
        wz, wtok1 = load_weight_cast(C, ph, wd["L1_attn_Wz"], D, D, "sc_wz")
        wh, wtok2 = load_weight_cast(C, ph, wd["L1_attn_Wh"], D, D, "sc_wh")
        negb = sb("sc_negb", [128, PG], F32)
        inb = [sb(f"sc_in{i}", [128, PG * 512], BF16) for i in range(seq // 512)]
        zb = sb("sc_zb", [128, 512], F32)
        ab = sb("sc_ab", [128, seq], F32)
        bb = sb("sc_bb", [128, seq], F32)
        hb = sb("sc_hb", [128, seq], F32)
        xt_b = [sb(f"sc_xt{i}", [128, 512], F32) for i in range(2)]
        out_b = [sb(f"sc_ou{i}", [128, 512], F32) for i in range(2)]
        accz = [psum(f"sc_az{i}", [128, 512], F32) for i in range(2)]
        acch = [psum(f"sc_ah{i}", [128, 512], F32) for i in range(2)]
        ch_in = [C.chan(ph, "sc_chi") for i in range(seq // 512)]
        ch_xt = [C.chan(ph, "sc_chx") for i in range(2)]
        ch_o = [C.chan(ph, "sc_cho", kind="sw") for i in range(2)]
        nb = C.inc("v", nc.vector.tensor_scalar(negb[:], bias_pp[:, 0:PG], -1.0, 0.0, ALU.mult, ALU.add))
        nc.vector.drain()
        C.wait(nc.scalar, nb)
        C.wait(nc.tensor, [wtok1, wtok2])

        inb_free = [None] * SB
        ld_toks = [None] * SB
        accz_free = [None, None]
        acch_free = [None, None]
        xt_free = [None, None]
        out_free = [None, None]
        buf_free = None
        for b in range(n_batch):
            for tb in range(SB):
                gtb = b * SB + tb
                C.wait(nc.sync, inb_free[tb])
                ld_toks[tb] = C.inc(ch_in[tb], nc.sync.dma_start(
                    out=inb[tb][:].rearrange("p (g c) -> p g c", g=PG),
                    in_=xn_sc[:, :, _ts(gtb, 512)].rearrange("g p c -> p g c")), 16)
            for ot in range(PG):
                vb_prev = None
                for tb in range(SB):
                    sl = tb
                    if ot == 0:
                        C.wait(nc.tensor, ld_toks[tb])
                    m = tb % 2
                    C.wait(nc.tensor, accz_free[m])
                    mm = None
                    for it in range(PG):
                        mm = nc.tensor.matmul(
                            accz[m][:], wz[:, it * D + ot * 128:it * D + (ot + 1) * 128],
                            inb[sl][:, _ts(it, 512)], start=(it == 0), stop=(it == PG - 1))
                    rz = C.inc("t", mm)
                    C.wait(nc.tensor, acch_free[m])
                    mm = None
                    for it in range(PG):
                        mm = nc.tensor.matmul(
                            acch[m][:], wh[:, it * D + ot * 128:it * D + (ot + 1) * 128],
                            inb[sl][:, _ts(it, 512)], start=(it == 0), stop=(it == PG - 1))
                    rh = C.inc("t", mm)
                    C.wait(nc.scalar, rz)
                    C.wait(nc.scalar, buf_free)
                    C.wait(nc.scalar, vb_prev)
                    sz = C.inc("s", nc.scalar.activation(zb[:], accz[m][:], AF.Sigmoid,
                                                         bias=bias_pp[:, ot:ot + 1]))
                    nc.scalar.drain()
                    sa = C.inc("s", nc.scalar.activation(ab[:, _ts(tb, 512)], accz[m][:], AF.Sigmoid,
                                                         scale=-1.0, bias=negb[:, ot:ot + 1]))
                    accz_free[m] = sa
                    C.wait(nc.vector, [sz, sa, rh])
                    vb = C.inc("v", nc.vector.scalar_tensor_tensor(
                        bb[:, _ts(tb, 512)], acch[m][:], bias_pp[:, PG + ot:PG + ot + 1],
                        zb[:], ALU.add, ALU.mult))
                    acch_free[m] = vb
                    vb_prev = vb
                C.wait(nc.vector, C.tok("s"))
                nc.vector.drain()
                sc = C.inc("v", nc.vector.tensor_tensor_scan(
                    hb[:], ab[:], bb[:], 0.0, ALU.mult, ALU.add))
                buf_free = sc
                for tb in range(SB):
                    m2 = tb % 2
                    gtb = b * SB + tb
                    C.wait(nc.sync, xt_free[m2])
                    ldx = C.inc(ch_xt[m2], nc.sync.dma_start(
                        out=xt_b[m2][:], in_=xT_sc[ot, :, _ts(gtb, 512)]), 16)
                    C.wait(nc.vector, [ldx, sc, out_free[m2]])
                    u = C.inc("v", nc.vector.tensor_add(out_b[m2][:], hb[:, _ts(tb, 512)], xt_b[m2][:]))
                    xt_free[m2] = u
                    C.wait(nc.gpsimd, u)
                    d = C.inc(ch_o[m2], nc.gpsimd.dma_start(
                        out=xT_sc[ot, :, _ts(gtb, 512)], in_=out_b[m2][:]), 16)
                    out_free[m2] = d
            t_all = C.tok("t")
            inb_free = [t_all] * SB
        C.barrier()


def emit_moe(C, wd, y_sc, xT_sc, h_sc, wrow_sc, n_tok, *, ident_b, ident_f, rmsw_pp):
    nc = C.nc
    NTB = n_tok // 512
    NT = n_tok // 128
    E_t = DE // 128
    with ExitStack() as ph:
        sb = lambda nm, shape, dt: ph.enter_context(nc.sbuf_tensor(C.u(nm), shape, dt))
        psum = lambda nm, shape, dt: ph.enter_context(nc.psum_tensor(C.u(nm), shape, dt))
        gwf = sb("mg_gwf", [8, D], F32)
        gwT = sb("mg_gwT", [128, PG * 8], BF16)
        logit = sb("mg_log", [8, n_tok], F32)
        ltm = sb("mg_ltm", [128, NT * 8], F32)
        wtm = sb("mg_wtm", [128, NT * 8], F32)
        wfm = sb("mg_wfm", [8, n_tok], F32)
        sms = sb("mg_sms", [128, 8], F32)
        tt8 = sb("mg_tt8", [128, 8], F32)
        inb = [sb(f"mg_in{i}", [128, PG * 512], BF16) for i in range(2)]
        ps8 = psum("mg_ps8", [8, 512], F32)
        pst = psum("mg_pst", [128, 1024], BF16)
        pstf = psum("mg_pstf", [128, 512], F32)
        ch_in = [C.chan(ph, "mg_chi") for i in range(2)]
        ch_g = C.chan(ph, "mg_chg")
        ldg = C.inc(ch_g, nc.sync.dma_start(out=gwf[:], in_=wd["L1_ffn_gate_w"][:, :]), 16)
        gwb = sb("mg_gwb", [8, D], BF16)
        C.wait(nc.vector, ldg)
        cb = C.inc("v", nc.vector.tensor_copy(gwb[:], gwf[:]))
        C.wait(nc.tensor, cb)
        tr = None
        for it in range(PG):
            tr = nc.tensor.transpose(pst[:, it * 8:(it + 1) * 8], gwb[:, _ts(it)], ident_b[0:8, 0:8])
        trt = C.inc("t", tr)
        C.wait(nc.vector, trt)
        gt = None
        for it in range(PG):
            gt = nc.vector.tensor_scalar(
                gwT[:, it * 8:(it + 1) * 8], pst[:, it * 8:(it + 1) * 8],
                rmsw_pp[:, it:it + 1], 0.0, ALU.mult, ALU.add)
        gt = C.inc("v", gt)
        C.wait(nc.tensor, gt)
        inb_free = [None, None]
        lg_ev = None
        for tb in range(NTB):
            sl = tb % 2
            C.wait(nc.sync, inb_free[sl])
            ldi = C.inc(ch_in[sl], nc.sync.dma_start(
                out=inb[sl][:].rearrange("p (g c) -> p g c", g=PG),
                in_=y_sc[:, :, _ts(tb, 512)].rearrange("g p c -> p g c")), 16)
            C.wait(nc.tensor, ldi)
            C.wait(nc.tensor, lg_ev)
            mm = None
            for it in range(PG):
                mm = nc.tensor.matmul(ps8[:], gwT[:, it * 8:(it + 1) * 8],
                                      inb[sl][:, _ts(it, 512)],
                                      start=(it == 0), stop=(it == PG - 1))
            r = C.inc("t", mm)
            C.wait(nc.vector, r)
            lg_ev = C.inc("v", nc.vector.tensor_copy(logit[:, _ts(tb, 512)], ps8[:]))
            inb_free[sl] = C.tok("t")
        C.wait(nc.tensor, lg_ev)
        wb_ev = None
        for tt in range(NT):
            tr = nc.tensor.transpose(pstf[:, 0:8], logit[:, _ts(tt)], ident_f[0:8, 0:8])
            r = C.inc("t", tr)
            C.wait(nc.vector, r)
            ev = C.inc("v", nc.vector.tensor_copy(ltm[:, tt * 8:(tt + 1) * 8], pstf[:, 0:8]))
            C.wait(nc.tensor, ev)
            lt = ltm[:, tt * 8:(tt + 1) * 8]
            wt = wtm[:, tt * 8:(tt + 1) * 8]
            nc.vector.drain()
            nc.vector.tensor_reduce(sms[:, 0:1], lt, AX.X, ALU.max)
            nc.vector.drain()
            nc.vector.tensor_tensor(tt8[:], lt, sms[:, 0:1].broadcast_to([128, 8]), ALU.is_equal)
            nc.vector.drain()
            nc.vector.scalar_tensor_tensor(wt, tt8[:], -1e30, lt, ALU.mult, ALU.add)
            nc.vector.drain()
            nc.vector.tensor_reduce(sms[:, 1:2], wt, AX.X, ALU.max)
            nc.vector.drain()
            nc.vector.tensor_tensor(wt, wt, sms[:, 1:2].broadcast_to([128, 8]), ALU.is_equal)
            nc.vector.drain()
            sg = C.inc("v", nc.vector.tensor_sub(sms[:, 2:3], sms[:, 0:1], sms[:, 1:2]))
            C.wait(nc.scalar, sg)
            ss = C.inc("s", nc.scalar.activation(sms[:, 3:4], sms[:, 2:3], AF.Sigmoid))
            C.wait(nc.vector, ss)
            nc.vector.tensor_scalar(tt8[:], tt8[:], sms[:, 3:4], 0.0, ALU.mult, ALU.add)
            nc.vector.drain()
            nc.vector.scalar_tensor_tensor(wt, wt, sms[:, 3:4], wt, ALU.mult, ALU.subtract)
            nc.vector.drain()
            wv = C.inc("v", nc.vector.tensor_sub(wt, tt8[:], wt))
            C.wait(nc.tensor, wv)
            tr2 = nc.tensor.transpose(pstf[0:8, 0:128], wt, ident_f)
            r2 = C.inc("t", tr2)
            C.wait(nc.vector, r2)
            wb_ev = C.inc("v", nc.vector.tensor_copy(wfm[:, _ts(tt)], pstf[0:8, 0:128]))
            C.wait(nc.tensor, wb_ev)
        C.wait(nc.gpsimd, wb_ev)
        dw = C.inc("qg", nc.gpsimd.dma_start(out=wrow_sc[:, :], in_=wfm[:]), 16)
        C.wait(nc.gpsimd, dw)
        C.barrier()

    with ExitStack() as phe:
        wbe = phe.enter_context(nc.sbuf_tensor(C.u("me_wbe"), [128, n_tok], F32))
        ch_wb = C.chan(phe, "me_chwb")
        for e in range(NE):
            C.wait(nc.sync, C.tok("v"))
            lw = C.inc(ch_wb, nc.sync.dma_start(
                out=wbe[:], in_=wrow_sc[e:e + 1, :].broadcast_to([128, n_tok])), 16)
            with ExitStack() as ph:
                silu_b = [ph.enter_context(nc.sbuf_tensor(C.u(f"mA_sl{i}"), [128, 512], F32)) for i in range(2)]
                out_b = [ph.enter_context(nc.sbuf_tensor(C.u(f"mA_ou{i}"), [128, 512], BF16)) for i in range(2)]
                ch_o = [C.chan(ph, "mA_cho", kind="sw") for i in range(2)]
                free_tok = [None, None]
                kctr = [0]

                def epA(C, ot, tb, pss, rdy, aux):
                    k = kctr[0] % 2
                    kctr[0] += 1
                    C.wait(nc.scalar, rdy)
                    C.wait(nc.scalar, free_tok[k])
                    a = C.inc("s", nc.scalar.activation(silu_b[k][:], pss[0][:], AF.Silu))
                    C.wait(nc.vector, a)
                    C.wait(nc.vector, rdy)
                    u = C.inc("v", nc.vector.scalar_tensor_tensor(
                        out_b[k][:], pss[1][:], 1.0, silu_b[k][:], ALU.mult, ALU.mult))
                    C.wait(nc.gpsimd, u)
                    d = C.inc(ch_o[k], nc.gpsimd.dma_start(
                        out=h_sc[ot, :, _ts(tb, 512)], in_=out_b[k][:]), 16)
                    free_tok[k] = [u, d]
                    return [a, u]

                emit_gemm_oi(C, [wd["L1_ffn_W1"][e], wd["L1_ffn_W2"][e]], DE, D,
                             y_sc, n_tok, og=11, epilogue=epA, ident_f=ident_f,
                             name=C.u("mA"), rhs_scale=rmsw_pp)
            with ExitStack() as ph:
                t_b = [ph.enter_context(nc.sbuf_tensor(C.u(f"mB_t{i}"), [128, 512], F32)) for i in range(2)]
                xt_b = [ph.enter_context(nc.sbuf_tensor(C.u(f"mB_xt{i}"), [128, 512], F32)) for i in range(3)]
                out_b = [ph.enter_context(nc.sbuf_tensor(C.u(f"mB_ou{i}"), [128, 512], F32)) for i in range(2)]
                ch_xt = [C.chan(ph, "mB_chx") for i in range(3)]
                ch_o = [C.chan(ph, "mB_cho", kind="sw") for i in range(2)]
                xt_free = [None] * 3
                out_free = [None, None]
                t_free = [None, None]
                prectr = [0]
                epctr = [0]

                def preB(C, ot, tb):
                    k = prectr[0] % 3
                    prectr[0] += 1
                    C.wait(nc.sync, xt_free[k])
                    ld = C.inc(ch_xt[k], nc.sync.dma_start(
                        out=xt_b[k][:], in_=xT_sc[ot, :, _ts(tb, 512)]), 16)
                    return (k, ld)

                def epB(C, ot, tb, pss, rdy, aux):
                    k, ld = aux
                    m = epctr[0] % 2
                    epctr[0] += 1
                    C.wait(nc.vector, [rdy, lw, t_free[m]])
                    t1 = C.inc("v", nc.vector.tensor_mul(t_b[m][:], pss[0][:], wbe[:, _ts(tb, 512)]))
                    nc.vector.drain()
                    C.wait(nc.vector, [ld, out_free[m]])
                    u = C.inc("v", nc.vector.tensor_add(out_b[m][:], t_b[m][:], xt_b[k][:]))
                    xt_free[k] = u
                    t_free[m] = u
                    C.wait(nc.gpsimd, u)
                    d = C.inc(ch_o[m], nc.gpsimd.dma_start(
                        out=xT_sc[ot, :, _ts(tb, 512)], in_=out_b[m][:]), 16)
                    out_free[m] = d
                    return [t1]

                emit_gemm_oi(C, [wd["L1_ffn_W3"][e]], D, DE, h_sc, n_tok, og=8,
                             epilogue=epB, pre=preB, ident_f=ident_f, name=C.u("mB"))


def emit_out(C, xT_sc, y_out, n_tok, *, ident_f):
    nc = C.nc
    NT = n_tok // 128
    with ExitStack() as ph:
        sb = lambda nm, shape, dt: ph.enter_context(nc.sbuf_tensor(C.u(nm), shape, dt))
        psum = lambda nm, shape, dt: ph.enter_context(nc.psum_tensor(C.u(nm), shape, dt))
        ldf = [sb(f"ou_ld{i}", [128, D], F32) for i in range(2)]
        otm = [sb(f"ou_tm{i}", [128, D], F32) for i in range(2)]
        ps_f = [psum(f"ou_ps{i}", [128, 512], F32) for i in range(2)]
        ch_l = [C.chan(ph, "ou_chl") for i in range(2)]
        ch_s = [C.chan(ph, "ou_chs", kind="sw") for i in range(2)]
        ldf_free = [None, None]
        otm_free = [None, None]
        psf_free = [None, None]
        for tt in range(NT):
            j = tt % 2
            C.wait(nc.sync, ldf_free[j])
            ld = C.inc(ch_l[j], nc.sync.dma_start(
                out=ldf[j][:].rearrange("p (g c) -> p g c", g=PG),
                in_=xT_sc[:, :, _ts(tt)].rearrange("g p c -> p g c")), 16)
            C.wait(nc.tensor, ld)
            C.wait(nc.tensor, psf_free[j])
            ev = None
            for g0 in range(0, PG, 4):
                tr = None
                for g in range(g0, g0 + 4):
                    tr = nc.tensor.transpose(ps_f[j][:, _ts(g - g0)], ldf[j][:, _ts(g)], ident_f)
                trt = C.inc("t", tr)
                C.wait(nc.vector, trt)
                C.wait(nc.vector, otm_free[j])
                for g in range(g0, g0 + 4):
                    ev = nc.vector.tensor_copy(otm[j][:, _ts(g)], ps_f[j][:, _ts(g - g0)])
                ev = C.inc("v", ev)
                if g0 + 4 < PG:
                    C.wait(nc.tensor, ev)
            psf_free[j] = ev
            ldf_free[j] = C.tok("t")
            C.wait(nc.gpsimd, ev)
            st = C.inc(ch_s[j], nc.gpsimd.dma_start(out=y_out[_ts(tt), :], in_=otm[j][:]), 16)
            otm_free[j] = st
        C.barrier()


N_TOK_FULL = 4096
SEQ_FULL = 2048
N_CORES = 8

WSHAPES = {
    "L0_attn_W_q": (D, D), "L0_attn_W_dkv": (D, LAT), "L0_attn_W_uk": (LAT, D),
    "L0_attn_W_uv": (LAT, D), "L0_attn_W_o": (D, D),
    "L0_ffn_w1_w": (DH, D), "L0_ffn_w2_w": (DH, D), "L0_ffn_h1_w": (DH, DH),
    "L0_ffn_h2_w": (DH, DH), "L0_ffn_w3_w": (D, DH),
    "L1_attn_Wz": (D, D), "L1_attn_Wh": (D, D),
    "L1_ffn_gate_w": (NE, D), "L1_ffn_W1": (NE, DE, D), "L1_ffn_W2": (NE, DE, D),
    "L1_ffn_W3": (NE, D, DE),
}


def host_consts(seq):
    p = np.arange(128)
    dl = p % 64
    inv = 1.0 / 10000.0 ** (np.arange(32, dtype=np.float64) / 32)
    tpos = np.arange(seq, dtype=np.float64)
    ang = tpos[None, :] * inv[dl % 32][:, None]
    cos_t = np.cos(ang).astype(BF16NP)
    sin_t = np.where((dl < 32)[:, None], -np.sin(ang), np.sin(ang)).astype(BF16NP)
    perm = np.zeros((128, 128), np.float32)
    for m in range(128):
        blk = (m // 64) * 64
        perm[blk + ((m - blk + 32) % 64), m] = 1.0
    mask = np.where(np.arange(128)[:, None] >= np.arange(128)[None, :],
                    0.0, -1e30).astype(np.float32)
    cvec = np.zeros((128, 16), np.float32)
    cvec[:, 0] = LN_EPS
    cvec[:, 1] = RMS_EPS * D
    return {"c_ident_b": np.eye(128, dtype=BF16NP),
            "c_ident_f": np.eye(128, dtype=np.float32),
            "c_perm": perm.astype(BF16NP),
            "c_cos": cos_t, "c_sin": sin_t, "c_mask": mask, "c_cvec": cvec}


def pack_params(inputs):
    r = {}
    scl = np.empty((128, 136), np.float32)
    scl[:, 0:32] = inputs["L0_ffn_w1_s"].reshape(32, 128).T
    scl[:, 32:64] = inputs["L0_ffn_w2_s"].reshape(32, 128).T
    scl[:, 64:96] = inputs["L0_ffn_h1_s"].reshape(32, 128).T
    scl[:, 96:128] = inputs["L0_ffn_h2_s"].reshape(32, 128).T
    scl[:, 128:136] = inputs["L0_ffn_w3_s"].reshape(8, 128).T
    r["p_scl"] = scl
    gb = np.empty((128, 4), np.float32)
    gb[:, 0:2] = inputs["L0_attn_ln_gamma"].reshape(2, 128).T
    gb[:, 2:4] = inputs["L0_attn_ln_beta"].reshape(2, 128).T
    r["p_gam_bet"] = gb
    bp = np.empty((128, 2 * PG), np.float32)
    bp[:, 0:PG] = inputs["L1_attn_bz"].reshape(PG, 128).T
    bp[:, PG:] = inputs["L1_attn_bh"].reshape(PG, 128).T
    r["p_bias"] = bp
    r["p_rmsw"] = np.ascontiguousarray(inputs["L1_ffn_rms_w"].reshape(PG, 128).T)
    return r


def build_full(n_tok=N_TOK_FULL, seq=SEQ_FULL):
    nc = bass.Bass("TRN2", target_bir_lowering=False)
    F = {"x": nc.dram_tensor("x", [n_tok, D], F32, kind="ExternalInput").ap()}
    wd = {}
    for nm, shp in WSHAPES.items():
        wd[nm] = nc.dram_tensor(nm, list(shp), F32, kind="ExternalInput").ap()
    cshapes = {"c_ident_b": ([128, 128], BF16), "c_ident_f": ([128, 128], F32),
               "c_perm": ([128, 128], BF16), "c_cos": ([128, seq], BF16),
               "c_sin": ([128, seq], BF16), "c_mask": ([128, 128], F32),
               "c_cvec": ([128, 16], F32), "p_scl": ([128, 136], F32),
               "p_gam_bet": ([128, 4], F32), "p_bias": ([128, 2 * PG], F32),
               "p_rmsw": ([128, PG], F32)}
    cd = {nm: nc.dram_tensor(nm, list(shp), dt, kind="ExternalInput").ap()
          for nm, (shp, dt) in cshapes.items()}
    y_out = nc.dram_tensor("y", [n_tok, D], F32, kind="ExternalOutput").ap()

    xT_sc = nc.dram_tensor("xT_sc", [PG, 128, n_tok], F32, kind="Internal").ap()
    xn0_sc = nc.dram_tensor("xn0_sc", [PG, 128, n_tok], BF16, kind="Internal").ap()
    xn1_sc = nc.dram_tensor("xn1_sc", [PG, 128, n_tok], BF16, kind="Internal").ap()
    y_sc = nc.dram_tensor("ysc", [PG, 128, n_tok], BF16, kind="Internal").ap()
    u_sc = nc.dram_tensor("u_sc", [DH // 128, 128, n_tok], BF16, kind="Internal").ap()
    g_sc = nc.dram_tensor("g_sc", [DH // 128, 128, n_tok], BF16, kind="Internal").ap()
    wrow_sc = nc.dram_tensor("wrow_sc", [NE, n_tok], F32, kind="Internal").ap()

    with ExitStack() as es:
        C = Ctx(nc, es)
        cs = {}
        ld = None
        for nm, (shp, dt) in cshapes.items():
            t = es.enter_context(nc.sbuf_tensor(nm + "_sb", shp, dt))
            ld = C.inc("qs", nc.sync.dma_start(out=t[:], in_=cd[nm][:, :]), 16)
            cs[nm] = t[:]
        C.wait(nc.sync, ld)
        C.barrier()
        emit_ln(C, F["x"], xT_sc, xn0_sc, n_tok, is_ln1=False, y_sc=None,
                ident_b=cs["c_ident_b"], ident_f=cs["c_ident_f"], cst=cs["c_cvec"])
        emit_attn(C, wd, xn0_sc, xT_sc, n_tok, seq,
                  ident_b=cs["c_ident_b"], ident_f=cs["c_ident_f"],
                  perm_b=cs["c_perm"], cos_t=cs["c_cos"], sin_t=cs["c_sin"],
                  mask128=cs["c_mask"], gam_bet=cs["p_gam_bet"], cst=cs["c_cvec"])
        scl = cs["p_scl"]
        emit_ffn0(C, wd, xn0_sc, u_sc, g_sc, xT_sc, n_tok,
                  scl[:, 0:32], scl[:, 32:64], scl[:, 64:96], scl[:, 96:128],
                  scl[:, 128:136], cs["c_ident_f"])
        emit_ln(C, xT_sc, xT_sc, xn1_sc, n_tok, is_ln1=True, y_sc=y_sc,
                ident_b=cs["c_ident_b"], ident_f=cs["c_ident_f"], cst=cs["c_cvec"])
        emit_scan(C, wd, xn1_sc, xT_sc, n_tok, seq, bias_pp=cs["p_bias"])
        emit_moe(C, wd, y_sc, xT_sc, u_sc[0:DE // 128], wrow_sc, n_tok,
                 ident_b=cs["c_ident_b"], ident_f=cs["c_ident_f"],
                 rmsw_pp=cs["p_rmsw"])
        C.barrier()
        emit_out(C, xT_sc, y_out, n_tok, ident_f=cs["c_ident_f"])
    return nc


def kernel(**inputs):
    x = np.ascontiguousarray(np.asarray(inputs["L0_attn_x"], dtype=np.float32))
    B, S, _ = x.shape
    per = B // N_CORES
    consts = host_consts(S)
    params = pack_params({k: np.asarray(v, dtype=np.float32) for k, v in inputs.items()
                          if k.endswith(("_s", "gamma", "beta", "bz", "bh", "rms_w"))})
    weights = {nm: np.ascontiguousarray(np.asarray(inputs[nm], dtype=np.float32))
               for nm in WSHAPES}
    nc = build_full(per * S, S)
    in_maps = []
    for c in range(N_CORES):
        m = {"x": np.ascontiguousarray(x[c * per:(c + 1) * per].reshape(per * S, D))}
        m.update(weights)
        m.update(consts)
        m.update(params)
        in_maps.append(m)
    res = run_bass_kernel_spmd(nc, in_maps, core_ids=list(range(N_CORES)))
    out = np.concatenate([res.results[c]["y"].reshape(per, S, D)
                          for c in range(N_CORES)], axis=0)
    return out


# version 03 (final)
# speedup vs baseline: 1.0385x; 1.0177x over previous
import numpy as np
import ml_dtypes
from contextlib import ExitStack

import concourse.bass as bass
import concourse.mybir as mybir
from concourse.bass_utils import run_bass_kernel_spmd

F32 = mybir.dt.float32
BF16 = mybir.dt.bfloat16
AF = mybir.ActivationFunctionType
ALU = mybir.AluOpType
AX = mybir.AxisListType
BF16NP = ml_dtypes.bfloat16

D = 1024
PG = D // 128
NH = 16
HD = 64
LAT = 256
DH = 4096
NE = 8
DE = 2816
LN_EPS = 1e-5
RMS_EPS = 1e-6


class Ctx:

    def __init__(self, nc, es):
        self.nc = nc
        self.es = es
        self._sems = {}
        self._by_num = {}
        for nm in ("qs", "qg", "qa", "t", "v", "s", "g"):
            h = es.enter_context(nc.semaphore("sem_" + nm))
            self._sems[nm] = [h, 0]

    def inc(self, nm, instr, k=1):
        ent = self._sems[nm]
        instr.then_inc(ent[0], k)
        ent[1] += k
        return (nm, ent[1])

    def n(self, nm):
        return self._sems[nm][1]

    def tok(self, nm):
        return (nm, self._sems[nm][1])

    _uctr = 0

    def u(self, nm):
        Ctx._uctr += 1
        return f"{nm}_{Ctx._uctr}"

    POOL_HW = 44
    POOL_SW = 24

    def chan(self, scope_es, name="ch", kind="hw"):
        if not hasattr(self, "_pool"):
            self._pool = {"hw": [], "sw": []}
            self._pool_i = {"hw": 0, "sw": 0}
            for i in range(self.POOL_HW):
                key = f"hch{i}"
                h = self.es.enter_context(self.nc.semaphore(key))
                self._sems[key] = [h, 0]
                self._pool["hw"].append(key)
            for i in range(self.POOL_SW):
                key = f"sch{i}"
                h = self.es.enter_context(self.nc.semaphore(key))
                self._sems[key] = [h, 0]
                self._pool["sw"].append(key)
        pool = self._pool[kind]
        key = pool[self._pool_i[kind] % len(pool)]
        self._pool_i[kind] += 1
        return key

    def wait(self, eng, tok):
        if tok is None:
            return
        if isinstance(tok, list):
            best = {}
            for t in tok:
                if t is None:
                    continue
                nm, val = t
                best[nm] = max(best.get(nm, 0), val)
            for nm, val in best.items():
                self.wait(eng, (nm, val))
            return
        nm, val = tok
        ent = self._sems[nm]
        assert val <= ent[1], f"wait({nm},{val}) but only {ent[1]} emitted"
        if val > 0:
            eng.wait_ge(ent[0], val)

    def barrier(self):
        nc = self.nc
        for key, ent in list(self._sems.items()):
            if key in ("t", "v", "s", "g"):
                continue
            if ent[1] > 0:
                nc.vector.wait_ge(ent[0], ent[1])
        nc.all_engine_barrier()


def _ts(i, n=128):
    return slice(i * n, (i + 1) * n)


def _ceil(a, b):
    return (a + b - 1) // b


def emit_ln(C, x_in_tm, xT_sc, xn_sc, n_tok, *, is_ln1, y_sc, ident_b, ident_f, cst):
    nc = C.nc
    NT = n_tok // 128
    with ExitStack() as ph:
        sb = lambda nm, shape, dt: ph.enter_context(nc.sbuf_tensor(C.u(nm), shape, dt))
        psum = lambda nm, shape, dt: ph.enter_context(nc.psum_tensor(C.u(nm), shape, dt))
        NB = 3
        xt = [sb(f"ln_x{i}", [128, D], F32) for i in range(NB)]
        fxn = [sb(f"ln_fxn{i}", [128, D], F32) for i in range(NB)]
        xsqd = sb("ln_xsqd", [128, D], F32)
        stats = [sb(f"ln_st{i}", [128, 12], F32) for i in range(NB)]
        xn_tm = [sb(f"ln_xn{i}", [128, D], BF16) for i in range(NB)]
        xs_tm = [sb(f"ln_xs{i}", [128, D], F32) for i in range(NB)]
        y_tm = [sb(f"ln_y{i}", [128, D], BF16) for i in range(NB)]
        stg_b = [sb(f"ln_sgb{i}", [128, D], BF16) for i in range(2)]
        stg_f = [sb(f"ln_sgf{i}", [128, D], F32) for i in range(2)]
        stg_y = [sb(f"ln_sgy{i}", [128, D], BF16) for i in range(2)]
        ldf = [sb(f"ln_ldf{i}", [128, D], F32) for i in range(2)]
        ps_b = [psum(f"ln_pb{i}", [128, 1024], BF16) for i in range(2)]
        ps_f = [psum(f"ln_pf{i}", [128, 512], F32) for i in range(2)]

        x_free = [None] * NB
        st_free = [None] * NB
        buf_free = [None] * NB
        stg_free = [None] * 2
        psb_free = [None] * 2
        psf_free = [None] * 2
        ldf_free = [None] * 2
        ch_x = [C.chan(ph, "ln_chx") for i in range(NB)]
        ch_ldf = [C.chan(ph, "ln_chl") for i in range(2)]
        ch_st = [C.chan(ph, "ln_chs", kind="sw") for i in range(2)]

        for tt in range(NT):
            i = tt % NB
            j = tt % 2
            ts = _ts(tt)
            if is_ln1:
                C.wait(nc.sync, ldf_free[j])
                ld = C.inc(ch_ldf[j], nc.sync.dma_start(
                    out=ldf[j][:].rearrange("p (g c) -> p g c", g=PG),
                    in_=x_in_tm[:, :, ts].rearrange("g p c -> p g c")), 16)
                C.wait(nc.tensor, ld)
                C.wait(nc.tensor, psf_free[j])
                C.wait(nc.scalar, x_free[i])
                ev = None
                for g0 in range(0, PG, 4):
                    tr = None
                    for g in range(g0, g0 + 4):
                        tr = nc.tensor.transpose(ps_f[j][:, _ts(g - g0)], ldf[j][:, _ts(g)], ident_f)
                    tt_tok = C.inc("t", tr)
                    C.wait(nc.scalar, tt_tok)
                    for g in range(g0, g0 + 4):
                        ev = nc.scalar.copy(xt[i][:, _ts(g)], ps_f[j][:, _ts(g - g0)])
                    ev = C.inc("s", ev)
                    if g0 + 4 < PG:
                        C.wait(nc.tensor, ev)
                psf_free[j] = ev
                ldf_free[j] = C.tok("t")
                nc.scalar.drain()
                ld = ev
            else:
                C.wait(nc.sync, x_free[i])
                ld = C.inc(ch_x[i], nc.sync.dma_start(out=xt[i][:], in_=x_in_tm[ts, :]), 16)

            st = stats[i]
            s1, ssq, mean, var, rstd = st[:, 0:1], st[:, 1:2], st[:, 2:3], st[:, 3:4], st[:, 4:5]
            msq, lnv = st[:, 5:6], st[:, 6:7]
            C.wait(nc.scalar, ld)
            C.wait(nc.scalar, st_free[i])
            sq = C.inc("s", nc.scalar.activation(xsqd[:], xt[i][:], AF.Square, accum_out=ssq))
            C.wait(nc.vector, ld)
            C.wait(nc.vector, sq)
            nc.vector.tensor_reduce(s1, xt[i][:], AX.X, ALU.add)
            nc.vector.drain()
            nc.vector.tensor_scalar(mean, s1, 1.0 / D, 0.0, ALU.mult, ALU.add)
            nc.vector.drain()
            nc.vector.tensor_tensor(msq, mean, mean, ALU.mult)
            nc.vector.drain()
            vv = C.inc("v", nc.vector.scalar_tensor_tensor(var, ssq, 1.0 / D, msq, ALU.mult, ALU.subtract))
            C.wait(nc.scalar, vv)
            nc.scalar.activation(lnv, var, AF.Ln, bias=cst[:, 0:1])
            nc.scalar.drain()
            rs = C.inc("s", nc.scalar.activation(rstd, lnv, AF.Exp, scale=-0.5))
            C.wait(nc.vector, rs)
            C.wait(nc.vector, buf_free[i])
            if is_ln1:
                nc.vector.tensor_scalar(xn_tm[i][:], xt[i][:], mean, rstd, ALU.subtract, ALU.mult)
                fx = C.inc("v", nc.vector.tensor_scalar(fxn[i][:], xt[i][:], mean, rstd, ALU.subtract, ALU.mult))
                C.wait(nc.scalar, fx)
                sq2 = C.inc("s", nc.scalar.activation(xsqd[:], fxn[i][:], AF.Square, accum_out=st[:, 7:8]))
                nc.scalar.drain()
                nc.scalar.activation(st[:, 8:9], st[:, 7:8], AF.Ln, bias=cst[:, 1:2])
                nc.scalar.drain()
                rs2 = C.inc("s", nc.scalar.activation(st[:, 9:10], st[:, 8:9], AF.Exp, scale=-0.5))
                C.wait(nc.vector, rs2)
                nc.vector.tensor_scalar(st[:, 10:11], st[:, 9:10], float(np.sqrt(D)), 0.0, ALU.mult, ALU.add)
                nc.vector.drain()
                nc.vector.tensor_scalar(y_tm[i][:], fxn[i][:], st[:, 10:11], 0.0, ALU.mult, ALU.add)
                apl = C.inc("v", nc.vector.tensor_add(xs_tm[i][:], xt[i][:], fxn[i][:]))
                sq2_tok = sq2
            else:
                apl = C.inc("v", nc.vector.tensor_scalar(xn_tm[i][:], xt[i][:], mean, rstd, ALU.subtract, ALU.mult))
                sq2_tok = None
            st_free[i] = apl

            src_f = xs_tm[i] if is_ln1 else xt[i]
            C.wait(nc.tensor, apl)
            C.wait(nc.tensor, psb_free[j])
            C.wait(nc.vector, stg_free[j])
            C.wait(nc.scalar, stg_free[j])
            tr = None
            for g in range(PG):
                tr = nc.tensor.transpose(ps_b[j][:, _ts(g)], xn_tm[i][:, _ts(g)], ident_b)
            tb = C.inc("t", tr)
            C.wait(nc.vector, tb)
            ev_b = None
            for g in range(PG):
                ev_b = nc.vector.tensor_copy(stg_b[j][:, _ts(g)], ps_b[j][:, _ts(g)])
            ev_b = C.inc("v", ev_b)
            psb_free[j] = ev_b
            C.wait(nc.tensor, psf_free[j])
            ev_f = None
            for g0 in range(0, PG, 4):
                tr = None
                for g in range(g0, g0 + 4):
                    tr = nc.tensor.transpose(ps_f[j][:, _ts(g - g0)], src_f[:, _ts(g)], ident_f)
                tf = C.inc("t", tr)
                C.wait(nc.scalar, tf)
                for g in range(g0, g0 + 4):
                    ev_f = nc.scalar.copy(stg_f[j][:, _ts(g)], ps_f[j][:, _ts(g - g0)])
                ev_f = C.inc("s", ev_f)
                if g0 + 4 < PG:
                    C.wait(nc.tensor, ev_f)
            psf_free[j] = ev_f
            ev_y = None
            if is_ln1:
                C.wait(nc.tensor, ev_b)
                tr = None
                for g in range(PG):
                    tr = nc.tensor.transpose(ps_b[j][:, _ts(g)], y_tm[i][:, _ts(g)], ident_b)
                ty = C.inc("t", tr)
                C.wait(nc.vector, ty)
                for g in range(PG):
                    ev_y = nc.vector.tensor_copy(stg_y[j][:, _ts(g)], ps_b[j][:, _ts(g)])
                ev_y = C.inc("v", ev_y)
                psb_free[j] = ev_y
            t_done = C.tok("t")
            buf_free[i] = [t_done, sq2_tok]
            x_free[i] = [apl, sq, t_done]

            C.wait(nc.gpsimd, ev_b if ev_y is None else ev_y)
            C.wait(nc.gpsimd, ev_f)
            C.inc(ch_st[j], nc.gpsimd.dma_start(
                out=xn_sc[:, :, ts].rearrange("g p c -> p g c"),
                in_=stg_b[j][:].rearrange("p (g c) -> p g c", g=PG)), 16)
            sg = C.inc(ch_st[j], nc.gpsimd.dma_start(
                out=xT_sc[:, :, ts].rearrange("g p c -> p g c"),
                in_=stg_f[j][:].rearrange("p (g c) -> p g c", g=PG)), 16)
            if is_ln1:
                sg = C.inc(ch_st[j], nc.gpsimd.dma_start(
                    out=y_sc[:, :, ts].rearrange("g p c -> p g c"),
                    in_=stg_y[j][:].rearrange("p (g c) -> p g c", g=PG)), 16)
            stg_free[j] = sg
        C.barrier()


def emit_gemm_oi(C, Ws, O, I, in_sc, n_tok, *, og, epilogue, pre=None,
                 ident_f, name, tb_bytes=512, rhs_scale=None):
    nc = C.nc
    O_t, I_t, NTB = O // 128, I // 128, n_tok // 512
    NW = len(Ws)
    with ExitStack() as ph:
        sb = lambda nm, shape, dt: ph.enter_context(nc.sbuf_tensor(C.u(name + nm), shape, dt))
        psum = lambda nm, shape, dt: ph.enter_context(nc.psum_tensor(C.u(name + nm), shape, dt))
        wstage = [sb(f"_wst{i}", [128, I], F32) for i in range(2)]
        wT = [sb(f"_wT{w}", [128, og * I_t * 128], BF16) for w in range(NW)]
        inb = [sb(f"_in{i}", [128, I_t * 512], BF16) for i in range(2)]
        wps = [psum(f"_wps{i}", [128, 512], F32) for i in range(2)]
        acc = [psum(f"_acc{i}", [128, 512], F32) for i in range(2 * NW)]

        wstage_free = [None] * 2
        wps_free = [None] * 2
        inb_free = [None] * 2
        acc_free = [None] * (2 * NW)
        group_mm_done = None
        ch_w = [C.chan(ph, name + "_chw") for i in range(2)]
        ch_in = [C.chan(ph, name + "_chi") for i in range(2)]

        n_groups = _ceil(O_t, og)
        for grp in range(n_groups):
            ots = list(range(grp * og, min((grp + 1) * og, O_t)))
            k = 0
            for w in range(NW):
                for osl, ot in enumerate(ots):
                    sl = k % 2
                    k += 1
                    C.wait(nc.sync, wstage_free[sl])
                    ld = C.inc(ch_w[sl], nc.sync.dma_start(
                        out=wstage[sl][:], in_=Ws[w][_ts(ot), :]), 16)
                    C.wait(nc.tensor, ld)
                    ev = None
                    for it0 in range(0, I_t, 4):
                        b = (it0 // 4) % 2
                        C.wait(nc.tensor, wps_free[b])
                        tr = None
                        nsub = min(4, I_t - it0)
                        for q in range(nsub):
                            tr = nc.tensor.transpose(
                                wps[b][:, _ts(q)], wstage[sl][:, _ts(it0 + q)], ident_f)
                        trt = C.inc("t", tr)
                        C.wait(nc.vector, trt)
                        if group_mm_done is not None:
                            C.wait(nc.vector, group_mm_done)
                        for q in range(nsub):
                            if rhs_scale is None:
                                ev = nc.vector.tensor_copy(
                                    wT[w][:, _ts(osl * I_t + it0 + q)], wps[b][:, _ts(q)])
                            else:
                                ev = nc.vector.tensor_scalar(
                                    wT[w][:, _ts(osl * I_t + it0 + q)], wps[b][:, _ts(q)],
                                    rhs_scale[:, it0 + q:it0 + q + 1], 0.0, ALU.mult, ALU.add)
                        ev = C.inc("v", ev)
                        wps_free[b] = ev
                    wstage_free[sl] = C.tok("t")
                    wT_ready = ev
            wT_tok = wT_ready

            for tb in range(NTB):
                sl = tb % 2
                C.wait(nc.sync, inb_free[sl])
                ldi = C.inc(ch_in[sl], nc.sync.dma_start(
                    out=inb[sl][:].rearrange("p (g c) -> p g c", g=I_t),
                    in_=in_sc[:, :, _ts(tb, 512)].rearrange("g p c -> p g c")), 16)
                C.wait(nc.tensor, ldi)
                if tb == 0:
                    C.wait(nc.tensor, wT_tok)
                last_mm = None
                for osl, ot in enumerate(ots):
                    aux = pre(C, ot, tb) if pre is not None else None
                    pss = []
                    for w in range(NW):
                        a = (((tb * len(ots) + osl) % 2) * NW) + w
                        C.wait(nc.tensor, acc_free[a])
                        for it in range(I_t):
                            last_mm = nc.tensor.matmul(
                                acc[a][:], wT[w][:, _ts(osl * I_t + it)],
                                inb[sl][:, _ts(it, 512)],
                                start=(it == 0), stop=(it == I_t - 1))
                        pss.append(acc[a])
                    rdy = C.inc("t", last_mm)
                    frees = epilogue(C, ot, tb, pss, rdy, aux)
                    for w in range(NW):
                        a = (((tb * len(ots) + osl) % 2) * NW) + w
                        acc_free[a] = frees
                inb_free[sl] = C.tok("t")
            group_mm_done = C.tok("t")
        C.barrier()


def emit_ffn0(C, wd, xn_sc, u_sc, g_sc, xT_sc, n_tok, sc_w1, sc_w2, sc_h1,
              sc_h2, sc_w3, ident_f, stages="ABCD"):
    nc = C.nc

    if "A" in stages:
      with ExitStack() as ph:
        silu_b = [ph.enter_context(nc.sbuf_tensor(C.u(f"fA_sl{i}"), [128, 512], F32)) for i in range(2)]
        out_b = [ph.enter_context(nc.sbuf_tensor(C.u(f"fA_ou{i}"), [128, 512], BF16)) for i in range(2)]
        ch_o = [C.chan(ph, "fA_cho", kind="sw") for i in range(2)]
        free_tok = [None, None]
        kctr = [0]

        def epA(C, ot, tb, pss, rdy, aux):
            k = kctr[0] % 2
            kctr[0] += 1
            C.wait(nc.scalar, rdy)
            C.wait(nc.scalar, free_tok[k])
            a = C.inc("s", nc.scalar.activation(silu_b[k][:], pss[0][:], AF.Silu,
                                                scale=sc_w1[:, ot:ot + 1]))
            C.wait(nc.vector, a)
            C.wait(nc.vector, rdy)
            u = C.inc("v", nc.vector.scalar_tensor_tensor(
                out_b[k][:], pss[1][:], sc_w2[:, ot:ot + 1], silu_b[k][:],
                ALU.mult, ALU.mult))
            C.wait(nc.gpsimd, u)
            d = C.inc(ch_o[k], nc.gpsimd.dma_start(
                out=u_sc[ot, :, _ts(tb, 512)], in_=out_b[k][:]), 16)
            free_tok[k] = [u, d]
            return [a, u]

        emit_gemm_oi(C, [wd["L0_ffn_w1_w"], wd["L0_ffn_w2_w"]], DH, D, xn_sc,
                     n_tok, og=8, epilogue=epA, ident_f=ident_f, name="fA")

    for tag, wkey, scl, src, dst in (("fB", "L0_ffn_h1_w", sc_h1, u_sc, g_sc),
                                     ("fC", "L0_ffn_h2_w", sc_h2, g_sc, u_sc)):
        if tag[1] not in stages:
            continue
        with ExitStack() as ph:
            out_b = [ph.enter_context(nc.sbuf_tensor(C.u(f"{tag}_ou{i}"), [128, 512], BF16)) for i in range(2)]
            ch_o = [C.chan(ph, tag + "_cho", kind="sw") for i in range(2)]
            free_tok = [None, None]
            kctr = [0]

            def epB(C, ot, tb, pss, rdy, aux, out_b=out_b, free_tok=free_tok,
                    kctr=kctr, scl=scl, dst=dst, ch_o=ch_o):
                k = kctr[0] % 2
                kctr[0] += 1
                C.wait(nc.scalar, rdy)
                C.wait(nc.scalar, free_tok[k])
                a = C.inc("s", nc.scalar.activation(out_b[k][:], pss[0][:], AF.Gelu,
                                                    scale=scl[:, ot:ot + 1]))
                C.wait(nc.gpsimd, a)
                d = C.inc(ch_o[k], nc.gpsimd.dma_start(
                    out=dst[ot, :, _ts(tb, 512)], in_=out_b[k][:]), 16)
                free_tok[k] = [d]
                return [a]

            emit_gemm_oi(C, [wd[wkey]], DH, DH, src, n_tok, og=8, epilogue=epB,
                         ident_f=ident_f, name=tag)

    if "D" in stages:
      with ExitStack() as ph:
        xt_b = [ph.enter_context(nc.sbuf_tensor(C.u(f"fD_xt{i}"), [128, 512], F32)) for i in range(3)]
        out_b = [ph.enter_context(nc.sbuf_tensor(C.u(f"fD_ou{i}"), [128, 512], F32)) for i in range(2)]
        ch_xt = [C.chan(ph, "fD_chx", kind="sw") for i in range(3)]
        ch_o = [C.chan(ph, "fD_cho", kind="sw") for i in range(2)]
        xt_free = [None] * 3
        out_free = [None, None]
        prectr = [0]
        epctr = [0]

        def preD(C, ot, tb):
            k = prectr[0] % 3
            prectr[0] += 1
            C.wait(nc.gpsimd, xt_free[k])
            ld = C.inc(ch_xt[k], nc.gpsimd.dma_start(
                out=xt_b[k][:], in_=xT_sc[ot, :, _ts(tb, 512)]), 16)
            return (k, ld)

        def epD(C, ot, tb, pss, rdy, aux):
            k, ld = aux
            m = epctr[0] % 2
            epctr[0] += 1
            C.wait(nc.vector, rdy)
            C.wait(nc.vector, ld)
            C.wait(nc.vector, out_free[m])
            u = C.inc("v", nc.vector.scalar_tensor_tensor(
                out_b[m][:], pss[0][:], sc_w3[:, ot:ot + 1], xt_b[k][:],
                ALU.mult, ALU.add))
            xt_free[k] = u
            C.wait(nc.gpsimd, u)
            d = C.inc(ch_o[m], nc.gpsimd.dma_start(
                out=xT_sc[ot, :, _ts(tb, 512)], in_=out_b[m][:]), 16)
            out_free[m] = [d]
            return [u]

        emit_gemm_oi(C, [wd["L0_ffn_w3_w"]], D, DH, u_sc, n_tok, og=8,
                     epilogue=epD, pre=preD, ident_f=ident_f, name="fD")


def emit_weight_load(C, stage_es, dst_t, W_dram, I, O, name):
    nc = C.nc
    I_t = I // 128
    dst = dst_t
    stage = [stage_es.enter_context(nc.sbuf_tensor(C.u(name + f"_st{i}"), [128, O], F32)) for i in range(2)]
    ch_st = [C.chan(stage_es, name + "_ch") for i in range(2)]
    st_free = [None, None]
    ev = None
    for it in range(I_t):
        sl = it % 2
        C.wait(nc.sync, st_free[sl])
        ld = C.inc(ch_st[sl], nc.sync.dma_start(out=stage[sl][:], in_=W_dram[_ts(it), :]), 16)
        C.wait(nc.vector, ld)
        ev = C.inc("v", nc.vector.tensor_copy(dst[:, it * O:(it + 1) * O], stage[sl][:]))
        st_free[sl] = ev
    return ev


def load_weight_cast(C, ph, W_dram, I, O, name, stage_es=None):
    nc = C.nc
    if stage_es is None:
        stage_es = ph
    I_t = I // 128
    dst = ph.enter_context(nc.sbuf_tensor(C.u(name), [128, I_t * O], BF16))
    stage = [stage_es.enter_context(nc.sbuf_tensor(C.u(name + f"_st{i}"), [128, O], F32)) for i in range(2)]
    ch_st = [C.chan(ph, name + "_ch") for i in range(2)]
    st_free = [None, None]
    ev = None
    for it in range(I_t):
        sl = it % 2
        C.wait(nc.sync, st_free[sl])
        ld = C.inc(ch_st[sl], nc.sync.dma_start(out=stage[sl][:], in_=W_dram[_ts(it), :]), 16)
        C.wait(nc.vector, ld)
        ev = C.inc("v", nc.vector.tensor_copy(dst[:, it * O:(it + 1) * O], stage[sl][:]))
        st_free[sl] = ev
    return dst, ev


def emit_attn(C, wd, xn_sc, xT_sc, n_tok, seq, *, ident_b, ident_f, perm_b,
              cos_t, sin_t, mask128, gam_bet, cst, dbg=None, upto=3):
    nc = C.nc
    n_batch = n_tok // seq
    SB = seq // 512
    ST = seq // 128
    LT = LAT // 128
    with ExitStack() as ph:
        sb = lambda nm, shape, dt: ph.enter_context(nc.sbuf_tensor(C.u(nm), shape, dt))
        psum = lambda nm, shape, dt: ph.enter_context(nc.psum_tensor(C.u(nm), shape, dt))
        q_sb = sb("at_q", [128, PG * seq], BF16)
        k_sb = sb("at_k", [128, PG * seq], BF16)
        v_tm = sb("at_v", [128, ST * D], BF16)
        inb = [sb(f"at_in{i}", [128, PG * 512], BF16) for i in range(2)]
        ch_in = [C.chan(ph, "at_chi") for i in range(2)]
        ch_xt = [C.chan(ph, "at_chx") for i in range(2)]
        ch_o = [C.chan(ph, "at_cho", kind="sw") for i in range(2)]
        wdkv = ph.enter_context(nc.sbuf_tensor(C.u("at_wdkv"), [128, PG * LAT], BF16))
        wuk = ph.enter_context(nc.sbuf_tensor(C.u("at_wuk"), [128, (LAT // 128) * D], BF16))
        wuv = ph.enter_context(nc.sbuf_tensor(C.u("at_wuv"), [128, (LAT // 128) * D], BF16))
        wq = ph.enter_context(nc.sbuf_tensor(C.u("at_wq"), [128, PG * D], BF16))
        wo = ph.enter_context(nc.sbuf_tensor(C.u("at_wo"), [128, PG * D], BF16))
        with ExitStack() as wsc:
            wt1 = emit_weight_load(C, wsc, wdkv, wd["L0_attn_W_dkv"], D, LAT, "at_wdkv")
            wt2 = emit_weight_load(C, wsc, wuk, wd["L0_attn_W_uk"], LAT, D, "at_wuk")
            wt3 = emit_weight_load(C, wsc, wuv, wd["L0_attn_W_uv"], LAT, D, "at_wuv")
            wt4 = emit_weight_load(C, wsc, wq, wd["L0_attn_W_q"], D, D, "at_wq")
            wt5 = emit_weight_load(C, wsc, wo, wd["L0_attn_W_o"], D, D, "at_wo")
            C.wait(nc.vector, [wt1, wt2, wt3, wt4, wt5])
            C.barrier()

        for b in range(n_batch):
            cols = lambda tb: slice((b * SB + tb) * 512, (b * SB + tb + 1) * 512)

            with ExitStack() as s1:
                sb1 = lambda nm, shape, dt: s1.enter_context(nc.sbuf_tensor(C.u(nm), shape, dt))
                ps1 = lambda nm, shape, dt: s1.enter_context(nc.psum_tensor(C.u(nm), shape, dt))
                wtok, wtok2, wtok3 = wt1, wt2, wt3
                latn = sb1("at_latn", [128, LT * seq], BF16)
                lat_c = sb1("at_latc", [128, LT * 512], F32)
                sqd = sb1("at_sqd", [128, 512], F32)
                srow = sb1("at_srow", [1, 512], F32)
                srow2 = sb1("at_srow2", [1, 512], F32)
                mbc = sb1("at_mbc", [128, 512], F32)
                rsc = sb1("at_rsc", [128, 512], F32)
                ones128 = sb1("at_o128", [128, 1], F32)
                ones1 = sb1("at_o1", [1, 128], F32)
                acc = [ps1(f"at_ac{i}", [128, 512], F32) for i in range(4)]
                ps_row = ps1("at_pr", [1, 512], F32)
                ps_bc = ps1("at_bc", [128, 512], F32)
                ms = C.inc("v", nc.vector.memset(ones128[:], 1.0))
                ms = C.inc("v", nc.vector.memset(ones1[:], 1.0))
                nc.vector.drain()
                C.wait(nc.tensor, ms)
                C.wait(nc.tensor, [wtok, wtok2, wtok3])
                inb_free = [C.tok("t"), C.tok("t")]
                acc_free = [None] * 4
                lat_free = None
                bc_free = None
                latn_done = None
                for tb in range(SB):
                    sl = tb % 2
                    C.wait(nc.sync, inb_free[sl])
                    ldi = C.inc(ch_in[sl], nc.sync.dma_start(
                        out=inb[sl][:].rearrange("p (g c) -> p g c", g=PG),
                        in_=xn_sc[:, :, cols(tb)].rearrange("g p c -> p g c")), 16)
                    C.wait(nc.tensor, ldi)
                    lat_ev = None
                    for ot in range(LT):
                        a = ot % 2
                        C.wait(nc.tensor, acc_free[a])
                        mm = None
                        for it in range(PG):
                            mm = nc.tensor.matmul(
                                acc[a][:], wdkv[:, it * LAT + ot * 128:it * LAT + (ot + 1) * 128],
                                inb[sl][:, _ts(it, 512)], start=(it == 0), stop=(it == PG - 1))
                        rdy = C.inc("t", mm)
                        C.wait(nc.vector, rdy)
                        C.wait(nc.vector, lat_free)
                        lat_ev = C.inc("v", nc.vector.tensor_copy(lat_c[:, _ts(ot, 512)], acc[a][:]))
                        acc_free[a] = lat_ev
                    inb_free[sl] = C.tok("t")
                    C.wait(nc.tensor, lat_ev)
                    mm = None
                    for g in range(LT):
                        mm = nc.tensor.matmul(ps_row[:], ones128[:], lat_c[:, _ts(g, 512)],
                                              start=(g == 0), stop=(g == LT - 1))
                    r1 = C.inc("t", mm)
                    C.wait(nc.vector, r1)
                    e1 = C.inc("v", nc.vector.tensor_copy(srow[0:1, :], ps_row[0:1, :]))
                    C.wait(nc.tensor, e1)
                    sq_rd = None
                    r2 = None
                    for g in range(LT):
                        C.wait(nc.vector, sq_rd)
                        lslc = lat_c[:, _ts(g, 512)]
                        sq = C.inc("v", nc.vector.tensor_mul(sqd[:], lslc, lslc))
                        C.wait(nc.tensor, sq)
                        r2 = C.inc("t", nc.tensor.matmul(ps_row[:], ones128[:], sqd[:],
                                                         start=(g == 0), stop=(g == LT - 1)))
                        sq_rd = r2
                    C.wait(nc.vector, r2)
                    e2 = C.inc("v", nc.vector.tensor_copy(srow2[0:1, :], ps_row[0:1, :]))
                    C.wait(nc.tensor, e2)
                    nc.vector.drain()
                    nc.vector.tensor_scalar(srow[0:1, :], srow[0:1, :], 1.0 / LAT, 0.0, ALU.mult, ALU.add)
                    nc.vector.drain()
                    nc.vector.tensor_scalar(srow2[0:1, :], srow2[0:1, :], 1.0 / LAT, 0.0, ALU.mult, ALU.add)
                    nc.vector.drain()
                    nc.vector.tensor_mul(sqd[0:1, :], srow[0:1, :], srow[0:1, :])
                    nc.vector.drain()
                    vv = C.inc("v", nc.vector.tensor_sub(srow2[0:1, :], srow2[0:1, :], sqd[0:1, :]))
                    C.wait(nc.scalar, vv)
                    nc.scalar.activation(srow2[0:1, :], srow2[0:1, :], AF.Ln, bias=cst[0:1, 0:1])
                    nc.scalar.drain()
                    rr = C.inc("s", nc.scalar.activation(srow2[0:1, :], srow2[0:1, :], AF.Exp, scale=-0.5))
                    C.wait(nc.tensor, rr)
                    C.wait(nc.tensor, bc_free)
                    mm = nc.tensor.matmul(ps_bc[:], ones1[:], srow[0:1, :], start=True, stop=True)
                    r = C.inc("t", mm)
                    C.wait(nc.vector, r)
                    b1 = C.inc("v", nc.vector.tensor_copy(mbc[:], ps_bc[:]))
                    C.wait(nc.tensor, b1)
                    mm = nc.tensor.matmul(ps_bc[:], ones1[:], srow2[0:1, :], start=True, stop=True)
                    r = C.inc("t", mm)
                    C.wait(nc.vector, r)
                    b2 = C.inc("v", nc.vector.tensor_copy(rsc[:], ps_bc[:]))
                    bc_free = b2
                    for g in range(LT):
                        nc.vector.tensor_sub(lat_c[:, _ts(g, 512)], lat_c[:, _ts(g, 512)], mbc[:])
                        nc.vector.drain()
                        nc.vector.tensor_mul(lat_c[:, _ts(g, 512)], lat_c[:, _ts(g, 512)], rsc[:])
                        nc.vector.drain()
                        latn_done = C.inc("v", nc.vector.tensor_scalar(
                            latn[:, g * seq + tb * 512: g * seq + (tb + 1) * 512],
                            lat_c[:, _ts(g, 512)],
                            gam_bet[:, g:g + 1], gam_bet[:, LT + g:LT + g + 1], ALU.mult, ALU.add))
                    lat_free = latn_done
                bdone = latn_done
                C.wait(nc.tensor, bdone)
                vtmp = sb1("at_vtmp", [128, 512], BF16)
                ps_tr = ps1("at_ptr", [128, 1024], BF16)
                vfm_dbg = sb1("at_vfm", [128, PG * seq], BF16) if (dbg is not None and "vfm" in dbg) else None
                for tb in range(SB):
                    for ot in range(PG):
                        a = (tb * PG + ot) % 4
                        C.wait(nc.tensor, acc_free[a])
                        mm = None
                        for it in range(LT):
                            mm = nc.tensor.matmul(
                                acc[a][:], wuk[:, it * D + ot * 128:it * D + (ot + 1) * 128],
                                latn[:, it * seq + tb * 512: it * seq + (tb + 1) * 512],
                                start=(it == 0), stop=(it == LT - 1))
                        rdy = C.inc("t", mm)
                        C.wait(nc.vector, rdy)
                        evk = C.inc("v", nc.vector.tensor_copy(
                            k_sb[:, ot * seq + tb * 512: ot * seq + (tb + 1) * 512], acc[a][:]))
                        C.wait(nc.tensor, evk)
                        mm = None
                        for it in range(LT):
                            mm = nc.tensor.matmul(
                                acc[a][:], wuv[:, it * D + ot * 128:it * D + (ot + 1) * 128],
                                latn[:, it * seq + tb * 512: it * seq + (tb + 1) * 512],
                                start=(it == 0), stop=(it == LT - 1))
                        rdy = C.inc("t", mm)
                        C.wait(nc.vector, rdy)
                        if dbg is not None and "vfm" in dbg:
                            nc.vector.tensor_copy(
                                vfm_dbg[:, ot * seq + tb * 512: ot * seq + (tb + 1) * 512], acc[a][:])
                        evv = C.inc("v", nc.vector.tensor_copy(vtmp[:], acc[a][:]))
                        C.wait(nc.tensor, evv)
                        trr = None
                        for c in range(4):
                            trr = nc.tensor.transpose(ps_tr[:, _ts(c)], vtmp[:, _ts(c)], ident_b)
                        trt = C.inc("t", trr)
                        C.wait(nc.vector, trt)
                        evt = None
                        for c in range(4):
                            kt = tb * 4 + c
                            evt = nc.vector.tensor_copy(
                                v_tm[:, kt * D + ot * 128: kt * D + (ot + 1) * 128], ps_tr[:, _ts(c)])
                        evt = C.inc("v", evt)
                        acc_free[a] = evt
                        C.wait(nc.tensor, evt)
                kv_barrier_marker = True
                C.barrier()
                if dbg is not None and "vfm" in dbg:
                    ddv = C.tok("v")
                    C.wait(nc.gpsimd, ddv)
                    dv2 = C.inc("qg", nc.gpsimd.dma_start(out=dbg["vfm"][b, :, :], in_=vfm_dbg[:]), 16)
                    C.wait(nc.gpsimd, dv2)
                    C.barrier()
            with ExitStack() as s2:
                sb2 = lambda nm, shape, dt: s2.enter_context(nc.sbuf_tensor(C.u(nm), shape, dt))
                ps2 = lambda nm, shape, dt: s2.enter_context(nc.psum_tensor(C.u(nm), shape, dt))
                wtok = wt4
                acc = [ps2(f"at_qa{i}", [128, 512], F32) for i in range(2)]
                ps_sw = ps2("at_sw", [128, 512], F32)
                t1 = sb2("at_t1", [128, 512], F32)
                t2 = sb2("at_t2", [128, 512], F32)
                acc_free = [None] * 2
                inb_free = [C.tok("t"), C.tok("t")]
                C.wait(nc.tensor, wtok)
                for tb in range(SB):
                    sl = tb % 2
                    C.wait(nc.sync, inb_free[sl])
                    ldi = C.inc(ch_in[sl], nc.sync.dma_start(
                        out=inb[sl][:].rearrange("p (g c) -> p g c", g=PG),
                        in_=xn_sc[:, :, cols(tb)].rearrange("g p c -> p g c")), 16)
                    C.wait(nc.tensor, ldi)
                    for ot in range(PG):
                        a = (tb * PG + ot) % 2
                        C.wait(nc.tensor, acc_free[a])
                        mm = None
                        for it in range(PG):
                            mm = nc.tensor.matmul(
                                acc[a][:], wq[:, it * D + ot * 128:it * D + (ot + 1) * 128],
                                inb[sl][:, _ts(it, 512)], start=(it == 0), stop=(it == PG - 1))
                        rdy = C.inc("t", mm)
                        C.wait(nc.vector, rdy)
                        ev = C.inc("v", nc.vector.tensor_copy(
                            q_sb[:, ot * seq + tb * 512: ot * seq + (tb + 1) * 512], acc[a][:]))
                        acc_free[a] = ev
                    inb_free[sl] = C.tok("t")
                if dbg is not None and "q0" in dbg:
                    dd = C.tok("v")
                    C.wait(nc.gpsimd, dd)
                    dq = C.inc("qg", nc.gpsimd.dma_start(out=dbg["q0"][b, :, :], in_=q_sb[:]), 16)
                    C.wait(nc.gpsimd, dq)
                    C.barrier()
                rope_done = C.tok("v")
                for which, buf in (("q", q_sb), ("k", k_sb)):
                    for g in range(PG):
                        for tc in range(SB):
                            slc = slice(g * seq + tc * 512, g * seq + (tc + 1) * 512)
                            C.wait(nc.tensor, rope_done)
                            mm = nc.tensor.matmul(ps_sw[:], perm_b, buf[:, slc], start=True, stop=True)
                            r = C.inc("t", mm)
                            C.wait(nc.vector, r)
                            nc.vector.tensor_mul(t1[:], buf[:, slc], cos_t[:, _ts(tc, 512)])
                            nc.vector.tensor_mul(t2[:], ps_sw[:], sin_t[:, _ts(tc, 512)])
                            nc.vector.drain()
                            rope_done = C.inc("v", nc.vector.tensor_add(buf[:, slc], t1[:], t2[:]))
                C.barrier()
            if dbg is not None:
                dd = C.tok("v")
                C.wait(nc.gpsimd, dd)
                C.wait(nc.gpsimd, C.tok("s"))
                dq = C.inc("qg", nc.gpsimd.dma_start(out=dbg["q"][b, :, :], in_=q_sb[:]), 16)
                dq = C.inc("qg", nc.gpsimd.dma_start(out=dbg["k"][b, :, :], in_=k_sb[:]), 16)
                dq = C.inc("qg", nc.gpsimd.dma_start(out=dbg["v"][b, :, :], in_=v_tm[:]), 16)
                C.wait(nc.gpsimd, dq)
                C.barrier()
            if upto < 3:
                continue
            with ExitStack() as s3:
                sb3 = lambda nm, shape, dt: s3.enter_context(nc.sbuf_tensor(C.u(nm), shape, dt))
                ps3 = lambda nm, shape, dt: s3.enter_context(nc.psum_tensor(C.u(nm), shape, dt))
                wtok = wt5
                ps_S = ps3("at_S", [128, min(seq, 2048)], F32)
                ps_tr2 = ps3("at_tr2", [128, 2048], BF16)
                ps_pv = ps3("at_pv", [64, 512], F32)
                ps_wo = ps3("at_wo_ps", [128, 512], F32)
                prow = [sb3(f"at_prow{i}", [128, min(seq, 2048)], BF16) for i in range(2)]
                PT = sb3("at_PT", [128, ST * 512], BF16)
                attnT = sb3("at_attnT", [128, PG * 512], BF16)
                sms = sb3("at_sms", [128, 8], F32)
                xt_b = [sb3(f"at_xt{i}", [128, 512], F32) for i in range(2)]
                out_b = [sb3(f"at_ou{i}", [128, 512], F32) for i in range(2)]
                rope_all = C.tok("v")
                C.wait(nc.tensor, rope_all)
                C.wait(nc.tensor, wtok)
                bank_free = [None] * 4
                prow_free = [None, None]
                tr_free = None
                pv_free = None
                wo_free = None
                xt_free = [None, None]
                out_free = [None, None]
                attnT_free = None
                for qb in range(SB):
                    for h in range(NH):
                        hp = (h % 2) * 64
                        hg = h // 2
                        C.wait(nc.gpsimd, pv_free)
                        gz = C.inc("g", nc.gpsimd.memset(
                            PT[:, (qb * 4) * 512:(qb * 4 + 4) * 512], 0.0))
                        pend_tr = None
                        for qt4 in range(4):
                            qt = qb * 4 + qt4
                            valid = (qt + 1) * 128
                            nkb = _ceil(valid, 512)
                            last_s = None
                            for kb in range(nkb):
                                C.wait(nc.tensor, bank_free[kb])
                                last_s = nc.tensor.matmul(
                                    ps_S[:, _ts(kb, 512)],
                                    q_sb[hp:hp + 64, hg * seq + qt * 128: hg * seq + (qt + 1) * 128],
                                    k_sb[hp:hp + 64, hg * seq + kb * 512: hg * seq + (kb + 1) * 512],
                                    start=True, stop=True)
                            smm = C.inc("t", last_s)
                            if pend_tr is not None:
                                pqt4, ppn = pend_tr
                                pqt = qb * 4 + pqt4
                                C.wait(nc.tensor, ppn)
                                C.wait(nc.tensor, tr_free)
                                tr = None
                                for kt in range(pqt + 1):
                                    tr = nc.tensor.transpose(ps_tr2[:, _ts(kt)],
                                                             prow[pqt4 % 2][:, _ts(kt)], ident_b)
                                trt = C.inc("t", tr)
                                prow_free[pqt4 % 2] = trt
                                C.wait(nc.vector, trt)
                                C.wait(nc.vector, gz)
                                tr_free = C.inc("v", nc.vector.tensor_copy(
                                    PT[:].rearrange("p (k c) -> p k c", c=512)[:, 0:pqt + 1, pqt4 * 128: pqt4 * 128 + 128],
                                    ps_tr2[:].rearrange("p (k c) -> p k c", c=128)[:, 0:pqt + 1, :]))
                            C.wait(nc.vector, smm)
                            dg = slice(qt * 128, qt * 128 + 128)
                            nc.vector.tensor_add(ps_S[:, dg], ps_S[:, dg], mask128)
                            nc.vector.drain()
                            _sv = ps_S[:, 0:valid]
                            nc.vector.tensor_reduce(
                                sms[:, 0:1],
                                bass.AP(_sv.tensor, _sv.offset,
                                        [[_sv.ap[0][0], 128], [4, valid // 4]]),
                                AX.X, ALU.max, negate=True)
                            nc.vector.drain()
                            nm = C.inc("v", nc.vector.tensor_scalar(sms[:, 1:2], sms[:, 0:1], 0.125, 0.0, ALU.mult, ALU.add))
                            C.wait(nc.scalar, nm)
                            C.wait(nc.scalar, prow_free[qt4 % 2])
                            for kb in range(nkb):
                                hi = min(valid, (kb + 1) * 512)
                                ex = C.inc("s", nc.scalar.activation(
                                    prow[qt4 % 2][:, kb * 512:hi], ps_S[:, kb * 512:hi], AF.Exp,
                                    bias=sms[:, 1:2], scale=0.125, accum_out=sms[:, 4 + kb:5 + kb]))
                                bank_free[kb] = ex
                            nc.scalar.drain()
                            C.wait(nc.vector, C.tok("s"))
                            nc.vector.tensor_reduce(sms[:, 2:3], sms[:, 4:4 + nkb], AX.X, ALU.add)
                            nc.vector.drain()
                            nc.vector.reciprocal(sms[:, 3:4], sms[:, 2:3])
                            nc.vector.drain()
                            pn = C.inc("v", nc.vector.tensor_scalar(
                                prow[qt4 % 2][:, 0:valid], prow[qt4 % 2][:, 0:valid], sms[:, 3:4], 0.0, ALU.mult, ALU.add))
                            prow_free[qt4 % 2] = None
                            pend_tr = (qt4, pn)
                        pqt4, ppn = pend_tr
                        pqt = qb * 4 + pqt4
                        C.wait(nc.tensor, ppn)
                        C.wait(nc.tensor, tr_free)
                        tr = None
                        for kt in range(pqt + 1):
                            tr = nc.tensor.transpose(ps_tr2[:, _ts(kt)],
                                                     prow[pqt4 % 2][:, _ts(kt)], ident_b)
                        trt = C.inc("t", tr)
                        C.wait(nc.vector, trt)
                        C.wait(nc.vector, gz)
                        tr_free = C.inc("v", nc.vector.tensor_copy(
                            PT[:].rearrange("p (k c) -> p k c", c=512)[:, 0:pqt + 1, pqt4 * 128: pqt4 * 128 + 128],
                            ps_tr2[:].rearrange("p (k c) -> p k c", c=128)[:, 0:pqt + 1, :]))
                        prow_free[0] = C.tok("t")
                        prow_free[1] = C.tok("t")
                        pt_done = tr_free
                        C.wait(nc.tensor, pt_done)
                        C.wait(nc.tensor, pv_free)
                        mm = None
                        nkt = qb * 4 + 4
                        for kt in range(nkt):
                            mm = nc.tensor.matmul(
                                ps_pv[:, :],
                                v_tm[:, kt * D + h * 64: kt * D + (h + 1) * 64],
                                PT[:, kt * 512:(kt + 1) * 512],
                                start=(kt == 0), stop=(kt == nkt - 1))
                        pvt = C.inc("t", mm)
                        C.wait(nc.vector, pvt)
                        C.wait(nc.vector, attnT_free)
                        pv_free = C.inc("v", nc.vector.tensor_copy(
                            attnT[hp:hp + 64, hg * 512:(hg + 1) * 512], ps_pv[:, :]))
                    at_done = C.tok("v")
                    C.wait(nc.tensor, at_done)
                    gcols = slice((b * SB + qb) * 512, (b * SB + qb + 1) * 512)
                    for ot in range(PG):
                        m = ot % 2
                        C.wait(nc.sync, xt_free[m])
                        ldx = C.inc(ch_xt[m], nc.sync.dma_start(
                            out=xt_b[m][:], in_=xT_sc[ot, :, gcols]), 16)
                        C.wait(nc.tensor, wo_free)
                        mm = None
                        for it in range(PG):
                            mm = nc.tensor.matmul(
                                ps_wo[:], wo[:, it * D + ot * 128:it * D + (ot + 1) * 128],
                                attnT[:, _ts(it, 512)], start=(it == 0), stop=(it == PG - 1))
                        rdy = C.inc("t", mm)
                        C.wait(nc.vector, rdy)
                        C.wait(nc.vector, out_free[m])
                        C.wait(nc.vector, ldx)
                        u = C.inc("v", nc.vector.tensor_add(out_b[m][:], ps_wo[:], xt_b[m][:]))
                        xt_free[m] = u
                        wo_free = u
                        C.wait(nc.gpsimd, u)
                        d = C.inc(ch_o[m], nc.gpsimd.dma_start(
                            out=xT_sc[ot, :, gcols], in_=out_b[m][:]), 16)
                        out_free[m] = d
                    attnT_free = C.tok("t")
                C.barrier()
        C.barrier()


def emit_scan(C, wd, xn_sc, xT_sc, n_tok, seq, *, bias_pp):
    nc = C.nc
    n_batch = n_tok // seq
    SB = seq // 512
    with ExitStack() as ph:
        sb = lambda nm, shape, dt: ph.enter_context(nc.sbuf_tensor(C.u(nm), shape, dt))
        psum = lambda nm, shape, dt: ph.enter_context(nc.psum_tensor(C.u(nm), shape, dt))
        wz, wtok1 = load_weight_cast(C, ph, wd["L1_attn_Wz"], D, D, "sc_wz")
        wh, wtok2 = load_weight_cast(C, ph, wd["L1_attn_Wh"], D, D, "sc_wh")
        negb = sb("sc_negb", [128, PG], F32)
        inb = [sb(f"sc_in{i}", [128, PG * 512], BF16) for i in range(seq // 512)]
        zb = sb("sc_zb", [128, 512], F32)
        ab = sb("sc_ab", [128, seq], F32)
        bb = sb("sc_bb", [128, seq], F32)
        hb = sb("sc_hb", [128, seq], F32)
        xt_b = [sb(f"sc_xt{i}", [128, 512], F32) for i in range(2)]
        out_b = [sb(f"sc_ou{i}", [128, 512], F32) for i in range(2)]
        accz = [psum(f"sc_az{i}", [128, 512], F32) for i in range(2)]
        acch = [psum(f"sc_ah{i}", [128, 512], F32) for i in range(2)]
        ch_in = [C.chan(ph, "sc_chi") for i in range(seq // 512)]
        ch_xt = [C.chan(ph, "sc_chx") for i in range(2)]
        ch_o = [C.chan(ph, "sc_cho", kind="sw") for i in range(2)]
        nb = C.inc("v", nc.vector.tensor_scalar(negb[:], bias_pp[:, 0:PG], -1.0, 0.0, ALU.mult, ALU.add))
        nc.vector.drain()
        C.wait(nc.scalar, nb)
        C.wait(nc.tensor, [wtok1, wtok2])

        inb_free = [None] * SB
        ld_toks = [None] * SB
        accz_free = [None, None]
        acch_free = [None, None]
        xt_free = [None, None]
        out_free = [None, None]
        buf_free = None
        for b in range(n_batch):
            for tb in range(SB):
                gtb = b * SB + tb
                C.wait(nc.sync, inb_free[tb])
                ld_toks[tb] = C.inc(ch_in[tb], nc.sync.dma_start(
                    out=inb[tb][:].rearrange("p (g c) -> p g c", g=PG),
                    in_=xn_sc[:, :, _ts(gtb, 512)].rearrange("g p c -> p g c")), 16)
            for ot in range(PG):
                vb_prev = None
                for tb in range(SB):
                    sl = tb
                    if ot == 0:
                        C.wait(nc.tensor, ld_toks[tb])
                    m = tb % 2
                    C.wait(nc.tensor, accz_free[m])
                    mm = None
                    for it in range(PG):
                        mm = nc.tensor.matmul(
                            accz[m][:], wz[:, it * D + ot * 128:it * D + (ot + 1) * 128],
                            inb[sl][:, _ts(it, 512)], start=(it == 0), stop=(it == PG - 1))
                    rz = C.inc("t", mm)
                    C.wait(nc.tensor, acch_free[m])
                    mm = None
                    for it in range(PG):
                        mm = nc.tensor.matmul(
                            acch[m][:], wh[:, it * D + ot * 128:it * D + (ot + 1) * 128],
                            inb[sl][:, _ts(it, 512)], start=(it == 0), stop=(it == PG - 1))
                    rh = C.inc("t", mm)
                    C.wait(nc.scalar, rz)
                    C.wait(nc.scalar, buf_free)
                    C.wait(nc.scalar, vb_prev)
                    sz = C.inc("s", nc.scalar.activation(zb[:], accz[m][:], AF.Sigmoid,
                                                         bias=bias_pp[:, ot:ot + 1]))
                    nc.scalar.drain()
                    sa = C.inc("s", nc.scalar.activation(ab[:, _ts(tb, 512)], accz[m][:], AF.Sigmoid,
                                                         scale=-1.0, bias=negb[:, ot:ot + 1]))
                    accz_free[m] = sa
                    C.wait(nc.vector, [sz, sa, rh])
                    vb = C.inc("v", nc.vector.scalar_tensor_tensor(
                        bb[:, _ts(tb, 512)], acch[m][:], bias_pp[:, PG + ot:PG + ot + 1],
                        zb[:], ALU.add, ALU.mult))
                    acch_free[m] = vb
                    vb_prev = vb
                C.wait(nc.vector, C.tok("s"))
                nc.vector.drain()
                sc = C.inc("v", nc.vector.tensor_tensor_scan(
                    hb[:], ab[:], bb[:], 0.0, ALU.mult, ALU.add))
                buf_free = sc
                for tb in range(SB):
                    m2 = tb % 2
                    gtb = b * SB + tb
                    C.wait(nc.sync, xt_free[m2])
                    ldx = C.inc(ch_xt[m2], nc.sync.dma_start(
                        out=xt_b[m2][:], in_=xT_sc[ot, :, _ts(gtb, 512)]), 16)
                    C.wait(nc.vector, [ldx, sc, out_free[m2]])
                    u = C.inc("v", nc.vector.tensor_add(out_b[m2][:], hb[:, _ts(tb, 512)], xt_b[m2][:]))
                    xt_free[m2] = u
                    C.wait(nc.gpsimd, u)
                    d = C.inc(ch_o[m2], nc.gpsimd.dma_start(
                        out=xT_sc[ot, :, _ts(gtb, 512)], in_=out_b[m2][:]), 16)
                    out_free[m2] = d
            t_all = C.tok("t")
            inb_free = [t_all] * SB
        C.barrier()


def emit_moe(C, wd, y_sc, xT_sc, h_sc, wrow_sc, n_tok, *, ident_b, ident_f, rmsw_pp):
    nc = C.nc
    NTB = n_tok // 512
    NT = n_tok // 128
    E_t = DE // 128
    with ExitStack() as ph:
        sb = lambda nm, shape, dt: ph.enter_context(nc.sbuf_tensor(C.u(nm), shape, dt))
        psum = lambda nm, shape, dt: ph.enter_context(nc.psum_tensor(C.u(nm), shape, dt))
        gwf = sb("mg_gwf", [8, D], F32)
        gwT = sb("mg_gwT", [128, PG * 8], BF16)
        logit = sb("mg_log", [8, n_tok], F32)
        ltm = sb("mg_ltm", [128, NT * 8], F32)
        wtm = sb("mg_wtm", [128, NT * 8], F32)
        wfm = sb("mg_wfm", [8, n_tok], F32)
        sms = sb("mg_sms", [128, 8], F32)
        tt8 = sb("mg_tt8", [128, 8], F32)
        inb = [sb(f"mg_in{i}", [128, PG * 512], BF16) for i in range(2)]
        ps8 = psum("mg_ps8", [8, 512], F32)
        pst = psum("mg_pst", [128, 1024], BF16)
        pstf = psum("mg_pstf", [128, 512], F32)
        ch_in = [C.chan(ph, "mg_chi") for i in range(2)]
        ch_g = C.chan(ph, "mg_chg")
        ldg = C.inc(ch_g, nc.sync.dma_start(out=gwf[:], in_=wd["L1_ffn_gate_w"][:, :]), 16)
        gwb = sb("mg_gwb", [8, D], BF16)
        C.wait(nc.vector, ldg)
        cb = C.inc("v", nc.vector.tensor_copy(gwb[:], gwf[:]))
        C.wait(nc.tensor, cb)
        tr = None
        for it in range(PG):
            tr = nc.tensor.transpose(pst[:, it * 8:(it + 1) * 8], gwb[:, _ts(it)], ident_b[0:8, 0:8])
        trt = C.inc("t", tr)
        C.wait(nc.vector, trt)
        gt = None
        for it in range(PG):
            gt = nc.vector.tensor_scalar(
                gwT[:, it * 8:(it + 1) * 8], pst[:, it * 8:(it + 1) * 8],
                rmsw_pp[:, it:it + 1], 0.0, ALU.mult, ALU.add)
        gt = C.inc("v", gt)
        C.wait(nc.tensor, gt)
        inb_free = [None, None]
        lg_ev = None
        for tb in range(NTB):
            sl = tb % 2
            C.wait(nc.sync, inb_free[sl])
            ldi = C.inc(ch_in[sl], nc.sync.dma_start(
                out=inb[sl][:].rearrange("p (g c) -> p g c", g=PG),
                in_=y_sc[:, :, _ts(tb, 512)].rearrange("g p c -> p g c")), 16)
            C.wait(nc.tensor, ldi)
            C.wait(nc.tensor, lg_ev)
            mm = None
            for it in range(PG):
                mm = nc.tensor.matmul(ps8[:], gwT[:, it * 8:(it + 1) * 8],
                                      inb[sl][:, _ts(it, 512)],
                                      start=(it == 0), stop=(it == PG - 1))
            r = C.inc("t", mm)
            C.wait(nc.vector, r)
            lg_ev = C.inc("v", nc.vector.tensor_copy(logit[:, _ts(tb, 512)], ps8[:]))
            inb_free[sl] = C.tok("t")
        C.wait(nc.tensor, lg_ev)
        wb_ev = None
        for tt in range(NT):
            tr = nc.tensor.transpose(pstf[:, 0:8], logit[:, _ts(tt)], ident_f[0:8, 0:8])
            r = C.inc("t", tr)
            C.wait(nc.vector, r)
            ev = C.inc("v", nc.vector.tensor_copy(ltm[:, tt * 8:(tt + 1) * 8], pstf[:, 0:8]))
            C.wait(nc.tensor, ev)
            lt = ltm[:, tt * 8:(tt + 1) * 8]
            wt = wtm[:, tt * 8:(tt + 1) * 8]
            nc.vector.drain()
            nc.vector.tensor_reduce(sms[:, 0:1], lt, AX.X, ALU.max)
            nc.vector.drain()
            nc.vector.tensor_tensor(tt8[:], lt, sms[:, 0:1].broadcast_to([128, 8]), ALU.is_equal)
            nc.vector.drain()
            nc.vector.scalar_tensor_tensor(wt, tt8[:], -1e30, lt, ALU.mult, ALU.add)
            nc.vector.drain()
            nc.vector.tensor_reduce(sms[:, 1:2], wt, AX.X, ALU.max)
            nc.vector.drain()
            nc.vector.tensor_tensor(wt, wt, sms[:, 1:2].broadcast_to([128, 8]), ALU.is_equal)
            nc.vector.drain()
            sg = C.inc("v", nc.vector.tensor_sub(sms[:, 2:3], sms[:, 0:1], sms[:, 1:2]))
            C.wait(nc.scalar, sg)
            ss = C.inc("s", nc.scalar.activation(sms[:, 3:4], sms[:, 2:3], AF.Sigmoid))
            C.wait(nc.vector, ss)
            nc.vector.tensor_scalar(tt8[:], tt8[:], sms[:, 3:4], 0.0, ALU.mult, ALU.add)
            nc.vector.drain()
            nc.vector.scalar_tensor_tensor(wt, wt, sms[:, 3:4], wt, ALU.mult, ALU.subtract)
            nc.vector.drain()
            wv = C.inc("v", nc.vector.tensor_sub(wt, tt8[:], wt))
            C.wait(nc.tensor, wv)
            tr2 = nc.tensor.transpose(pstf[0:8, 0:128], wt, ident_f)
            r2 = C.inc("t", tr2)
            C.wait(nc.vector, r2)
            wb_ev = C.inc("v", nc.vector.tensor_copy(wfm[:, _ts(tt)], pstf[0:8, 0:128]))
            C.wait(nc.tensor, wb_ev)
        C.wait(nc.gpsimd, wb_ev)
        dw = C.inc("qg", nc.gpsimd.dma_start(out=wrow_sc[:, :], in_=wfm[:]), 16)
        C.wait(nc.gpsimd, dw)
        C.barrier()

    with ExitStack() as phe:
        wbe = phe.enter_context(nc.sbuf_tensor(C.u("me_wbe"), [128, n_tok], F32))
        ch_wb = C.chan(phe, "me_chwb")
        for e in range(NE):
            C.wait(nc.sync, C.tok("v"))
            lw = C.inc(ch_wb, nc.sync.dma_start(
                out=wbe[:], in_=wrow_sc[e:e + 1, :].broadcast_to([128, n_tok])), 16)
            with ExitStack() as ph:
                silu_b = [ph.enter_context(nc.sbuf_tensor(C.u(f"mA_sl{i}"), [128, 512], F32)) for i in range(2)]
                out_b = [ph.enter_context(nc.sbuf_tensor(C.u(f"mA_ou{i}"), [128, 512], BF16)) for i in range(2)]
                ch_o = [C.chan(ph, "mA_cho", kind="sw") for i in range(2)]
                free_tok = [None, None]
                kctr = [0]

                def epA(C, ot, tb, pss, rdy, aux):
                    k = kctr[0] % 2
                    kctr[0] += 1
                    C.wait(nc.scalar, rdy)
                    C.wait(nc.scalar, free_tok[k])
                    a = C.inc("s", nc.scalar.activation(silu_b[k][:], pss[0][:], AF.Silu))
                    C.wait(nc.vector, a)
                    C.wait(nc.vector, rdy)
                    u = C.inc("v", nc.vector.scalar_tensor_tensor(
                        out_b[k][:], pss[1][:], 1.0, silu_b[k][:], ALU.mult, ALU.mult))
                    C.wait(nc.gpsimd, u)
                    d = C.inc(ch_o[k], nc.gpsimd.dma_start(
                        out=h_sc[ot, :, _ts(tb, 512)], in_=out_b[k][:]), 16)
                    free_tok[k] = [u, d]
                    return [a, u]

                emit_gemm_oi(C, [wd["L1_ffn_W1"][e], wd["L1_ffn_W2"][e]], DE, D,
                             y_sc, n_tok, og=22, epilogue=epA, ident_f=ident_f,
                             name=C.u("mA"), rhs_scale=rmsw_pp)
            with ExitStack() as ph:
                t_b = [ph.enter_context(nc.sbuf_tensor(C.u(f"mB_t{i}"), [128, 512], F32)) for i in range(2)]
                xt_b = [ph.enter_context(nc.sbuf_tensor(C.u(f"mB_xt{i}"), [128, 512], F32)) for i in range(3)]
                out_b = [ph.enter_context(nc.sbuf_tensor(C.u(f"mB_ou{i}"), [128, 512], F32)) for i in range(2)]
                ch_xt = [C.chan(ph, "mB_chx") for i in range(3)]
                ch_o = [C.chan(ph, "mB_cho", kind="sw") for i in range(2)]
                xt_free = [None] * 3
                out_free = [None, None]
                t_free = [None, None]
                prectr = [0]
                epctr = [0]

                def preB(C, ot, tb):
                    k = prectr[0] % 3
                    prectr[0] += 1
                    C.wait(nc.sync, xt_free[k])
                    ld = C.inc(ch_xt[k], nc.sync.dma_start(
                        out=xt_b[k][:], in_=xT_sc[ot, :, _ts(tb, 512)]), 16)
                    return (k, ld)

                def epB(C, ot, tb, pss, rdy, aux):
                    k, ld = aux
                    m = epctr[0] % 2
                    epctr[0] += 1
                    C.wait(nc.vector, [rdy, lw, t_free[m]])
                    t1 = C.inc("v", nc.vector.tensor_mul(t_b[m][:], pss[0][:], wbe[:, _ts(tb, 512)]))
                    nc.vector.drain()
                    C.wait(nc.vector, [ld, out_free[m]])
                    u = C.inc("v", nc.vector.tensor_add(out_b[m][:], t_b[m][:], xt_b[k][:]))
                    xt_free[k] = u
                    t_free[m] = u
                    C.wait(nc.gpsimd, u)
                    d = C.inc(ch_o[m], nc.gpsimd.dma_start(
                        out=xT_sc[ot, :, _ts(tb, 512)], in_=out_b[m][:]), 16)
                    out_free[m] = d
                    return [t1]

                emit_gemm_oi(C, [wd["L1_ffn_W3"][e]], D, DE, h_sc, n_tok, og=8,
                             epilogue=epB, pre=preB, ident_f=ident_f, name=C.u("mB"))


def emit_out(C, xT_sc, y_out, n_tok, *, ident_f):
    nc = C.nc
    NT = n_tok // 128
    with ExitStack() as ph:
        sb = lambda nm, shape, dt: ph.enter_context(nc.sbuf_tensor(C.u(nm), shape, dt))
        psum = lambda nm, shape, dt: ph.enter_context(nc.psum_tensor(C.u(nm), shape, dt))
        ldf = [sb(f"ou_ld{i}", [128, D], F32) for i in range(2)]
        otm = [sb(f"ou_tm{i}", [128, D], F32) for i in range(2)]
        ps_f = [psum(f"ou_ps{i}", [128, 512], F32) for i in range(2)]
        ch_l = [C.chan(ph, "ou_chl") for i in range(2)]
        ch_s = [C.chan(ph, "ou_chs", kind="sw") for i in range(2)]
        ldf_free = [None, None]
        otm_free = [None, None]
        psf_free = [None, None]
        for tt in range(NT):
            j = tt % 2
            C.wait(nc.sync, ldf_free[j])
            ld = C.inc(ch_l[j], nc.sync.dma_start(
                out=ldf[j][:].rearrange("p (g c) -> p g c", g=PG),
                in_=xT_sc[:, :, _ts(tt)].rearrange("g p c -> p g c")), 16)
            C.wait(nc.tensor, ld)
            C.wait(nc.tensor, psf_free[j])
            ev = None
            for g0 in range(0, PG, 4):
                tr = None
                for g in range(g0, g0 + 4):
                    tr = nc.tensor.transpose(ps_f[j][:, _ts(g - g0)], ldf[j][:, _ts(g)], ident_f)
                trt = C.inc("t", tr)
                C.wait(nc.vector, trt)
                C.wait(nc.vector, otm_free[j])
                for g in range(g0, g0 + 4):
                    ev = nc.vector.tensor_copy(otm[j][:, _ts(g)], ps_f[j][:, _ts(g - g0)])
                ev = C.inc("v", ev)
                if g0 + 4 < PG:
                    C.wait(nc.tensor, ev)
            psf_free[j] = ev
            ldf_free[j] = C.tok("t")
            C.wait(nc.gpsimd, ev)
            st = C.inc(ch_s[j], nc.gpsimd.dma_start(out=y_out[_ts(tt), :], in_=otm[j][:]), 16)
            otm_free[j] = st
        C.barrier()


N_TOK_FULL = 4096
SEQ_FULL = 2048
N_CORES = 8

WSHAPES = {
    "L0_attn_W_q": (D, D), "L0_attn_W_dkv": (D, LAT), "L0_attn_W_uk": (LAT, D),
    "L0_attn_W_uv": (LAT, D), "L0_attn_W_o": (D, D),
    "L0_ffn_w1_w": (DH, D), "L0_ffn_w2_w": (DH, D), "L0_ffn_h1_w": (DH, DH),
    "L0_ffn_h2_w": (DH, DH), "L0_ffn_w3_w": (D, DH),
    "L1_attn_Wz": (D, D), "L1_attn_Wh": (D, D),
    "L1_ffn_gate_w": (NE, D), "L1_ffn_W1": (NE, DE, D), "L1_ffn_W2": (NE, DE, D),
    "L1_ffn_W3": (NE, D, DE),
}


def host_consts(seq):
    p = np.arange(128)
    dl = p % 64
    inv = 1.0 / 10000.0 ** (np.arange(32, dtype=np.float64) / 32)
    tpos = np.arange(seq, dtype=np.float64)
    ang = tpos[None, :] * inv[dl % 32][:, None]
    cos_t = np.cos(ang).astype(BF16NP)
    sin_t = np.where((dl < 32)[:, None], -np.sin(ang), np.sin(ang)).astype(BF16NP)
    perm = np.zeros((128, 128), np.float32)
    for m in range(128):
        blk = (m // 64) * 64
        perm[blk + ((m - blk + 32) % 64), m] = 1.0
    mask = np.where(np.arange(128)[:, None] >= np.arange(128)[None, :],
                    0.0, -1e30).astype(np.float32)
    cvec = np.zeros((128, 16), np.float32)
    cvec[:, 0] = LN_EPS
    cvec[:, 1] = RMS_EPS * D
    return {"c_ident_b": np.eye(128, dtype=BF16NP),
            "c_ident_f": np.eye(128, dtype=np.float32),
            "c_perm": perm.astype(BF16NP),
            "c_cos": cos_t, "c_sin": sin_t, "c_mask": mask, "c_cvec": cvec}


def pack_params(inputs):
    r = {}
    scl = np.empty((128, 136), np.float32)
    scl[:, 0:32] = inputs["L0_ffn_w1_s"].reshape(32, 128).T
    scl[:, 32:64] = inputs["L0_ffn_w2_s"].reshape(32, 128).T
    scl[:, 64:96] = inputs["L0_ffn_h1_s"].reshape(32, 128).T
    scl[:, 96:128] = inputs["L0_ffn_h2_s"].reshape(32, 128).T
    scl[:, 128:136] = inputs["L0_ffn_w3_s"].reshape(8, 128).T
    r["p_scl"] = scl
    gb = np.empty((128, 4), np.float32)
    gb[:, 0:2] = inputs["L0_attn_ln_gamma"].reshape(2, 128).T
    gb[:, 2:4] = inputs["L0_attn_ln_beta"].reshape(2, 128).T
    r["p_gam_bet"] = gb
    bp = np.empty((128, 2 * PG), np.float32)
    bp[:, 0:PG] = inputs["L1_attn_bz"].reshape(PG, 128).T
    bp[:, PG:] = inputs["L1_attn_bh"].reshape(PG, 128).T
    r["p_bias"] = bp
    r["p_rmsw"] = np.ascontiguousarray(inputs["L1_ffn_rms_w"].reshape(PG, 128).T)
    return r


def build_full(n_tok=N_TOK_FULL, seq=SEQ_FULL):
    nc = bass.Bass("TRN2", target_bir_lowering=False)
    F = {"x": nc.dram_tensor("x", [n_tok, D], F32, kind="ExternalInput").ap()}
    wd = {}
    for nm, shp in WSHAPES.items():
        wd[nm] = nc.dram_tensor(nm, list(shp), F32, kind="ExternalInput").ap()
    cshapes = {"c_ident_b": ([128, 128], BF16), "c_ident_f": ([128, 128], F32),
               "c_perm": ([128, 128], BF16), "c_cos": ([128, seq], BF16),
               "c_sin": ([128, seq], BF16), "c_mask": ([128, 128], F32),
               "c_cvec": ([128, 16], F32), "p_scl": ([128, 136], F32),
               "p_gam_bet": ([128, 4], F32), "p_bias": ([128, 2 * PG], F32),
               "p_rmsw": ([128, PG], F32)}
    cd = {nm: nc.dram_tensor(nm, list(shp), dt, kind="ExternalInput").ap()
          for nm, (shp, dt) in cshapes.items()}
    y_out = nc.dram_tensor("y", [n_tok, D], F32, kind="ExternalOutput").ap()

    xT_sc = nc.dram_tensor("xT_sc", [PG, 128, n_tok], F32, kind="Internal").ap()
    xn0_sc = nc.dram_tensor("xn0_sc", [PG, 128, n_tok], BF16, kind="Internal").ap()
    xn1_sc = nc.dram_tensor("xn1_sc", [PG, 128, n_tok], BF16, kind="Internal").ap()
    y_sc = nc.dram_tensor("ysc", [PG, 128, n_tok], BF16, kind="Internal").ap()
    u_sc = nc.dram_tensor("u_sc", [DH // 128, 128, n_tok], BF16, kind="Internal").ap()
    g_sc = nc.dram_tensor("g_sc", [DH // 128, 128, n_tok], BF16, kind="Internal").ap()
    wrow_sc = nc.dram_tensor("wrow_sc", [NE, n_tok], F32, kind="Internal").ap()

    with ExitStack() as es:
        C = Ctx(nc, es)
        cs = {}
        ld = None
        for nm, (shp, dt) in cshapes.items():
            t = es.enter_context(nc.sbuf_tensor(nm + "_sb", shp, dt))
            ld = C.inc("qs", nc.sync.dma_start(out=t[:], in_=cd[nm][:, :]), 16)
            cs[nm] = t[:]
        C.wait(nc.sync, ld)
        C.barrier()
        emit_ln(C, F["x"], xT_sc, xn0_sc, n_tok, is_ln1=False, y_sc=None,
                ident_b=cs["c_ident_b"], ident_f=cs["c_ident_f"], cst=cs["c_cvec"])
        emit_attn(C, wd, xn0_sc, xT_sc, n_tok, seq,
                  ident_b=cs["c_ident_b"], ident_f=cs["c_ident_f"],
                  perm_b=cs["c_perm"], cos_t=cs["c_cos"], sin_t=cs["c_sin"],
                  mask128=cs["c_mask"], gam_bet=cs["p_gam_bet"], cst=cs["c_cvec"])
        scl = cs["p_scl"]
        emit_ffn0(C, wd, xn0_sc, u_sc, g_sc, xT_sc, n_tok,
                  scl[:, 0:32], scl[:, 32:64], scl[:, 64:96], scl[:, 96:128],
                  scl[:, 128:136], cs["c_ident_f"])
        emit_ln(C, xT_sc, xT_sc, xn1_sc, n_tok, is_ln1=True, y_sc=y_sc,
                ident_b=cs["c_ident_b"], ident_f=cs["c_ident_f"], cst=cs["c_cvec"])
        emit_scan(C, wd, xn1_sc, xT_sc, n_tok, seq, bias_pp=cs["p_bias"])
        emit_moe(C, wd, y_sc, xT_sc, u_sc[0:DE // 128], wrow_sc, n_tok,
                 ident_b=cs["c_ident_b"], ident_f=cs["c_ident_f"],
                 rmsw_pp=cs["p_rmsw"])
        C.barrier()
        emit_out(C, xT_sc, y_out, n_tok, ident_f=cs["c_ident_f"])
    return nc


def kernel(**inputs):
    x = np.ascontiguousarray(np.asarray(inputs["L0_attn_x"], dtype=np.float32))
    B, S, _ = x.shape
    per = B // N_CORES
    consts = host_consts(S)
    params = pack_params({k: np.asarray(v, dtype=np.float32) for k, v in inputs.items()
                          if k.endswith(("_s", "gamma", "beta", "bz", "bh", "rms_w"))})
    weights = {nm: np.ascontiguousarray(np.asarray(inputs[nm], dtype=np.float32))
               for nm in WSHAPES}
    nc = build_full(per * S, S)
    in_maps = []
    for c in range(N_CORES):
        m = {"x": np.ascontiguousarray(x[c * per:(c + 1) * per].reshape(per * S, D))}
        m.update(weights)
        m.update(consts)
        m.update(params)
        in_maps.append(m)
    res = run_bass_kernel_spmd(nc, in_maps, core_ids=list(range(N_CORES)))
    out = np.concatenate([res.results[c]["y"].reshape(per, S, D)
                          for c in range(N_CORES)], axis=0)
    return out


# version 04
# speedup vs baseline: 1.0436x; 1.0049x over previous
import numpy as np
import ml_dtypes
from contextlib import ExitStack

import concourse.bass as bass
import concourse.mybir as mybir
from concourse.bass_utils import run_bass_kernel_spmd

F32 = mybir.dt.float32
BF16 = mybir.dt.bfloat16
AF = mybir.ActivationFunctionType
ALU = mybir.AluOpType
AX = mybir.AxisListType
BF16NP = ml_dtypes.bfloat16

D = 1024
PG = D // 128
NH = 16
HD = 64
LAT = 256
DH = 4096
NE = 8
DE = 2816
LN_EPS = 1e-5
RMS_EPS = 1e-6


class Ctx:

    def __init__(self, nc, es):
        self.nc = nc
        self.es = es
        self._sems = {}
        self._by_num = {}
        for nm in ("qs", "qg", "qa", "t", "v", "s", "g"):
            h = es.enter_context(nc.semaphore("sem_" + nm))
            self._sems[nm] = [h, 0]

    def inc(self, nm, instr, k=1):
        ent = self._sems[nm]
        instr.then_inc(ent[0], k)
        ent[1] += k
        return (nm, ent[1])

    def n(self, nm):
        return self._sems[nm][1]

    def tok(self, nm):
        return (nm, self._sems[nm][1])

    _uctr = 0

    def u(self, nm):
        Ctx._uctr += 1
        return f"{nm}_{Ctx._uctr}"

    POOL_HW = 44
    POOL_SW = 24

    def chan(self, scope_es, name="ch", kind="hw"):
        if not hasattr(self, "_pool"):
            self._pool = {"hw": [], "sw": []}
            self._pool_i = {"hw": 0, "sw": 0}
            for i in range(self.POOL_HW):
                key = f"hch{i}"
                h = self.es.enter_context(self.nc.semaphore(key))
                self._sems[key] = [h, 0]
                self._pool["hw"].append(key)
            for i in range(self.POOL_SW):
                key = f"sch{i}"
                h = self.es.enter_context(self.nc.semaphore(key))
                self._sems[key] = [h, 0]
                self._pool["sw"].append(key)
        pool = self._pool[kind]
        key = pool[self._pool_i[kind] % len(pool)]
        self._pool_i[kind] += 1
        return key

    def wait(self, eng, tok):
        if tok is None:
            return
        if isinstance(tok, list):
            best = {}
            for t in tok:
                if t is None:
                    continue
                nm, val = t
                best[nm] = max(best.get(nm, 0), val)
            for nm, val in best.items():
                self.wait(eng, (nm, val))
            return
        nm, val = tok
        ent = self._sems[nm]
        assert val <= ent[1], f"wait({nm},{val}) but only {ent[1]} emitted"
        if val > 0:
            eng.wait_ge(ent[0], val)

    def barrier(self):
        nc = self.nc
        for key, ent in list(self._sems.items()):
            if key in ("t", "v", "s", "g"):
                continue
            if ent[1] > 0:
                nc.vector.wait_ge(ent[0], ent[1])
        nc.all_engine_barrier()


def _ts(i, n=128):
    return slice(i * n, (i + 1) * n)


def _ceil(a, b):
    return (a + b - 1) // b


def emit_ln(C, x_in_tm, xT_sc, xn_sc, n_tok, *, is_ln1, y_sc, ident_b, ident_f, cst):
    nc = C.nc
    NT = n_tok // 128
    with ExitStack() as ph:
        sb = lambda nm, shape, dt: ph.enter_context(nc.sbuf_tensor(C.u(nm), shape, dt))
        psum = lambda nm, shape, dt: ph.enter_context(nc.psum_tensor(C.u(nm), shape, dt))
        NB = 3
        xt = [sb(f"ln_x{i}", [128, D], F32) for i in range(NB)]
        fxn = [sb(f"ln_fxn{i}", [128, D], F32) for i in range(NB)]
        xsqd = sb("ln_xsqd", [128, D], F32)
        stats = [sb(f"ln_st{i}", [128, 12], F32) for i in range(NB)]
        xn_tm = [sb(f"ln_xn{i}", [128, D], BF16) for i in range(NB)]
        xs_tm = [sb(f"ln_xs{i}", [128, D], F32) for i in range(NB)]
        y_tm = [sb(f"ln_y{i}", [128, D], BF16) for i in range(NB)]
        stg_b = [sb(f"ln_sgb{i}", [128, D], BF16) for i in range(2)]
        stg_f = [sb(f"ln_sgf{i}", [128, D], F32) for i in range(2)]
        stg_y = [sb(f"ln_sgy{i}", [128, D], BF16) for i in range(2)]
        ldf = [sb(f"ln_ldf{i}", [128, D], F32) for i in range(2)]
        ps_b = [psum(f"ln_pb{i}", [128, 1024], BF16) for i in range(2)]
        ps_f = [psum(f"ln_pf{i}", [128, 512], F32) for i in range(2)]

        x_free = [None] * NB
        st_free = [None] * NB
        buf_free = [None] * NB
        stg_free = [None] * 2
        psb_free = [None] * 2
        psf_free = [None] * 2
        ldf_free = [None] * 2
        ch_x = [C.chan(ph, "ln_chx") for i in range(NB)]
        ch_ldf = [C.chan(ph, "ln_chl") for i in range(2)]
        ch_st = [C.chan(ph, "ln_chs", kind="sw") for i in range(2)]

        for tt in range(NT):
            i = tt % NB
            j = tt % 2
            ts = _ts(tt)
            if is_ln1:
                C.wait(nc.sync, ldf_free[j])
                ld = C.inc(ch_ldf[j], nc.sync.dma_start(
                    out=ldf[j][:].rearrange("p (g c) -> p g c", g=PG),
                    in_=x_in_tm[:, :, ts].rearrange("g p c -> p g c")), 16)
                C.wait(nc.tensor, ld)
                C.wait(nc.tensor, psf_free[j])
                C.wait(nc.scalar, x_free[i])
                ev = None
                for g0 in range(0, PG, 4):
                    tr = None
                    for g in range(g0, g0 + 4):
                        tr = nc.tensor.transpose(ps_f[j][:, _ts(g - g0)], ldf[j][:, _ts(g)], ident_f)
                    tt_tok = C.inc("t", tr)
                    C.wait(nc.scalar, tt_tok)
                    for g in range(g0, g0 + 4):
                        ev = nc.scalar.copy(xt[i][:, _ts(g)], ps_f[j][:, _ts(g - g0)])
                    ev = C.inc("s", ev)
                    if g0 + 4 < PG:
                        C.wait(nc.tensor, ev)
                psf_free[j] = ev
                ldf_free[j] = C.tok("t")
                nc.scalar.drain()
                ld = ev
            else:
                C.wait(nc.sync, x_free[i])
                ld = C.inc(ch_x[i], nc.sync.dma_start(out=xt[i][:], in_=x_in_tm[ts, :]), 16)

            st = stats[i]
            s1, ssq, mean, var, rstd = st[:, 0:1], st[:, 1:2], st[:, 2:3], st[:, 3:4], st[:, 4:5]
            msq, lnv = st[:, 5:6], st[:, 6:7]
            C.wait(nc.scalar, ld)
            C.wait(nc.scalar, st_free[i])
            sq = C.inc("s", nc.scalar.activation(xsqd[:], xt[i][:], AF.Square, accum_out=ssq))
            C.wait(nc.vector, ld)
            C.wait(nc.vector, sq)
            nc.vector.tensor_reduce(s1, xt[i][:], AX.X, ALU.add)
            nc.vector.drain()
            nc.vector.tensor_scalar(mean, s1, 1.0 / D, 0.0, ALU.mult, ALU.add)
            nc.vector.drain()
            nc.vector.tensor_tensor(msq, mean, mean, ALU.mult)
            nc.vector.drain()
            vv = C.inc("v", nc.vector.scalar_tensor_tensor(var, ssq, 1.0 / D, msq, ALU.mult, ALU.subtract))
            C.wait(nc.scalar, vv)
            nc.scalar.activation(lnv, var, AF.Ln, bias=cst[:, 0:1])
            nc.scalar.drain()
            rs = C.inc("s", nc.scalar.activation(rstd, lnv, AF.Exp, scale=-0.5))
            C.wait(nc.vector, rs)
            C.wait(nc.vector, buf_free[i])
            if is_ln1:
                nc.vector.tensor_scalar(xn_tm[i][:], xt[i][:], mean, rstd, ALU.subtract, ALU.mult)
                fx = C.inc("v", nc.vector.tensor_scalar(fxn[i][:], xt[i][:], mean, rstd, ALU.subtract, ALU.mult))
                C.wait(nc.scalar, fx)
                sq2 = C.inc("s", nc.scalar.activation(xsqd[:], fxn[i][:], AF.Square, accum_out=st[:, 7:8]))
                nc.scalar.drain()
                nc.scalar.activation(st[:, 8:9], st[:, 7:8], AF.Ln, bias=cst[:, 1:2])
                nc.scalar.drain()
                rs2 = C.inc("s", nc.scalar.activation(st[:, 9:10], st[:, 8:9], AF.Exp, scale=-0.5))
                C.wait(nc.vector, rs2)
                nc.vector.tensor_scalar(st[:, 10:11], st[:, 9:10], float(np.sqrt(D)), 0.0, ALU.mult, ALU.add)
                nc.vector.drain()
                nc.vector.tensor_scalar(y_tm[i][:], fxn[i][:], st[:, 10:11], 0.0, ALU.mult, ALU.add)
                apl = C.inc("v", nc.vector.tensor_add(xs_tm[i][:], xt[i][:], fxn[i][:]))
                sq2_tok = sq2
            else:
                apl = C.inc("v", nc.vector.tensor_scalar(xn_tm[i][:], xt[i][:], mean, rstd, ALU.subtract, ALU.mult))
                sq2_tok = None
            st_free[i] = apl

            src_f = xs_tm[i] if is_ln1 else xt[i]
            C.wait(nc.tensor, apl)
            C.wait(nc.tensor, psb_free[j])
            C.wait(nc.vector, stg_free[j])
            C.wait(nc.scalar, stg_free[j])
            tr = None
            for g in range(PG):
                tr = nc.tensor.transpose(ps_b[j][:, _ts(g)], xn_tm[i][:, _ts(g)], ident_b)
            tb = C.inc("t", tr)
            C.wait(nc.vector, tb)
            ev_b = None
            for g in range(PG):
                ev_b = nc.vector.tensor_copy(stg_b[j][:, _ts(g)], ps_b[j][:, _ts(g)])
            ev_b = C.inc("v", ev_b)
            psb_free[j] = ev_b
            C.wait(nc.tensor, psf_free[j])
            ev_f = None
            for g0 in range(0, PG, 4):
                tr = None
                for g in range(g0, g0 + 4):
                    tr = nc.tensor.transpose(ps_f[j][:, _ts(g - g0)], src_f[:, _ts(g)], ident_f)
                tf = C.inc("t", tr)
                C.wait(nc.scalar, tf)
                for g in range(g0, g0 + 4):
                    ev_f = nc.scalar.copy(stg_f[j][:, _ts(g)], ps_f[j][:, _ts(g - g0)])
                ev_f = C.inc("s", ev_f)
                if g0 + 4 < PG:
                    C.wait(nc.tensor, ev_f)
            psf_free[j] = ev_f
            ev_y = None
            if is_ln1:
                C.wait(nc.tensor, ev_b)
                tr = None
                for g in range(PG):
                    tr = nc.tensor.transpose(ps_b[j][:, _ts(g)], y_tm[i][:, _ts(g)], ident_b)
                ty = C.inc("t", tr)
                C.wait(nc.vector, ty)
                for g in range(PG):
                    ev_y = nc.vector.tensor_copy(stg_y[j][:, _ts(g)], ps_b[j][:, _ts(g)])
                ev_y = C.inc("v", ev_y)
                psb_free[j] = ev_y
            t_done = C.tok("t")
            buf_free[i] = [t_done, sq2_tok]
            x_free[i] = [apl, sq, t_done]

            C.wait(nc.gpsimd, ev_b if ev_y is None else ev_y)
            C.wait(nc.gpsimd, ev_f)
            C.inc(ch_st[j], nc.gpsimd.dma_start(
                out=xn_sc[:, :, ts].rearrange("g p c -> p g c"),
                in_=stg_b[j][:].rearrange("p (g c) -> p g c", g=PG)), 16)
            sg = C.inc(ch_st[j], nc.gpsimd.dma_start(
                out=xT_sc[:, :, ts].rearrange("g p c -> p g c"),
                in_=stg_f[j][:].rearrange("p (g c) -> p g c", g=PG)), 16)
            if is_ln1:
                sg = C.inc(ch_st[j], nc.gpsimd.dma_start(
                    out=y_sc[:, :, ts].rearrange("g p c -> p g c"),
                    in_=stg_y[j][:].rearrange("p (g c) -> p g c", g=PG)), 16)
            stg_free[j] = sg
        C.barrier()


def emit_gemm_oi(C, Ws, O, I, in_sc, n_tok, *, og, epilogue, pre=None,
                 ident_f, name, tb_bytes=512, rhs_scale=None):
    nc = C.nc
    O_t, I_t, NTB = O // 128, I // 128, n_tok // 512
    NW = len(Ws)
    with ExitStack() as ph:
        sb = lambda nm, shape, dt: ph.enter_context(nc.sbuf_tensor(C.u(name + nm), shape, dt))
        psum = lambda nm, shape, dt: ph.enter_context(nc.psum_tensor(C.u(name + nm), shape, dt))
        wstage = [sb(f"_wst{i}", [128, I], F32) for i in range(2)]
        wT = [sb(f"_wT{w}", [128, og * I_t * 128], BF16) for w in range(NW)]
        NIB = 3 if I_t <= 16 else 2
        inb = [sb(f"_in{i}", [128, I_t * 512], BF16) for i in range(NIB)]
        wps = [psum(f"_wps{i}", [128, 512], F32) for i in range(2)]
        acc = [psum(f"_acc{i}", [128, 512], F32) for i in range(2 * NW)]

        wstage_free = [None] * 2
        wps_free = [None] * 2
        inb_free = [None] * NIB
        acc_free = [None] * (2 * NW)
        group_mm_done = None
        ch_w = [C.chan(ph, name + "_chw") for i in range(2)]
        ch_in = [C.chan(ph, name + "_chi") for i in range(NIB)]

        n_groups = _ceil(O_t, og)
        for grp in range(n_groups):
            ots = list(range(grp * og, min((grp + 1) * og, O_t)))
            k = 0
            for w in range(NW):
                for osl, ot in enumerate(ots):
                    sl = k % 2
                    k += 1
                    C.wait(nc.sync, wstage_free[sl])
                    ld = C.inc(ch_w[sl], nc.sync.dma_start(
                        out=wstage[sl][:], in_=Ws[w][_ts(ot), :]), 16)
                    C.wait(nc.tensor, ld)
                    ev = None
                    for it0 in range(0, I_t, 4):
                        b = (it0 // 4) % 2
                        C.wait(nc.tensor, wps_free[b])
                        tr = None
                        nsub = min(4, I_t - it0)
                        for q in range(nsub):
                            tr = nc.tensor.transpose(
                                wps[b][:, _ts(q)], wstage[sl][:, _ts(it0 + q)], ident_f)
                        trt = C.inc("t", tr)
                        C.wait(nc.vector, trt)
                        if group_mm_done is not None:
                            C.wait(nc.vector, group_mm_done)
                        for q in range(nsub):
                            if rhs_scale is None:
                                ev = nc.vector.tensor_copy(
                                    wT[w][:, _ts(osl * I_t + it0 + q)], wps[b][:, _ts(q)])
                            else:
                                ev = nc.vector.tensor_scalar(
                                    wT[w][:, _ts(osl * I_t + it0 + q)], wps[b][:, _ts(q)],
                                    rhs_scale[:, it0 + q:it0 + q + 1], 0.0, ALU.mult, ALU.add)
                        ev = C.inc("v", ev)
                        wps_free[b] = ev
                    wstage_free[sl] = C.tok("t")
                    wT_ready = ev
            wT_tok = wT_ready

            for tb in range(NTB):
                sl = tb % NIB
                C.wait(nc.sync, inb_free[sl])
                ldi = C.inc(ch_in[sl], nc.sync.dma_start(
                    out=inb[sl][:].rearrange("p (g c) -> p g c", g=I_t),
                    in_=in_sc[:, :, _ts(tb, 512)].rearrange("g p c -> p g c")), 16)
                C.wait(nc.tensor, ldi)
                if tb == 0:
                    C.wait(nc.tensor, wT_tok)
                last_mm = None
                for osl, ot in enumerate(ots):
                    aux = pre(C, ot, tb) if pre is not None else None
                    pss = []
                    for w in range(NW):
                        a = (((tb * len(ots) + osl) % 2) * NW) + w
                        C.wait(nc.tensor, acc_free[a])
                        for it in range(I_t):
                            last_mm = nc.tensor.matmul(
                                acc[a][:], wT[w][:, _ts(osl * I_t + it)],
                                inb[sl][:, _ts(it, 512)],
                                start=(it == 0), stop=(it == I_t - 1))
                        pss.append(acc[a])
                    rdy = C.inc("t", last_mm)
                    frees = epilogue(C, ot, tb, pss, rdy, aux)
                    for w in range(NW):
                        a = (((tb * len(ots) + osl) % 2) * NW) + w
                        acc_free[a] = frees
                inb_free[sl] = C.tok("t")
            group_mm_done = C.tok("t")
        C.barrier()


def emit_ffn0(C, wd, xn_sc, u_sc, g_sc, xT_sc, n_tok, sc_w1, sc_w2, sc_h1,
              sc_h2, sc_w3, ident_f, stages="ABCD"):
    nc = C.nc

    if "A" in stages:
      with ExitStack() as ph:
        silu_b = [ph.enter_context(nc.sbuf_tensor(C.u(f"fA_sl{i}"), [128, 512], F32)) for i in range(2)]
        out_b = [ph.enter_context(nc.sbuf_tensor(C.u(f"fA_ou{i}"), [128, 512], BF16)) for i in range(2)]
        ch_o = [C.chan(ph, "fA_cho", kind="sw") for i in range(2)]
        free_tok = [None, None]
        kctr = [0]

        def epA(C, ot, tb, pss, rdy, aux):
            k = kctr[0] % 2
            kctr[0] += 1
            C.wait(nc.scalar, rdy)
            C.wait(nc.scalar, free_tok[k])
            a = C.inc("s", nc.scalar.activation(silu_b[k][:], pss[0][:], AF.Silu,
                                                scale=sc_w1[:, ot:ot + 1]))
            C.wait(nc.vector, a)
            C.wait(nc.vector, rdy)
            u = C.inc("v", nc.vector.scalar_tensor_tensor(
                out_b[k][:], pss[1][:], sc_w2[:, ot:ot + 1], silu_b[k][:],
                ALU.mult, ALU.mult))
            C.wait(nc.gpsimd, u)
            d = C.inc(ch_o[k], nc.gpsimd.dma_start(
                out=u_sc[ot, :, _ts(tb, 512)], in_=out_b[k][:]), 16)
            free_tok[k] = [u, d]
            return [a, u]

        emit_gemm_oi(C, [wd["L0_ffn_w1_w"], wd["L0_ffn_w2_w"]], DH, D, xn_sc,
                     n_tok, og=8, epilogue=epA, ident_f=ident_f, name="fA")

    for tag, wkey, scl, src, dst in (("fB", "L0_ffn_h1_w", sc_h1, u_sc, g_sc),
                                     ("fC", "L0_ffn_h2_w", sc_h2, g_sc, u_sc)):
        if tag[1] not in stages:
            continue
        with ExitStack() as ph:
            out_b = [ph.enter_context(nc.sbuf_tensor(C.u(f"{tag}_ou{i}"), [128, 512], BF16)) for i in range(2)]
            ch_o = [C.chan(ph, tag + "_cho", kind="sw") for i in range(2)]
            free_tok = [None, None]
            kctr = [0]

            def epB(C, ot, tb, pss, rdy, aux, out_b=out_b, free_tok=free_tok,
                    kctr=kctr, scl=scl, dst=dst, ch_o=ch_o):
                k = kctr[0] % 2
                kctr[0] += 1
                C.wait(nc.scalar, rdy)
                C.wait(nc.scalar, free_tok[k])
                a = C.inc("s", nc.scalar.activation(out_b[k][:], pss[0][:], AF.Gelu,
                                                    scale=scl[:, ot:ot + 1]))
                C.wait(nc.gpsimd, a)
                d = C.inc(ch_o[k], nc.gpsimd.dma_start(
                    out=dst[ot, :, _ts(tb, 512)], in_=out_b[k][:]), 16)
                free_tok[k] = [d]
                return [a]

            emit_gemm_oi(C, [wd[wkey]], DH, DH, src, n_tok, og=8, epilogue=epB,
                         ident_f=ident_f, name=tag)

    if "D" in stages:
      with ExitStack() as ph:
        xt_b = [ph.enter_context(nc.sbuf_tensor(C.u(f"fD_xt{i}"), [128, 512], F32)) for i in range(3)]
        out_b = [ph.enter_context(nc.sbuf_tensor(C.u(f"fD_ou{i}"), [128, 512], F32)) for i in range(2)]
        ch_xt = [C.chan(ph, "fD_chx", kind="sw") for i in range(3)]
        ch_o = [C.chan(ph, "fD_cho", kind="sw") for i in range(2)]
        xt_free = [None] * 3
        out_free = [None, None]
        prectr = [0]
        epctr = [0]

        def preD(C, ot, tb):
            k = prectr[0] % 3
            prectr[0] += 1
            C.wait(nc.gpsimd, xt_free[k])
            ld = C.inc(ch_xt[k], nc.gpsimd.dma_start(
                out=xt_b[k][:], in_=xT_sc[ot, :, _ts(tb, 512)]), 16)
            return (k, ld)

        def epD(C, ot, tb, pss, rdy, aux):
            k, ld = aux
            m = epctr[0] % 2
            epctr[0] += 1
            C.wait(nc.vector, rdy)
            C.wait(nc.vector, ld)
            C.wait(nc.vector, out_free[m])
            u = C.inc("v", nc.vector.scalar_tensor_tensor(
                out_b[m][:], pss[0][:], sc_w3[:, ot:ot + 1], xt_b[k][:],
                ALU.mult, ALU.add))
            xt_free[k] = u
            C.wait(nc.gpsimd, u)
            d = C.inc(ch_o[m], nc.gpsimd.dma_start(
                out=xT_sc[ot, :, _ts(tb, 512)], in_=out_b[m][:]), 16)
            out_free[m] = [d]
            return [u]

        emit_gemm_oi(C, [wd["L0_ffn_w3_w"]], D, DH, u_sc, n_tok, og=8,
                     epilogue=epD, pre=preD, ident_f=ident_f, name="fD")


def emit_weight_load(C, stage_es, dst_t, W_dram, I, O, name):
    nc = C.nc
    I_t = I // 128
    dst = dst_t
    stage = [stage_es.enter_context(nc.sbuf_tensor(C.u(name + f"_st{i}"), [128, O], F32)) for i in range(2)]
    ch_st = [C.chan(stage_es, name + "_ch") for i in range(2)]
    st_free = [None, None]
    ev = None
    for it in range(I_t):
        sl = it % 2
        C.wait(nc.sync, st_free[sl])
        ld = C.inc(ch_st[sl], nc.sync.dma_start(out=stage[sl][:], in_=W_dram[_ts(it), :]), 16)
        C.wait(nc.vector, ld)
        ev = C.inc("v", nc.vector.tensor_copy(dst[:, it * O:(it + 1) * O], stage[sl][:]))
        st_free[sl] = ev
    return ev


def load_weight_cast(C, ph, W_dram, I, O, name, stage_es=None):
    nc = C.nc
    if stage_es is None:
        stage_es = ph
    I_t = I // 128
    dst = ph.enter_context(nc.sbuf_tensor(C.u(name), [128, I_t * O], BF16))
    stage = [stage_es.enter_context(nc.sbuf_tensor(C.u(name + f"_st{i}"), [128, O], F32)) for i in range(2)]
    ch_st = [C.chan(ph, name + "_ch") for i in range(2)]
    st_free = [None, None]
    ev = None
    for it in range(I_t):
        sl = it % 2
        C.wait(nc.sync, st_free[sl])
        ld = C.inc(ch_st[sl], nc.sync.dma_start(out=stage[sl][:], in_=W_dram[_ts(it), :]), 16)
        C.wait(nc.vector, ld)
        ev = C.inc("v", nc.vector.tensor_copy(dst[:, it * O:(it + 1) * O], stage[sl][:]))
        st_free[sl] = ev
    return dst, ev


def emit_attn(C, wd, xn_sc, xT_sc, n_tok, seq, *, ident_b, ident_f, perm_b,
              cos_t, sin_t, mask128, gam_bet, cst, dbg=None, upto=3):
    nc = C.nc
    n_batch = n_tok // seq
    SB = seq // 512
    ST = seq // 128
    LT = LAT // 128
    with ExitStack() as ph:
        sb = lambda nm, shape, dt: ph.enter_context(nc.sbuf_tensor(C.u(nm), shape, dt))
        psum = lambda nm, shape, dt: ph.enter_context(nc.psum_tensor(C.u(nm), shape, dt))
        q_sb = sb("at_q", [128, PG * seq], BF16)
        k_sb = sb("at_k", [128, PG * seq], BF16)
        v_tm = sb("at_v", [128, ST * D], BF16)
        inb = [sb(f"at_in{i}", [128, PG * 512], BF16) for i in range(2)]
        ch_in = [C.chan(ph, "at_chi") for i in range(2)]
        ch_xt = [C.chan(ph, "at_chx") for i in range(2)]
        ch_o = [C.chan(ph, "at_cho", kind="sw") for i in range(2)]
        wdkv = ph.enter_context(nc.sbuf_tensor(C.u("at_wdkv"), [128, PG * LAT], BF16))
        wuk = ph.enter_context(nc.sbuf_tensor(C.u("at_wuk"), [128, (LAT // 128) * D], BF16))
        wuv = ph.enter_context(nc.sbuf_tensor(C.u("at_wuv"), [128, (LAT // 128) * D], BF16))
        wq = ph.enter_context(nc.sbuf_tensor(C.u("at_wq"), [128, PG * D], BF16))
        wo = ph.enter_context(nc.sbuf_tensor(C.u("at_wo"), [128, PG * D], BF16))
        with ExitStack() as wsc:
            wt1 = emit_weight_load(C, wsc, wdkv, wd["L0_attn_W_dkv"], D, LAT, "at_wdkv")
            wt2 = emit_weight_load(C, wsc, wuk, wd["L0_attn_W_uk"], LAT, D, "at_wuk")
            wt3 = emit_weight_load(C, wsc, wuv, wd["L0_attn_W_uv"], LAT, D, "at_wuv")
            wt4 = emit_weight_load(C, wsc, wq, wd["L0_attn_W_q"], D, D, "at_wq")
            wt5 = emit_weight_load(C, wsc, wo, wd["L0_attn_W_o"], D, D, "at_wo")
            C.wait(nc.vector, [wt1, wt2, wt3, wt4, wt5])
            C.barrier()

        for b in range(n_batch):
            cols = lambda tb: slice((b * SB + tb) * 512, (b * SB + tb + 1) * 512)

            with ExitStack() as s1:
                sb1 = lambda nm, shape, dt: s1.enter_context(nc.sbuf_tensor(C.u(nm), shape, dt))
                ps1 = lambda nm, shape, dt: s1.enter_context(nc.psum_tensor(C.u(nm), shape, dt))
                wtok, wtok2, wtok3 = wt1, wt2, wt3
                latn = sb1("at_latn", [128, LT * seq], BF16)
                lat_c = sb1("at_latc", [128, LT * 512], F32)
                sqd = sb1("at_sqd", [128, 512], F32)
                srow = sb1("at_srow", [1, 512], F32)
                srow2 = sb1("at_srow2", [1, 512], F32)
                mbc = sb1("at_mbc", [128, 512], F32)
                rsc = sb1("at_rsc", [128, 512], F32)
                ones128 = sb1("at_o128", [128, 1], F32)
                ones1 = sb1("at_o1", [1, 128], F32)
                acc = [ps1(f"at_ac{i}", [128, 512], F32) for i in range(4)]
                ps_row = ps1("at_pr", [1, 512], F32)
                ps_bc = ps1("at_bc", [128, 512], F32)
                ms = C.inc("v", nc.vector.memset(ones128[:], 1.0))
                ms = C.inc("v", nc.vector.memset(ones1[:], 1.0))
                nc.vector.drain()
                C.wait(nc.tensor, ms)
                C.wait(nc.tensor, [wtok, wtok2, wtok3])
                inb_free = [C.tok("t"), C.tok("t")]
                acc_free = [None] * 4
                lat_free = None
                bc_free = None
                latn_done = None
                for tb in range(SB):
                    sl = tb % 2
                    C.wait(nc.sync, inb_free[sl])
                    ldi = C.inc(ch_in[sl], nc.sync.dma_start(
                        out=inb[sl][:].rearrange("p (g c) -> p g c", g=PG),
                        in_=xn_sc[:, :, cols(tb)].rearrange("g p c -> p g c")), 16)
                    C.wait(nc.tensor, ldi)
                    lat_ev = None
                    for ot in range(LT):
                        a = ot % 2
                        C.wait(nc.tensor, acc_free[a])
                        mm = None
                        for it in range(PG):
                            mm = nc.tensor.matmul(
                                acc[a][:], wdkv[:, it * LAT + ot * 128:it * LAT + (ot + 1) * 128],
                                inb[sl][:, _ts(it, 512)], start=(it == 0), stop=(it == PG - 1))
                        rdy = C.inc("t", mm)
                        C.wait(nc.vector, rdy)
                        C.wait(nc.vector, lat_free)
                        lat_ev = C.inc("v", nc.vector.tensor_copy(lat_c[:, _ts(ot, 512)], acc[a][:]))
                        acc_free[a] = lat_ev
                    inb_free[sl] = C.tok("t")
                    C.wait(nc.tensor, lat_ev)
                    mm = None
                    for g in range(LT):
                        mm = nc.tensor.matmul(ps_row[:], ones128[:], lat_c[:, _ts(g, 512)],
                                              start=(g == 0), stop=(g == LT - 1))
                    r1 = C.inc("t", mm)
                    C.wait(nc.vector, r1)
                    e1 = C.inc("v", nc.vector.tensor_copy(srow[0:1, :], ps_row[0:1, :]))
                    C.wait(nc.tensor, e1)
                    sq_rd = None
                    r2 = None
                    for g in range(LT):
                        C.wait(nc.vector, sq_rd)
                        lslc = lat_c[:, _ts(g, 512)]
                        sq = C.inc("v", nc.vector.tensor_mul(sqd[:], lslc, lslc))
                        C.wait(nc.tensor, sq)
                        r2 = C.inc("t", nc.tensor.matmul(ps_row[:], ones128[:], sqd[:],
                                                         start=(g == 0), stop=(g == LT - 1)))
                        sq_rd = r2
                    C.wait(nc.vector, r2)
                    e2 = C.inc("v", nc.vector.tensor_copy(srow2[0:1, :], ps_row[0:1, :]))
                    C.wait(nc.tensor, e2)
                    nc.vector.drain()
                    nc.vector.tensor_scalar(srow[0:1, :], srow[0:1, :], 1.0 / LAT, 0.0, ALU.mult, ALU.add)
                    nc.vector.drain()
                    nc.vector.tensor_scalar(srow2[0:1, :], srow2[0:1, :], 1.0 / LAT, 0.0, ALU.mult, ALU.add)
                    nc.vector.drain()
                    nc.vector.tensor_mul(sqd[0:1, :], srow[0:1, :], srow[0:1, :])
                    nc.vector.drain()
                    vv = C.inc("v", nc.vector.tensor_sub(srow2[0:1, :], srow2[0:1, :], sqd[0:1, :]))
                    C.wait(nc.scalar, vv)
                    nc.scalar.activation(srow2[0:1, :], srow2[0:1, :], AF.Ln, bias=cst[0:1, 0:1])
                    nc.scalar.drain()
                    rr = C.inc("s", nc.scalar.activation(srow2[0:1, :], srow2[0:1, :], AF.Exp, scale=-0.5))
                    C.wait(nc.tensor, rr)
                    C.wait(nc.tensor, bc_free)
                    mm = nc.tensor.matmul(ps_bc[:], ones1[:], srow[0:1, :], start=True, stop=True)
                    r = C.inc("t", mm)
                    C.wait(nc.vector, r)
                    b1 = C.inc("v", nc.vector.tensor_copy(mbc[:], ps_bc[:]))
                    C.wait(nc.tensor, b1)
                    mm = nc.tensor.matmul(ps_bc[:], ones1[:], srow2[0:1, :], start=True, stop=True)
                    r = C.inc("t", mm)
                    C.wait(nc.vector, r)
                    b2 = C.inc("v", nc.vector.tensor_copy(rsc[:], ps_bc[:]))
                    bc_free = b2
                    for g in range(LT):
                        nc.vector.tensor_sub(lat_c[:, _ts(g, 512)], lat_c[:, _ts(g, 512)], mbc[:])
                        nc.vector.drain()
                        nc.vector.tensor_mul(lat_c[:, _ts(g, 512)], lat_c[:, _ts(g, 512)], rsc[:])
                        nc.vector.drain()
                        latn_done = C.inc("v", nc.vector.tensor_scalar(
                            latn[:, g * seq + tb * 512: g * seq + (tb + 1) * 512],
                            lat_c[:, _ts(g, 512)],
                            gam_bet[:, g:g + 1], gam_bet[:, LT + g:LT + g + 1], ALU.mult, ALU.add))
                    lat_free = latn_done
                bdone = latn_done
                C.wait(nc.tensor, bdone)
                vtmp = sb1("at_vtmp", [128, 512], BF16)
                ps_tr = ps1("at_ptr", [128, 1024], BF16)
                vfm_dbg = sb1("at_vfm", [128, PG * seq], BF16) if (dbg is not None and "vfm" in dbg) else None
                for tb in range(SB):
                    for ot in range(PG):
                        a = (tb * PG + ot) % 4
                        C.wait(nc.tensor, acc_free[a])
                        mm = None
                        for it in range(LT):
                            mm = nc.tensor.matmul(
                                acc[a][:], wuk[:, it * D + ot * 128:it * D + (ot + 1) * 128],
                                latn[:, it * seq + tb * 512: it * seq + (tb + 1) * 512],
                                start=(it == 0), stop=(it == LT - 1))
                        rdy = C.inc("t", mm)
                        C.wait(nc.vector, rdy)
                        evk = C.inc("v", nc.vector.tensor_copy(
                            k_sb[:, ot * seq + tb * 512: ot * seq + (tb + 1) * 512], acc[a][:]))
                        C.wait(nc.tensor, evk)
                        mm = None
                        for it in range(LT):
                            mm = nc.tensor.matmul(
                                acc[a][:], wuv[:, it * D + ot * 128:it * D + (ot + 1) * 128],
                                latn[:, it * seq + tb * 512: it * seq + (tb + 1) * 512],
                                start=(it == 0), stop=(it == LT - 1))
                        rdy = C.inc("t", mm)
                        C.wait(nc.vector, rdy)
                        if dbg is not None and "vfm" in dbg:
                            nc.vector.tensor_copy(
                                vfm_dbg[:, ot * seq + tb * 512: ot * seq + (tb + 1) * 512], acc[a][:])
                        evv = C.inc("v", nc.vector.tensor_copy(vtmp[:], acc[a][:]))
                        C.wait(nc.tensor, evv)
                        trr = None
                        for c in range(4):
                            trr = nc.tensor.transpose(ps_tr[:, _ts(c)], vtmp[:, _ts(c)], ident_b)
                        trt = C.inc("t", trr)
                        C.wait(nc.vector, trt)
                        evt = None
                        for c in range(4):
                            kt = tb * 4 + c
                            evt = nc.vector.tensor_copy(
                                v_tm[:, kt * D + ot * 128: kt * D + (ot + 1) * 128], ps_tr[:, _ts(c)])
                        evt = C.inc("v", evt)
                        acc_free[a] = evt
                        C.wait(nc.tensor, evt)
                kv_barrier_marker = True
                C.barrier()
                if dbg is not None and "vfm" in dbg:
                    ddv = C.tok("v")
                    C.wait(nc.gpsimd, ddv)
                    dv2 = C.inc("qg", nc.gpsimd.dma_start(out=dbg["vfm"][b, :, :], in_=vfm_dbg[:]), 16)
                    C.wait(nc.gpsimd, dv2)
                    C.barrier()
            with ExitStack() as s2:
                sb2 = lambda nm, shape, dt: s2.enter_context(nc.sbuf_tensor(C.u(nm), shape, dt))
                ps2 = lambda nm, shape, dt: s2.enter_context(nc.psum_tensor(C.u(nm), shape, dt))
                wtok = wt4
                acc = [ps2(f"at_qa{i}", [128, 512], F32) for i in range(2)]
                ps_sw = ps2("at_sw", [128, 512], F32)
                t1 = sb2("at_t1", [128, 512], F32)
                t2 = sb2("at_t2", [128, 512], F32)
                acc_free = [None] * 2
                inb_free = [C.tok("t"), C.tok("t")]
                C.wait(nc.tensor, wtok)
                for tb in range(SB):
                    sl = tb % 2
                    C.wait(nc.sync, inb_free[sl])
                    ldi = C.inc(ch_in[sl], nc.sync.dma_start(
                        out=inb[sl][:].rearrange("p (g c) -> p g c", g=PG),
                        in_=xn_sc[:, :, cols(tb)].rearrange("g p c -> p g c")), 16)
                    C.wait(nc.tensor, ldi)
                    for ot in range(PG):
                        a = (tb * PG + ot) % 2
                        C.wait(nc.tensor, acc_free[a])
                        mm = None
                        for it in range(PG):
                            mm = nc.tensor.matmul(
                                acc[a][:], wq[:, it * D + ot * 128:it * D + (ot + 1) * 128],
                                inb[sl][:, _ts(it, 512)], start=(it == 0), stop=(it == PG - 1))
                        rdy = C.inc("t", mm)
                        C.wait(nc.vector, rdy)
                        ev = C.inc("v", nc.vector.tensor_copy(
                            q_sb[:, ot * seq + tb * 512: ot * seq + (tb + 1) * 512], acc[a][:]))
                        acc_free[a] = ev
                    inb_free[sl] = C.tok("t")
                if dbg is not None and "q0" in dbg:
                    dd = C.tok("v")
                    C.wait(nc.gpsimd, dd)
                    dq = C.inc("qg", nc.gpsimd.dma_start(out=dbg["q0"][b, :, :], in_=q_sb[:]), 16)
                    C.wait(nc.gpsimd, dq)
                    C.barrier()
                rope_done = C.tok("v")
                for which, buf in (("q", q_sb), ("k", k_sb)):
                    for g in range(PG):
                        for tc in range(SB):
                            slc = slice(g * seq + tc * 512, g * seq + (tc + 1) * 512)
                            C.wait(nc.tensor, rope_done)
                            mm = nc.tensor.matmul(ps_sw[:], perm_b, buf[:, slc], start=True, stop=True)
                            r = C.inc("t", mm)
                            C.wait(nc.vector, r)
                            nc.vector.tensor_mul(t1[:], buf[:, slc], cos_t[:, _ts(tc, 512)])
                            nc.vector.tensor_mul(t2[:], ps_sw[:], sin_t[:, _ts(tc, 512)])
                            nc.vector.drain()
                            rope_done = C.inc("v", nc.vector.tensor_add(buf[:, slc], t1[:], t2[:]))
                C.barrier()
            if dbg is not None:
                dd = C.tok("v")
                C.wait(nc.gpsimd, dd)
                C.wait(nc.gpsimd, C.tok("s"))
                dq = C.inc("qg", nc.gpsimd.dma_start(out=dbg["q"][b, :, :], in_=q_sb[:]), 16)
                dq = C.inc("qg", nc.gpsimd.dma_start(out=dbg["k"][b, :, :], in_=k_sb[:]), 16)
                dq = C.inc("qg", nc.gpsimd.dma_start(out=dbg["v"][b, :, :], in_=v_tm[:]), 16)
                C.wait(nc.gpsimd, dq)
                C.barrier()
            if upto < 3:
                continue
            with ExitStack() as s3:
                sb3 = lambda nm, shape, dt: s3.enter_context(nc.sbuf_tensor(C.u(nm), shape, dt))
                ps3 = lambda nm, shape, dt: s3.enter_context(nc.psum_tensor(C.u(nm), shape, dt))
                wtok = wt5
                ps_S = ps3("at_S", [128, min(seq, 2048)], F32)
                ps_tr2 = ps3("at_tr2", [128, 2048], BF16)
                ps_pv = ps3("at_pv", [64, 512], F32)
                ps_wo = ps3("at_wo_ps", [128, 512], F32)
                prow = [sb3(f"at_prow{i}", [128, min(seq, 2048)], BF16) for i in range(2)]
                PT = sb3("at_PT", [128, ST * 512], BF16)
                attnT = sb3("at_attnT", [128, PG * 512], BF16)
                sms = sb3("at_sms", [128, 8], F32)
                xt_b = [sb3(f"at_xt{i}", [128, 512], F32) for i in range(2)]
                out_b = [sb3(f"at_ou{i}", [128, 512], F32) for i in range(2)]
                rope_all = C.tok("v")
                C.wait(nc.tensor, rope_all)
                C.wait(nc.tensor, wtok)
                bank_free = [None] * 4
                prow_free = [None, None]
                tr_free = None
                pv_free = None
                wo_free = None
                xt_free = [None, None]
                out_free = [None, None]
                attnT_free = None
                for qb in range(SB):
                    for h in range(NH):
                        hp = (h % 2) * 64
                        hg = h // 2
                        C.wait(nc.gpsimd, pv_free)
                        gz = C.inc("g", nc.gpsimd.memset(
                            PT[:, (qb * 4) * 512:(qb * 4 + 4) * 512], 0.0))
                        pend_tr = None
                        for qt4 in range(4):
                            qt = qb * 4 + qt4
                            valid = (qt + 1) * 128
                            nkb = _ceil(valid, 512)
                            last_s = None
                            for kb in range(nkb):
                                C.wait(nc.tensor, bank_free[kb])
                                last_s = nc.tensor.matmul(
                                    ps_S[:, _ts(kb, 512)],
                                    q_sb[hp:hp + 64, hg * seq + qt * 128: hg * seq + (qt + 1) * 128],
                                    k_sb[hp:hp + 64, hg * seq + kb * 512: hg * seq + (kb + 1) * 512],
                                    start=True, stop=True)
                            smm = C.inc("t", last_s)
                            if pend_tr is not None:
                                pqt4, ppn = pend_tr
                                pqt = qb * 4 + pqt4
                                C.wait(nc.tensor, ppn)
                                C.wait(nc.tensor, tr_free)
                                tr = None
                                for kt in range(pqt + 1):
                                    tr = nc.tensor.transpose(ps_tr2[:, _ts(kt)],
                                                             prow[pqt4 % 2][:, _ts(kt)], ident_b)
                                trt = C.inc("t", tr)
                                prow_free[pqt4 % 2] = trt
                                C.wait(nc.vector, trt)
                                C.wait(nc.vector, gz)
                                tr_free = C.inc("v", nc.vector.tensor_copy(
                                    PT[:].rearrange("p (k c) -> p k c", c=512)[:, 0:pqt + 1, pqt4 * 128: pqt4 * 128 + 128],
                                    ps_tr2[:].rearrange("p (k c) -> p k c", c=128)[:, 0:pqt + 1, :]))
                            C.wait(nc.vector, smm)
                            dg = slice(qt * 128, qt * 128 + 128)
                            nc.vector.tensor_add(ps_S[:, dg], ps_S[:, dg], mask128)
                            nc.vector.drain()
                            _sv = ps_S[:, 0:valid]
                            nc.vector.tensor_reduce(
                                sms[:, 0:1],
                                bass.AP(_sv.tensor, _sv.offset,
                                        [[_sv.ap[0][0], 128], [4, valid // 4]]),
                                AX.X, ALU.max, negate=True)
                            nc.vector.drain()
                            nm = C.inc("v", nc.vector.tensor_scalar(sms[:, 1:2], sms[:, 0:1], 0.125, 0.0, ALU.mult, ALU.add))
                            C.wait(nc.scalar, nm)
                            C.wait(nc.scalar, prow_free[qt4 % 2])
                            for kb in range(nkb):
                                hi = min(valid, (kb + 1) * 512)
                                ex = C.inc("s", nc.scalar.activation(
                                    prow[qt4 % 2][:, kb * 512:hi], ps_S[:, kb * 512:hi], AF.Exp,
                                    bias=sms[:, 1:2], scale=0.125, accum_out=sms[:, 4 + kb:5 + kb]))
                                bank_free[kb] = ex
                            nc.scalar.drain()
                            C.wait(nc.vector, C.tok("s"))
                            nc.vector.tensor_reduce(sms[:, 2:3], sms[:, 4:4 + nkb], AX.X, ALU.add)
                            nc.vector.drain()
                            nc.vector.reciprocal(sms[:, 3:4], sms[:, 2:3])
                            nc.vector.drain()
                            pn = C.inc("v", nc.vector.tensor_scalar(
                                prow[qt4 % 2][:, 0:valid], prow[qt4 % 2][:, 0:valid], sms[:, 3:4], 0.0, ALU.mult, ALU.add))
                            prow_free[qt4 % 2] = None
                            pend_tr = (qt4, pn)
                        pqt4, ppn = pend_tr
                        pqt = qb * 4 + pqt4
                        C.wait(nc.tensor, ppn)
                        C.wait(nc.tensor, tr_free)
                        tr = None
                        for kt in range(pqt + 1):
                            tr = nc.tensor.transpose(ps_tr2[:, _ts(kt)],
                                                     prow[pqt4 % 2][:, _ts(kt)], ident_b)
                        trt = C.inc("t", tr)
                        C.wait(nc.vector, trt)
                        C.wait(nc.vector, gz)
                        tr_free = C.inc("v", nc.vector.tensor_copy(
                            PT[:].rearrange("p (k c) -> p k c", c=512)[:, 0:pqt + 1, pqt4 * 128: pqt4 * 128 + 128],
                            ps_tr2[:].rearrange("p (k c) -> p k c", c=128)[:, 0:pqt + 1, :]))
                        prow_free[0] = C.tok("t")
                        prow_free[1] = C.tok("t")
                        pt_done = tr_free
                        C.wait(nc.tensor, pt_done)
                        C.wait(nc.tensor, pv_free)
                        mm = None
                        nkt = qb * 4 + 4
                        for kt in range(nkt):
                            mm = nc.tensor.matmul(
                                ps_pv[:, :],
                                v_tm[:, kt * D + h * 64: kt * D + (h + 1) * 64],
                                PT[:, kt * 512:(kt + 1) * 512],
                                start=(kt == 0), stop=(kt == nkt - 1))
                        pvt = C.inc("t", mm)
                        C.wait(nc.vector, pvt)
                        C.wait(nc.vector, attnT_free)
                        pv_free = C.inc("v", nc.vector.tensor_copy(
                            attnT[hp:hp + 64, hg * 512:(hg + 1) * 512], ps_pv[:, :]))
                    at_done = C.tok("v")
                    C.wait(nc.tensor, at_done)
                    gcols = slice((b * SB + qb) * 512, (b * SB + qb + 1) * 512)
                    for ot in range(PG):
                        m = ot % 2
                        C.wait(nc.sync, xt_free[m])
                        ldx = C.inc(ch_xt[m], nc.sync.dma_start(
                            out=xt_b[m][:], in_=xT_sc[ot, :, gcols]), 16)
                        C.wait(nc.tensor, wo_free)
                        mm = None
                        for it in range(PG):
                            mm = nc.tensor.matmul(
                                ps_wo[:], wo[:, it * D + ot * 128:it * D + (ot + 1) * 128],
                                attnT[:, _ts(it, 512)], start=(it == 0), stop=(it == PG - 1))
                        rdy = C.inc("t", mm)
                        C.wait(nc.vector, rdy)
                        C.wait(nc.vector, out_free[m])
                        C.wait(nc.vector, ldx)
                        u = C.inc("v", nc.vector.tensor_add(out_b[m][:], ps_wo[:], xt_b[m][:]))
                        xt_free[m] = u
                        wo_free = u
                        C.wait(nc.gpsimd, u)
                        d = C.inc(ch_o[m], nc.gpsimd.dma_start(
                            out=xT_sc[ot, :, gcols], in_=out_b[m][:]), 16)
                        out_free[m] = d
                    attnT_free = C.tok("t")
                C.barrier()
        C.barrier()


def emit_scan(C, wd, xn_sc, xT_sc, n_tok, seq, *, bias_pp):
    nc = C.nc
    n_batch = n_tok // seq
    SB = seq // 512
    with ExitStack() as ph:
        sb = lambda nm, shape, dt: ph.enter_context(nc.sbuf_tensor(C.u(nm), shape, dt))
        psum = lambda nm, shape, dt: ph.enter_context(nc.psum_tensor(C.u(nm), shape, dt))
        wz, wtok1 = load_weight_cast(C, ph, wd["L1_attn_Wz"], D, D, "sc_wz")
        wh, wtok2 = load_weight_cast(C, ph, wd["L1_attn_Wh"], D, D, "sc_wh")
        negb = sb("sc_negb", [128, PG], F32)
        inb = [sb(f"sc_in{i}", [128, PG * 512], BF16) for i in range(seq // 512)]
        zb = sb("sc_zb", [128, 512], F32)
        ab = sb("sc_ab", [128, seq], F32)
        bb = sb("sc_bb", [128, seq], F32)
        hb = sb("sc_hb", [128, seq], F32)
        xt_b = [sb(f"sc_xt{i}", [128, 512], F32) for i in range(2)]
        out_b = [sb(f"sc_ou{i}", [128, 512], F32) for i in range(2)]
        accz = [psum(f"sc_az{i}", [128, 512], F32) for i in range(2)]
        acch = [psum(f"sc_ah{i}", [128, 512], F32) for i in range(2)]
        ch_in = [C.chan(ph, "sc_chi") for i in range(seq // 512)]
        ch_xt = [C.chan(ph, "sc_chx") for i in range(2)]
        ch_o = [C.chan(ph, "sc_cho", kind="sw") for i in range(2)]
        nb = C.inc("v", nc.vector.tensor_scalar(negb[:], bias_pp[:, 0:PG], -1.0, 0.0, ALU.mult, ALU.add))
        nc.vector.drain()
        C.wait(nc.scalar, nb)
        C.wait(nc.tensor, [wtok1, wtok2])

        inb_free = [None] * SB
        ld_toks = [None] * SB
        accz_free = [None, None]
        acch_free = [None, None]
        xt_free = [None, None]
        out_free = [None, None]
        buf_free = None
        for b in range(n_batch):
            for tb in range(SB):
                gtb = b * SB + tb
                C.wait(nc.sync, inb_free[tb])
                ld_toks[tb] = C.inc(ch_in[tb], nc.sync.dma_start(
                    out=inb[tb][:].rearrange("p (g c) -> p g c", g=PG),
                    in_=xn_sc[:, :, _ts(gtb, 512)].rearrange("g p c -> p g c")), 16)
            for ot in range(PG):
                vb_prev = None
                for tb in range(SB):
                    sl = tb
                    if ot == 0:
                        C.wait(nc.tensor, ld_toks[tb])
                    m = tb % 2
                    C.wait(nc.tensor, accz_free[m])
                    mm = None
                    for it in range(PG):
                        mm = nc.tensor.matmul(
                            accz[m][:], wz[:, it * D + ot * 128:it * D + (ot + 1) * 128],
                            inb[sl][:, _ts(it, 512)], start=(it == 0), stop=(it == PG - 1))
                    rz = C.inc("t", mm)
                    C.wait(nc.tensor, acch_free[m])
                    mm = None
                    for it in range(PG):
                        mm = nc.tensor.matmul(
                            acch[m][:], wh[:, it * D + ot * 128:it * D + (ot + 1) * 128],
                            inb[sl][:, _ts(it, 512)], start=(it == 0), stop=(it == PG - 1))
                    rh = C.inc("t", mm)
                    C.wait(nc.scalar, rz)
                    C.wait(nc.scalar, buf_free)
                    C.wait(nc.scalar, vb_prev)
                    sz = C.inc("s", nc.scalar.activation(zb[:], accz[m][:], AF.Sigmoid,
                                                         bias=bias_pp[:, ot:ot + 1]))
                    nc.scalar.drain()
                    sa = C.inc("s", nc.scalar.activation(ab[:, _ts(tb, 512)], accz[m][:], AF.Sigmoid,
                                                         scale=-1.0, bias=negb[:, ot:ot + 1]))
                    accz_free[m] = sa
                    C.wait(nc.vector, [sz, sa, rh])
                    vb = C.inc("v", nc.vector.scalar_tensor_tensor(
                        bb[:, _ts(tb, 512)], acch[m][:], bias_pp[:, PG + ot:PG + ot + 1],
                        zb[:], ALU.add, ALU.mult))
                    acch_free[m] = vb
                    vb_prev = vb
                C.wait(nc.vector, C.tok("s"))
                nc.vector.drain()
                sc = C.inc("v", nc.vector.tensor_tensor_scan(
                    hb[:], ab[:], bb[:], 0.0, ALU.mult, ALU.add))
                buf_free = sc
                for tb in range(SB):
                    m2 = tb % 2
                    gtb = b * SB + tb
                    C.wait(nc.sync, xt_free[m2])
                    ldx = C.inc(ch_xt[m2], nc.sync.dma_start(
                        out=xt_b[m2][:], in_=xT_sc[ot, :, _ts(gtb, 512)]), 16)
                    C.wait(nc.vector, [ldx, sc, out_free[m2]])
                    u = C.inc("v", nc.vector.tensor_add(out_b[m2][:], hb[:, _ts(tb, 512)], xt_b[m2][:]))
                    xt_free[m2] = u
                    C.wait(nc.gpsimd, u)
                    d = C.inc(ch_o[m2], nc.gpsimd.dma_start(
                        out=xT_sc[ot, :, _ts(gtb, 512)], in_=out_b[m2][:]), 16)
                    out_free[m2] = d
            t_all = C.tok("t")
            inb_free = [t_all] * SB
        C.barrier()


def emit_moe(C, wd, y_sc, xT_sc, h_sc, wrow_sc, n_tok, *, ident_b, ident_f, rmsw_pp):
    nc = C.nc
    NTB = n_tok // 512
    NT = n_tok // 128
    E_t = DE // 128
    with ExitStack() as ph:
        sb = lambda nm, shape, dt: ph.enter_context(nc.sbuf_tensor(C.u(nm), shape, dt))
        psum = lambda nm, shape, dt: ph.enter_context(nc.psum_tensor(C.u(nm), shape, dt))
        gwf = sb("mg_gwf", [8, D], F32)
        gwT = sb("mg_gwT", [128, PG * 8], BF16)
        logit = sb("mg_log", [8, n_tok], F32)
        ltm = sb("mg_ltm", [128, NT * 8], F32)
        wtm = sb("mg_wtm", [128, NT * 8], F32)
        wfm = sb("mg_wfm", [8, n_tok], F32)
        sms = sb("mg_sms", [128, 8], F32)
        tt8 = sb("mg_tt8", [128, 8], F32)
        inb = [sb(f"mg_in{i}", [128, PG * 512], BF16) for i in range(2)]
        ps8 = psum("mg_ps8", [8, 512], F32)
        pst = psum("mg_pst", [128, 1024], BF16)
        pstf = psum("mg_pstf", [128, 512], F32)
        ch_in = [C.chan(ph, "mg_chi") for i in range(2)]
        ch_g = C.chan(ph, "mg_chg")
        ldg = C.inc(ch_g, nc.sync.dma_start(out=gwf[:], in_=wd["L1_ffn_gate_w"][:, :]), 16)
        gwb = sb("mg_gwb", [8, D], BF16)
        C.wait(nc.vector, ldg)
        cb = C.inc("v", nc.vector.tensor_copy(gwb[:], gwf[:]))
        C.wait(nc.tensor, cb)
        tr = None
        for it in range(PG):
            tr = nc.tensor.transpose(pst[:, it * 8:(it + 1) * 8], gwb[:, _ts(it)], ident_b[0:8, 0:8])
        trt = C.inc("t", tr)
        C.wait(nc.vector, trt)
        gt = None
        for it in range(PG):
            gt = nc.vector.tensor_scalar(
                gwT[:, it * 8:(it + 1) * 8], pst[:, it * 8:(it + 1) * 8],
                rmsw_pp[:, it:it + 1], 0.0, ALU.mult, ALU.add)
        gt = C.inc("v", gt)
        C.wait(nc.tensor, gt)
        inb_free = [None, None]
        lg_ev = None
        for tb in range(NTB):
            sl = tb % 2
            C.wait(nc.sync, inb_free[sl])
            ldi = C.inc(ch_in[sl], nc.sync.dma_start(
                out=inb[sl][:].rearrange("p (g c) -> p g c", g=PG),
                in_=y_sc[:, :, _ts(tb, 512)].rearrange("g p c -> p g c")), 16)
            C.wait(nc.tensor, ldi)
            C.wait(nc.tensor, lg_ev)
            mm = None
            for it in range(PG):
                mm = nc.tensor.matmul(ps8[:], gwT[:, it * 8:(it + 1) * 8],
                                      inb[sl][:, _ts(it, 512)],
                                      start=(it == 0), stop=(it == PG - 1))
            r = C.inc("t", mm)
            C.wait(nc.vector, r)
            lg_ev = C.inc("v", nc.vector.tensor_copy(logit[:, _ts(tb, 512)], ps8[:]))
            inb_free[sl] = C.tok("t")
        C.wait(nc.tensor, lg_ev)
        wb_ev = None
        for tt in range(NT):
            tr = nc.tensor.transpose(pstf[:, 0:8], logit[:, _ts(tt)], ident_f[0:8, 0:8])
            r = C.inc("t", tr)
            C.wait(nc.vector, r)
            ev = C.inc("v", nc.vector.tensor_copy(ltm[:, tt * 8:(tt + 1) * 8], pstf[:, 0:8]))
            C.wait(nc.tensor, ev)
            lt = ltm[:, tt * 8:(tt + 1) * 8]
            wt = wtm[:, tt * 8:(tt + 1) * 8]
            nc.vector.drain()
            nc.vector.tensor_reduce(sms[:, 0:1], lt, AX.X, ALU.max)
            nc.vector.drain()
            nc.vector.tensor_tensor(tt8[:], lt, sms[:, 0:1].broadcast_to([128, 8]), ALU.is_equal)
            nc.vector.drain()
            nc.vector.scalar_tensor_tensor(wt, tt8[:], -1e30, lt, ALU.mult, ALU.add)
            nc.vector.drain()
            nc.vector.tensor_reduce(sms[:, 1:2], wt, AX.X, ALU.max)
            nc.vector.drain()
            nc.vector.tensor_tensor(wt, wt, sms[:, 1:2].broadcast_to([128, 8]), ALU.is_equal)
            nc.vector.drain()
            sg = C.inc("v", nc.vector.tensor_sub(sms[:, 2:3], sms[:, 0:1], sms[:, 1:2]))
            C.wait(nc.scalar, sg)
            ss = C.inc("s", nc.scalar.activation(sms[:, 3:4], sms[:, 2:3], AF.Sigmoid))
            C.wait(nc.vector, ss)
            nc.vector.tensor_scalar(tt8[:], tt8[:], sms[:, 3:4], 0.0, ALU.mult, ALU.add)
            nc.vector.drain()
            nc.vector.scalar_tensor_tensor(wt, wt, sms[:, 3:4], wt, ALU.mult, ALU.subtract)
            nc.vector.drain()
            wv = C.inc("v", nc.vector.tensor_sub(wt, tt8[:], wt))
            C.wait(nc.tensor, wv)
            tr2 = nc.tensor.transpose(pstf[0:8, 0:128], wt, ident_f)
            r2 = C.inc("t", tr2)
            C.wait(nc.vector, r2)
            wb_ev = C.inc("v", nc.vector.tensor_copy(wfm[:, _ts(tt)], pstf[0:8, 0:128]))
            C.wait(nc.tensor, wb_ev)
        C.wait(nc.gpsimd, wb_ev)
        dw = C.inc("qg", nc.gpsimd.dma_start(out=wrow_sc[:, :], in_=wfm[:]), 16)
        C.wait(nc.gpsimd, dw)
        C.barrier()

    with ExitStack() as phe:
        wbe = phe.enter_context(nc.sbuf_tensor(C.u("me_wbe"), [128, n_tok], F32))
        ch_wb = C.chan(phe, "me_chwb")
        for e in range(NE):
            C.wait(nc.sync, C.tok("v"))
            lw = C.inc(ch_wb, nc.sync.dma_start(
                out=wbe[:], in_=wrow_sc[e:e + 1, :].broadcast_to([128, n_tok])), 16)
            with ExitStack() as ph:
                silu_b = [ph.enter_context(nc.sbuf_tensor(C.u(f"mA_sl{i}"), [128, 512], F32)) for i in range(2)]
                out_b = [ph.enter_context(nc.sbuf_tensor(C.u(f"mA_ou{i}"), [128, 512], BF16)) for i in range(2)]
                ch_o = [C.chan(ph, "mA_cho", kind="sw") for i in range(2)]
                free_tok = [None, None]
                kctr = [0]

                def epA(C, ot, tb, pss, rdy, aux):
                    k = kctr[0] % 2
                    kctr[0] += 1
                    C.wait(nc.scalar, rdy)
                    C.wait(nc.scalar, free_tok[k])
                    a = C.inc("s", nc.scalar.activation(silu_b[k][:], pss[0][:], AF.Silu))
                    C.wait(nc.vector, a)
                    C.wait(nc.vector, rdy)
                    u = C.inc("v", nc.vector.scalar_tensor_tensor(
                        out_b[k][:], pss[1][:], 1.0, silu_b[k][:], ALU.mult, ALU.mult))
                    C.wait(nc.gpsimd, u)
                    d = C.inc(ch_o[k], nc.gpsimd.dma_start(
                        out=h_sc[ot, :, _ts(tb, 512)], in_=out_b[k][:]), 16)
                    free_tok[k] = [u, d]
                    return [a, u]

                emit_gemm_oi(C, [wd["L1_ffn_W1"][e], wd["L1_ffn_W2"][e]], DE, D,
                             y_sc, n_tok, og=22, epilogue=epA, ident_f=ident_f,
                             name=C.u("mA"), rhs_scale=rmsw_pp)
            with ExitStack() as ph:
                t_b = [ph.enter_context(nc.sbuf_tensor(C.u(f"mB_t{i}"), [128, 512], F32)) for i in range(2)]
                xt_b = [ph.enter_context(nc.sbuf_tensor(C.u(f"mB_xt{i}"), [128, 512], F32)) for i in range(3)]
                out_b = [ph.enter_context(nc.sbuf_tensor(C.u(f"mB_ou{i}"), [128, 512], F32)) for i in range(2)]
                ch_xt = [C.chan(ph, "mB_chx", kind="sw") for i in range(3)]
                ch_o = [C.chan(ph, "mB_cho", kind="sw") for i in range(2)]
                xt_free = [None] * 3
                out_free = [None, None]
                t_free = [None, None]
                prectr = [0]
                epctr = [0]

                def preB(C, ot, tb):
                    k = prectr[0] % 3
                    prectr[0] += 1
                    C.wait(nc.gpsimd, xt_free[k])
                    ld = C.inc(ch_xt[k], nc.gpsimd.dma_start(
                        out=xt_b[k][:], in_=xT_sc[ot, :, _ts(tb, 512)]), 16)
                    return (k, ld)

                def epB(C, ot, tb, pss, rdy, aux):
                    k, ld = aux
                    m = epctr[0] % 2
                    epctr[0] += 1
                    C.wait(nc.vector, [rdy, lw, t_free[m]])
                    t1 = C.inc("v", nc.vector.tensor_mul(t_b[m][:], pss[0][:], wbe[:, _ts(tb, 512)]))
                    nc.vector.drain()
                    C.wait(nc.vector, [ld, out_free[m]])
                    u = C.inc("v", nc.vector.tensor_add(out_b[m][:], t_b[m][:], xt_b[k][:]))
                    xt_free[k] = u
                    t_free[m] = u
                    C.wait(nc.gpsimd, u)
                    d = C.inc(ch_o[m], nc.gpsimd.dma_start(
                        out=xT_sc[ot, :, _ts(tb, 512)], in_=out_b[m][:]), 16)
                    out_free[m] = d
                    return [t1]

                emit_gemm_oi(C, [wd["L1_ffn_W3"][e]], D, DE, h_sc, n_tok, og=8,
                             epilogue=epB, pre=preB, ident_f=ident_f, name=C.u("mB"))


def emit_out(C, xT_sc, y_out, n_tok, *, ident_f):
    nc = C.nc
    NT = n_tok // 128
    with ExitStack() as ph:
        sb = lambda nm, shape, dt: ph.enter_context(nc.sbuf_tensor(C.u(nm), shape, dt))
        psum = lambda nm, shape, dt: ph.enter_context(nc.psum_tensor(C.u(nm), shape, dt))
        ldf = [sb(f"ou_ld{i}", [128, D], F32) for i in range(2)]
        otm = [sb(f"ou_tm{i}", [128, D], F32) for i in range(2)]
        ps_f = [psum(f"ou_ps{i}", [128, 512], F32) for i in range(2)]
        ch_l = [C.chan(ph, "ou_chl") for i in range(2)]
        ch_s = [C.chan(ph, "ou_chs", kind="sw") for i in range(2)]
        ldf_free = [None, None]
        otm_free = [None, None]
        psf_free = [None, None]
        for tt in range(NT):
            j = tt % 2
            C.wait(nc.sync, ldf_free[j])
            ld = C.inc(ch_l[j], nc.sync.dma_start(
                out=ldf[j][:].rearrange("p (g c) -> p g c", g=PG),
                in_=xT_sc[:, :, _ts(tt)].rearrange("g p c -> p g c")), 16)
            C.wait(nc.tensor, ld)
            C.wait(nc.tensor, psf_free[j])
            ev = None
            for g0 in range(0, PG, 4):
                tr = None
                for g in range(g0, g0 + 4):
                    tr = nc.tensor.transpose(ps_f[j][:, _ts(g - g0)], ldf[j][:, _ts(g)], ident_f)
                trt = C.inc("t", tr)
                C.wait(nc.vector, trt)
                C.wait(nc.vector, otm_free[j])
                for g in range(g0, g0 + 4):
                    ev = nc.vector.tensor_copy(otm[j][:, _ts(g)], ps_f[j][:, _ts(g - g0)])
                ev = C.inc("v", ev)
                if g0 + 4 < PG:
                    C.wait(nc.tensor, ev)
            psf_free[j] = ev
            ldf_free[j] = C.tok("t")
            C.wait(nc.gpsimd, ev)
            st = C.inc(ch_s[j], nc.gpsimd.dma_start(out=y_out[_ts(tt), :], in_=otm[j][:]), 16)
            otm_free[j] = st
        C.barrier()


N_TOK_FULL = 4096
SEQ_FULL = 2048
N_CORES = 8

WSHAPES = {
    "L0_attn_W_q": (D, D), "L0_attn_W_dkv": (D, LAT), "L0_attn_W_uk": (LAT, D),
    "L0_attn_W_uv": (LAT, D), "L0_attn_W_o": (D, D),
    "L0_ffn_w1_w": (DH, D), "L0_ffn_w2_w": (DH, D), "L0_ffn_h1_w": (DH, DH),
    "L0_ffn_h2_w": (DH, DH), "L0_ffn_w3_w": (D, DH),
    "L1_attn_Wz": (D, D), "L1_attn_Wh": (D, D),
    "L1_ffn_gate_w": (NE, D), "L1_ffn_W1": (NE, DE, D), "L1_ffn_W2": (NE, DE, D),
    "L1_ffn_W3": (NE, D, DE),
}


def host_consts(seq):
    p = np.arange(128)
    dl = p % 64
    inv = 1.0 / 10000.0 ** (np.arange(32, dtype=np.float64) / 32)
    tpos = np.arange(seq, dtype=np.float64)
    ang = tpos[None, :] * inv[dl % 32][:, None]
    cos_t = np.cos(ang).astype(BF16NP)
    sin_t = np.where((dl < 32)[:, None], -np.sin(ang), np.sin(ang)).astype(BF16NP)
    perm = np.zeros((128, 128), np.float32)
    for m in range(128):
        blk = (m // 64) * 64
        perm[blk + ((m - blk + 32) % 64), m] = 1.0
    mask = np.where(np.arange(128)[:, None] >= np.arange(128)[None, :],
                    0.0, -1e30).astype(np.float32)
    cvec = np.zeros((128, 16), np.float32)
    cvec[:, 0] = LN_EPS
    cvec[:, 1] = RMS_EPS * D
    return {"c_ident_b": np.eye(128, dtype=BF16NP),
            "c_ident_f": np.eye(128, dtype=np.float32),
            "c_perm": perm.astype(BF16NP),
            "c_cos": cos_t, "c_sin": sin_t, "c_mask": mask, "c_cvec": cvec}


def pack_params(inputs):
    r = {}
    scl = np.empty((128, 136), np.float32)
    scl[:, 0:32] = inputs["L0_ffn_w1_s"].reshape(32, 128).T
    scl[:, 32:64] = inputs["L0_ffn_w2_s"].reshape(32, 128).T
    scl[:, 64:96] = inputs["L0_ffn_h1_s"].reshape(32, 128).T
    scl[:, 96:128] = inputs["L0_ffn_h2_s"].reshape(32, 128).T
    scl[:, 128:136] = inputs["L0_ffn_w3_s"].reshape(8, 128).T
    r["p_scl"] = scl
    gb = np.empty((128, 4), np.float32)
    gb[:, 0:2] = inputs["L0_attn_ln_gamma"].reshape(2, 128).T
    gb[:, 2:4] = inputs["L0_attn_ln_beta"].reshape(2, 128).T
    r["p_gam_bet"] = gb
    bp = np.empty((128, 2 * PG), np.float32)
    bp[:, 0:PG] = inputs["L1_attn_bz"].reshape(PG, 128).T
    bp[:, PG:] = inputs["L1_attn_bh"].reshape(PG, 128).T
    r["p_bias"] = bp
    r["p_rmsw"] = np.ascontiguousarray(inputs["L1_ffn_rms_w"].reshape(PG, 128).T)
    return r


def build_full(n_tok=N_TOK_FULL, seq=SEQ_FULL):
    nc = bass.Bass("TRN2", target_bir_lowering=False)
    F = {"x": nc.dram_tensor("x", [n_tok, D], F32, kind="ExternalInput").ap()}
    wd = {}
    for nm, shp in WSHAPES.items():
        wd[nm] = nc.dram_tensor(nm, list(shp), F32, kind="ExternalInput").ap()
    cshapes = {"c_ident_b": ([128, 128], BF16), "c_ident_f": ([128, 128], F32),
               "c_perm": ([128, 128], BF16), "c_cos": ([128, seq], BF16),
               "c_sin": ([128, seq], BF16), "c_mask": ([128, 128], F32),
               "c_cvec": ([128, 16], F32), "p_scl": ([128, 136], F32),
               "p_gam_bet": ([128, 4], F32), "p_bias": ([128, 2 * PG], F32),
               "p_rmsw": ([128, PG], F32)}
    cd = {nm: nc.dram_tensor(nm, list(shp), dt, kind="ExternalInput").ap()
          for nm, (shp, dt) in cshapes.items()}
    y_out = nc.dram_tensor("y", [n_tok, D], F32, kind="ExternalOutput").ap()

    xT_sc = nc.dram_tensor("xT_sc", [PG, 128, n_tok], F32, kind="Internal").ap()
    xn0_sc = nc.dram_tensor("xn0_sc", [PG, 128, n_tok], BF16, kind="Internal").ap()
    xn1_sc = nc.dram_tensor("xn1_sc", [PG, 128, n_tok], BF16, kind="Internal").ap()
    y_sc = nc.dram_tensor("ysc", [PG, 128, n_tok], BF16, kind="Internal").ap()
    u_sc = nc.dram_tensor("u_sc", [DH // 128, 128, n_tok], BF16, kind="Internal").ap()
    g_sc = nc.dram_tensor("g_sc", [DH // 128, 128, n_tok], BF16, kind="Internal").ap()
    wrow_sc = nc.dram_tensor("wrow_sc", [NE, n_tok], F32, kind="Internal").ap()

    with ExitStack() as es:
        C = Ctx(nc, es)
        cs = {}
        ld = None
        for nm, (shp, dt) in cshapes.items():
            t = es.enter_context(nc.sbuf_tensor(nm + "_sb", shp, dt))
            ld = C.inc("qs", nc.sync.dma_start(out=t[:], in_=cd[nm][:, :]), 16)
            cs[nm] = t[:]
        C.wait(nc.sync, ld)
        C.barrier()
        emit_ln(C, F["x"], xT_sc, xn0_sc, n_tok, is_ln1=False, y_sc=None,
                ident_b=cs["c_ident_b"], ident_f=cs["c_ident_f"], cst=cs["c_cvec"])
        emit_attn(C, wd, xn0_sc, xT_sc, n_tok, seq,
                  ident_b=cs["c_ident_b"], ident_f=cs["c_ident_f"],
                  perm_b=cs["c_perm"], cos_t=cs["c_cos"], sin_t=cs["c_sin"],
                  mask128=cs["c_mask"], gam_bet=cs["p_gam_bet"], cst=cs["c_cvec"])
        scl = cs["p_scl"]
        emit_ffn0(C, wd, xn0_sc, u_sc, g_sc, xT_sc, n_tok,
                  scl[:, 0:32], scl[:, 32:64], scl[:, 64:96], scl[:, 96:128],
                  scl[:, 128:136], cs["c_ident_f"])
        emit_ln(C, xT_sc, xT_sc, xn1_sc, n_tok, is_ln1=True, y_sc=y_sc,
                ident_b=cs["c_ident_b"], ident_f=cs["c_ident_f"], cst=cs["c_cvec"])
        emit_scan(C, wd, xn1_sc, xT_sc, n_tok, seq, bias_pp=cs["p_bias"])
        emit_moe(C, wd, y_sc, xT_sc, u_sc[0:DE // 128], wrow_sc, n_tok,
                 ident_b=cs["c_ident_b"], ident_f=cs["c_ident_f"],
                 rmsw_pp=cs["p_rmsw"])
        C.barrier()
        emit_out(C, xT_sc, y_out, n_tok, ident_f=cs["c_ident_f"])
    return nc


def kernel(**inputs):
    x = np.ascontiguousarray(np.asarray(inputs["L0_attn_x"], dtype=np.float32))
    B, S, _ = x.shape
    per = B // N_CORES
    consts = host_consts(S)
    params = pack_params({k: np.asarray(v, dtype=np.float32) for k, v in inputs.items()
                          if k.endswith(("_s", "gamma", "beta", "bz", "bh", "rms_w"))})
    weights = {nm: np.ascontiguousarray(np.asarray(inputs[nm], dtype=np.float32))
               for nm in WSHAPES}
    nc = build_full(per * S, S)
    in_maps = []
    for c in range(N_CORES):
        m = {"x": np.ascontiguousarray(x[c * per:(c + 1) * per].reshape(per * S, D))}
        m.update(weights)
        m.update(consts)
        m.update(params)
        in_maps.append(m)
    res = run_bass_kernel_spmd(nc, in_maps, core_ids=list(range(N_CORES)))
    out = np.concatenate([res.results[c]["y"].reshape(per, S, D)
                          for c in range(N_CORES)], axis=0)
    return out
